# version 03
# baseline (speedup 1.0000x reference)
import numpy as np
from contextlib import ExitStack

import ml_dtypes
import concourse.bass as bass
import concourse.mybir as mybir
from concourse.bass_utils import run_bass_kernel_spmd

F32 = mybir.dt.float32
BF16 = mybir.dt.bfloat16
I32 = mybir.dt.int32
U8 = mybir.dt.uint8
AF = mybir.ActivationFunctionType
ALU = mybir.AluOpType
AX = mybir.AxisListType

D = 2048
NCHUNK = D // 128
BATCH = 4
SEQ = 2048
NCORES = 8

bf = ml_dtypes.bfloat16


class Op:
    __slots__ = ("engine", "emit", "reads", "writes", "inc", "waits", "tag",
                 "_need", "token")

    def __init__(self, engine, emit, reads, writes, inc, tag):
        self.engine = engine
        self.emit = emit
        self.reads = reads
        self.writes = writes
        self.inc = inc
        self.waits = []
        self.tag = tag


class Prog:
    ENGINES = ("sync", "gpsimd", "tensor", "vector", "scalar")
    NDMA = 10

    def __init__(self, nc):
        self.nc = nc
        self.ops = {e: [] for e in self.ENGINES}
        self.cnt = {e: 0 for e in self.ENGINES}
        self.dma_cnt = {"sync": 0, "gpsimd": 0}
        self.last_w = {}
        self.readers = {}
        self.wm = {e: {} for e in self.ENGINES}

    def _need(self, o, sem_id, val):
        cur = o._need.get(sem_id, 0)
        if val > cur:
            o._need[sem_id] = val

    def op(self, engine, emit, reads=(), writes=(), dma=False, tag=""):
        o = Op(engine, emit, tuple(reads), tuple(writes), 16 if dma else 1,
               tag)
        o._need = {}
        for k in o.reads:
            lw = self.last_w.get(k)
            if lw is not None:
                self._need(o, *lw)
        for k in o.writes:
            lw = self.last_w.get(k)
            if lw is not None:
                self._need(o, *lw)
            for sem_id, v in self.readers.get(k, {}).items():
                self._need(o, sem_id, v)
        if dma:
            assert engine in self.dma_cnt
            i = self.dma_cnt[engine]
            self.dma_cnt[engine] = i + 1
            slot = i % self.NDMA
            sem_id = ("dma", engine, slot)
            if i >= self.NDMA:
                self._need(o, sem_id, 16 * (i // self.NDMA))
            token = (sem_id, 16 * (i // self.NDMA + 1))
        else:
            self.cnt[engine] += 1
            token = (("eng", engine), self.cnt[engine])
        wmE = self.wm[engine]
        for sem_id, v in sorted(o._need.items(), key=lambda kv: str(kv[0])):
            if wmE.get(sem_id, 0) < v:
                o.waits.append((sem_id, v))
                wmE[sem_id] = v
        for k in o.writes:
            self.last_w[k] = token
            self.readers[k] = {}
        for k in o.reads:
            self.readers.setdefault(k, {})[token[0]] = token[1]
        o.token = token
        self.ops[engine].append(o)
        return o

    def emit_all(self, ctx):
        nc = self.nc
        sems = {}
        for e in self.ENGINES:
            sems[("eng", e)] = ctx.enter_context(nc.semaphore(f"sem_{e}"))
        for q in ("sync", "gpsimd"):
            for s in range(self.NDMA):
                sems[("dma", q, s)] = ctx.enter_context(
                    nc.semaphore(f"dmas_{q}_{s}"))
        block = ctx.enter_context(nc.Block())

        def body(engine_obj, ename):
            for o in self.ops[ename]:
                for sem_id, v in o.waits:
                    engine_obj.wait_ge(sems[sem_id], v)
                inst = o.emit(engine_obj)
                if inst is not None:
                    inst.then_inc(sems[o.token[0]], o.inc)

        finals = []
        for e in self.ENGINES:
            if self.cnt[e] > 0:
                finals.append((("eng", e), self.cnt[e]))
        for q in ("sync", "gpsimd"):
            n = self.dma_cnt[q]
            for s in range(self.NDMA):
                uses = (n - s + self.NDMA - 1) // self.NDMA
                if uses > 0:
                    finals.append((("dma", q, s), 16 * uses))

        @block.sync
        def _(sync):
            body(sync, "sync")
            for sem_id, v in finals:
                sync.wait_ge(sems[sem_id], v)

        @block.gpsimd
        def _(gpsimd):
            body(gpsimd, "gpsimd")

        @block.tensor
        def _(tensor):
            body(tensor, "tensor")

        @block.vector
        def _(vector):
            body(vector, "vector")

        @block.scalar
        def _(scalar):
            body(scalar, "scalar")


def bkey(h, tag=None):
    return (h.name if hasattr(h, "name") else id(h), tag)


def dma_load(p, engine, dst_ap, src_ap, dst_keys, src_keys=(), tag=""):
    def em(eng, dst_ap=dst_ap, src_ap=src_ap):
        return eng.dma_start(out=dst_ap, in_=src_ap)
    p.op(engine, em, reads=src_keys, writes=dst_keys, dma=True, tag=tag)


def dma_load_c(p, engine, dst3, src3, nC, dst_keys, src_keys=(), tag="",
               group=4):
    for c0, cl in split_chunks(nC, group):
        dma_load(p, engine, dst3[:, c0:c0 + cl, :], src3[:, c0:c0 + cl, :],
                 dst_keys=dst_keys, src_keys=src_keys, tag=tag)


def split_chunks(n, step):
    out = []
    i = 0
    while i < n:
        out.append((i, min(step, n - i)))
        i += step
    return out


class ConstPool:

    def __init__(self, p, nc, ctx, n=32):
        self.buf = ctx.enter_context(nc.sbuf_tensor("sb_constpool", [128, n], F32))
        self.p = p
        self.vals = {}

    def get(self, v):
        v = float(v)
        if v not in self.vals:
            i = len(self.vals)
            self.vals[v] = i
            self.p.op("vector",
                      lambda eng, i=i, v=v: eng.memset(self.buf[:, i:i + 1], v),
                      writes=[bkey(self.buf, i)], tag="const")
        i = self.vals[v]
        return self.buf[:, i:i + 1], bkey(self.buf, i)

    def get_p(self, v, pn):
        ap, k = self.get(v)
        return self.buf[0:pn, self.vals[float(v)]:self.vals[float(v)] + 1], k


def emit_layernorm(p, nc, T, x_sb, lnb, sq_bf, ones_bf, ps_m, ps_v,
                   mea, var, rstd, tmpf, xkey="x", eps=1e-5, lnf=None):
    kx = lambda c: bkey(x_sb, (xkey, c))
    inv_d = 1.0 / D
    for t0, tl in split_chunks(T, 512):
        sl = slice(t0, t0 + tl)
        for c in range(NCHUNK):
            p.op("vector",
                 lambda eng, c=c, sl=sl, tl=tl: eng.tensor_copy(
                     lnb[:, c, sl], x_sb[:, c, sl]),
                 reads=[kx(c)], writes=[bkey(lnb, c)], tag="ln_cast")
            p.op("scalar",
                 lambda eng, c=c, sl=sl, tl=tl: eng.activation(
                     sq_bf[:, c, 0:tl], x_sb[:, c, sl], AF.Square),
                 reads=[kx(c)], writes=[bkey(sq_bf, c)], tag="ln_sq")
        for c in range(NCHUNK):
            p.op("tensor",
                 lambda eng, c=c, sl=sl, tl=tl: eng.matmul(
                     ps_m[:, 0:tl], ones_bf[:], lnb[:, c, sl],
                     start=(c == 0), stop=(c == NCHUNK - 1)),
                 reads=[bkey(lnb, c), bkey(ones_bf)],
                 writes=[bkey(ps_m)], tag="ln_summ")
        for c in range(NCHUNK):
            p.op("tensor",
                 lambda eng, c=c, tl=tl: eng.matmul(
                     ps_v[:, 0:tl], ones_bf[:], sq_bf[:, c, 0:tl],
                     start=(c == 0), stop=(c == NCHUNK - 1)),
                 reads=[bkey(sq_bf, c), bkey(ones_bf)],
                 writes=[bkey(ps_v)], tag="ln_sumv")
        p.op("vector", lambda eng, tl=tl: eng.tensor_scalar(
            mea[:, 0:tl], ps_m[:, 0:tl], inv_d, None, ALU.mult),
            reads=[bkey(ps_m)], writes=[bkey(mea)], tag="ln_mean")
        p.op("vector", lambda eng, tl=tl: eng.tensor_scalar(
            var[:, 0:tl], ps_v[:, 0:tl], inv_d, None, ALU.mult),
            reads=[bkey(ps_v)], writes=[bkey(var)], tag="ln_var0")
        p.op("scalar", lambda eng, tl=tl: eng.activation(
            rstd[:, 0:tl], mea[:, 0:tl], AF.Square),
            reads=[bkey(mea)], writes=[bkey(rstd)], tag="ln_m2")
        p.op("vector", lambda eng, tl=tl: eng.tensor_sub(
            var[:, 0:tl], var[:, 0:tl], rstd[:, 0:tl]),
            reads=[bkey(var), bkey(rstd)], writes=[bkey(var)], tag="ln_var")
        eps_ap, eps_k = cp.get(eps)
        p.op("scalar", lambda eng, tl=tl, eps_ap=eps_ap: eng.activation(
            var[:, 0:tl], var[:, 0:tl], AF.Ln, bias=eps_ap),
            reads=[bkey(var), eps_k], writes=[bkey(var)], tag="ln_ln")
        p.op("scalar", lambda eng, tl=tl: eng.activation(
            rstd[:, 0:tl], var[:, 0:tl], AF.Exp, scale=-0.5),
            reads=[bkey(var)], writes=[bkey(rstd)], tag="ln_rstd")
        for c in range(NCHUNK):
            p.op("vector",
                 lambda eng, c=c, sl=sl, tl=tl: eng.tensor_sub(
                     tmpf[:, 0:tl], x_sb[:, c, sl], mea[:, 0:tl]),
                 reads=[kx(c), bkey(mea)], writes=[bkey(tmpf)], tag="ln_sub")
            p.op("vector",
                 lambda eng, c=c, sl=sl, tl=tl: eng.tensor_mul(
                     lnb[:, c, sl], tmpf[:, 0:tl], rstd[:, 0:tl]),
                 reads=[bkey(tmpf), bkey(rstd)], writes=[bkey(lnb, c)],
                 tag="ln_out")
            if lnf is not None:
                p.op("vector",
                     lambda eng, c=c, sl=sl, tl=tl: eng.tensor_mul(
                         lnf[:, c, sl], tmpf[:, 0:tl], rstd[:, 0:tl]),
                     reads=[bkey(tmpf), bkey(rstd)],
                     writes=[bkey(lnf, ("x", c))], tag="ln_outf")


def emit_router(p, nc, T, lnb, gwT_sb, wT8, ps_r, ps_t, l_sb, m1, m2, negm1,
                e2t, mask_u8, keep, neg8, iden_f32, tag="rt"):
    ntile = T // 128
    p.op("vector", lambda eng: eng.memset(neg8[:], -1e30), writes=[bkey(neg8)])
    for ti in range(ntile):
        t0 = ti * 128
        for c in range(NCHUNK):
            p.op("tensor",
                 lambda eng, c=c, t0=t0: eng.matmul(
                     ps_r[:, 0:8], lnb[:, c, t0:t0 + 128], gwT_sb[:, c, 0:8],
                     start=(c == 0), stop=(c == NCHUNK - 1)),
                 reads=[bkey(lnb, c), bkey(gwT_sb)],
                 writes=[bkey(ps_r)], tag=tag + "_mm")
        p.op("vector", lambda eng: eng.tensor_copy(l_sb[:, 0:8], ps_r[:, 0:8]),
             reads=[bkey(ps_r)], writes=[bkey(l_sb)], tag=tag)
        p.op("vector", lambda eng: eng.reduce_max(m1[:, 0:1], l_sb[:, 0:8],
                                                  axis=AX.X),
             reads=[bkey(l_sb)], writes=[bkey(m1)], tag=tag)
        p.op("vector", lambda eng: eng.tensor_scalar(mask_u8[:, 0:8],
                                                     l_sb[:, 0:8], m1[:, 0:1],
                                                     None, ALU.is_ge),
             reads=[bkey(l_sb), bkey(m1)], writes=[bkey(mask_u8)], tag=tag)
        p.op("vector", lambda eng: eng.select(keep[:, 0:8], mask_u8[:, 0:8],
                                              neg8[:, 0:8], l_sb[:, 0:8],
                                              add_drain=True),
             reads=[bkey(mask_u8), bkey(neg8), bkey(l_sb)],
             writes=[bkey(keep)], tag=tag)
        p.op("vector", lambda eng: eng.reduce_max(m2[:, 0:1], keep[:, 0:8],
                                                  axis=AX.X),
             reads=[bkey(keep)], writes=[bkey(m2)], tag=tag)
        p.op("vector", lambda eng: eng.tensor_scalar(negm1[:, 0:1], m1[:, 0:1],
                                                     -1.0, None, ALU.mult),
             reads=[bkey(m1)], writes=[bkey(negm1)], tag=tag)
        p.op("scalar", lambda eng: eng.activation(l_sb[:, 0:8], l_sb[:, 0:8],
                                                  AF.Exp, bias=negm1[:, 0:1]),
             reads=[bkey(l_sb), bkey(negm1)], writes=[bkey(l_sb)], tag=tag)
        p.op("scalar", lambda eng: eng.activation(e2t[:, 0:1], m2[:, 0:1],
                                                  AF.Exp, bias=negm1[:, 0:1]),
             reads=[bkey(m2), bkey(negm1)], writes=[bkey(e2t)], tag=tag)
        p.op("vector", lambda eng: eng.tensor_scalar(e2t[:, 0:1], e2t[:, 0:1],
                                                     1.0, None, ALU.add),
             reads=[bkey(e2t)], writes=[bkey(e2t)], tag=tag)
        p.op("vector", lambda eng: eng.reciprocal(e2t[:, 0:1], e2t[:, 0:1]),
             reads=[bkey(e2t)], writes=[bkey(e2t)], tag=tag)
        p.op("vector", lambda eng: eng.tensor_scalar(keep[:, 0:8],
                                                     ps_r[:, 0:8], m2[:, 0:1],
                                                     None, ALU.is_ge),
             reads=[bkey(ps_r), bkey(m2)], writes=[bkey(keep)], tag=tag)
        p.op("vector", lambda eng: eng.tensor_mul(l_sb[:, 0:8], l_sb[:, 0:8],
                                                  keep[:, 0:8]),
             reads=[bkey(l_sb), bkey(keep)], writes=[bkey(l_sb)], tag=tag)
        p.op("vector", lambda eng: eng.tensor_scalar(l_sb[:, 0:8], l_sb[:, 0:8],
                                                     e2t[:, 0:1], None,
                                                     ALU.mult),
             reads=[bkey(l_sb), bkey(e2t)], writes=[bkey(l_sb)], tag=tag)
        p.op("tensor", lambda eng: eng.transpose(ps_t[0:8, 0:128],
                                                 l_sb[:, 0:8], iden_f32[:]),
             reads=[bkey(l_sb), bkey(iden_f32)], writes=[bkey(ps_t)], tag=tag)
        p.op("vector",
             lambda eng, t0=t0: eng.tensor_copy(wT8[0:8, t0:t0 + 128],
                                                ps_t[0:8, 0:128]),
             reads=[bkey(ps_t)], writes=[bkey(wT8, ti)], tag=tag)





F_L0 = 1408
NF_L0 = F_L0 // 128


def emit_moe(p, nc, T, bufs, w, NF=NF_L0, norm="ln", nw_sb=None,
             rn_eps=1e-5, xres_ap=None, xres_t=None):
    x_sb = bufs["x_sb"]; lnb = bufs["lnb"]; PS = bufs["PS"]
    scr = bufs["scr_bf"]
    w_gu = bufs["w_gu"]
    w_dn = bufs["w_dn"]
    wT8 = bufs["wT8"]
    ones_bf = bufs["ones_bf"]; iden_f32 = bufs["iden_f32"]
    ind8 = bufs["ind8"]
    gwT_sb = bufs["gwT_sb"]

    skey = lambda off, ln: [bkey(scr, i) for i in range(off // 512, (off + ln + 511) // 512)]

    if norm == "ln":
        emit_layernorm_flat(p, nc, T, x_sb, lnb, scr, ones_bf, PS[0], PS[1],
                            bufs["mea"], bufs["var"], bufs["rstd"],
                            bufs["tmpf"], skey, cp=bufs["cp"])
    else:
        emit_rmsnorm_flat(p, nc, T, x_sb, lnb, scr, ones_bf, PS[1],
                          bufs["var"], bufs["rstd"], skey, bufs["cp"], nw_sb,
                          eps=rn_eps)
    if xres_ap is not None:
        for t0, tl in split_chunks(T, 256):
            dma_load_c(p, "sync", xres_t[:, :, 0:tl],
                       xres_ap[:, :, t0:t0 + tl], NCHUNK,
                       dst_keys=[bkey(xres_t, c) for c in range(NCHUNK)],
                       tag="xr_ld")
            for c in range(NCHUNK):
                p.op("vector",
                     lambda eng, c=c, t0=t0, tl=tl: eng.tensor_add(
                         x_sb[:, c, t0:t0 + tl], x_sb[:, c, t0:t0 + tl],
                         xres_t[:, c, 0:tl]),
                     reads=[bkey(x_sb, ("x", c)), bkey(xres_t, c)],
                     writes=[bkey(x_sb, ("x", c))], tag="xr_add")
    emit_router(p, nc, T, lnb, gwT_sb, wT8, PS[2], PS[3], bufs["l_sb"],
                bufs["m1"], bufs["m2"], bufs["negm1"], bufs["e2t"],
                bufs["mask_u8"], bufs["keep"], bufs["neg8"], iden_f32)

    strips = split_chunks(T, 512)
    for e in range(8):
        dma_load(p, "gpsimd", w_gu[:, :, 0:NF * 128],
                 w["gpT"][e].rearrange("q (c f) -> q c f", c=NCHUNK),
                 dst_keys=[bkey(w_gu, j) for j in range(4)], tag="gp_ld")
        for t0, tl in strips:
            for fc in range(NF):
                pg = PS[4 + fc % 2]
                for c in range(NCHUNK):
                    p.op("tensor",
                         lambda eng, c=c, fc=fc, t0=t0, tl=tl, pg=pg: eng.matmul(
                             pg[:, 0:tl],
                             w_gu[:, c, fc * 128:(fc + 1) * 128],
                             lnb[:, c, t0:t0 + tl],
                             start=(c == 0), stop=(c == NCHUNK - 1)),
                         reads=[bkey(w_gu, c // 4), bkey(lnb, c)],
                         writes=[bkey(pg)], tag="g_mm")
                off = fc * T + t0
                p.op("scalar",
                     lambda eng, fc=fc, t0=t0, tl=tl, pg=pg, off=off: eng.activation(
                         scr[:, off:off + tl], pg[:, 0:tl], AF.Silu),
                     reads=[bkey(pg)], writes=skey(off, tl), tag="g_silu")
        dma_load(p, "gpsimd", w_gu[:, :, 0:NF * 128],
                 w["upT"][e].rearrange("q (c f) -> q c f", c=NCHUNK),
                 dst_keys=[bkey(w_gu, j) for j in range(4)], tag="up_ld")
        for t0, tl in strips:
            for ti in range(tl // 128):
                p.op("tensor",
                     lambda eng, e=e, t0=t0, ti=ti: eng.matmul(
                         PS[6][:, ti * 128:(ti + 1) * 128],
                         ind8[0:8, e * 128:(e + 1) * 128],
                         wT8[0:8, t0 + ti * 128:t0 + (ti + 1) * 128],
                         start=True, stop=True),
                     reads=[bkey(ind8), bkey(wT8, (t0 + ti * 128) // 128)],
                     writes=[bkey(PS[6])], tag="wb_mm")
            for fc in range(NF):
                pu = PS[4 + fc % 2]
                for c in range(NCHUNK):
                    p.op("tensor",
                         lambda eng, c=c, fc=fc, t0=t0, tl=tl, pu=pu: eng.matmul(
                             pu[:, 0:tl],
                             w_gu[:, c, fc * 128:(fc + 1) * 128],
                             lnb[:, c, t0:t0 + tl],
                             start=(c == 0), stop=(c == NCHUNK - 1)),
                         reads=[bkey(w_gu, c // 4), bkey(lnb, c)],
                         writes=[bkey(pu)], tag="u_mm")
                off = fc * T + t0
                p.op("vector",
                     lambda eng, tl=tl, pu=pu, off=off: eng.tensor_mul(
                         scr[:, off:off + tl], scr[:, off:off + tl],
                         pu[:, 0:tl]),
                     reads=skey(off, tl) + [bkey(pu)], writes=skey(off, tl),
                     tag="h_mul")
                p.op("vector",
                     lambda eng, tl=tl, off=off: eng.tensor_mul(
                         scr[:, off:off + tl], scr[:, off:off + tl],
                         PS[6][:, 0:tl]),
                     reads=skey(off, tl) + [bkey(PS[6])], writes=skey(off, tl),
                     tag="h_wb")
        for half in range(2):
            dma_load(p, "gpsimd", w_dn[:, 0:NF, :],
                     w["dnT_h"][e, half].rearrange("q (c d) -> q c d",
                                                   c=NF),
                     dst_keys=[bkey(w_dn)], tag="dn_ld")
            for t0, tl in strips:
                for dcl in range(8):
                    dc = half * 8 + dcl
                    pd = PS[4 + dcl % 2]
                    for fc in range(NF):
                        off = fc * T + t0
                        p.op("tensor",
                             lambda eng, fc=fc, dcl=dcl, t0=t0, tl=tl, pd=pd,
                             off=off, NF=NF: eng.matmul(
                                 pd[:, 0:tl],
                                 w_dn[:, fc, dcl * 128:(dcl + 1) * 128],
                                 scr[:, off:off + tl],
                                 start=(fc == 0), stop=(fc == NF - 1)),
                             reads=[bkey(w_dn)] + skey(off, tl),
                             writes=[bkey(pd)], tag="d_mm")
                    p.op("vector",
                         lambda eng, dc=dc, t0=t0, tl=tl, pd=pd:
                         eng.scalar_tensor_tensor(
                             x_sb[:, dc, t0:t0 + tl], pd[:, 0:tl], 1.0,
                             x_sb[:, dc, t0:t0 + tl], ALU.mult, ALU.add),
                         reads=[bkey(pd), bkey(x_sb, ("x", dc))],
                         writes=[bkey(x_sb, ("x", dc))], tag="d_acc")


def emit_layernorm_flat(p, nc, T, x_sb, lnb, scr, ones_bf, ps_m, ps_v,
                        mea, var, rstd, tmpf, skey, cp=None, xkey="x",
                        eps=1e-5, lnf=None):
    kx = lambda c: bkey(x_sb, (xkey, c))
    inv_d = 1.0 / D
    for t0, tl in split_chunks(T, 512):
        sl = slice(t0, t0 + tl)
        for c in range(NCHUNK):
            off = c * 512
            p.op("vector",
                 lambda eng, c=c, sl=sl: eng.tensor_copy(
                     lnb[:, c, sl], x_sb[:, c, sl]),
                 reads=[kx(c)], writes=[bkey(lnb, c)], tag="ln_cast")
            p.op("scalar",
                 lambda eng, c=c, sl=sl, tl=tl, off=off: eng.activation(
                     scr[:, off:off + tl], x_sb[:, c, sl], AF.Square),
                 reads=[kx(c)], writes=skey(off, tl), tag="ln_sq")
        for c in range(NCHUNK):
            p.op("tensor",
                 lambda eng, c=c, sl=sl, tl=tl: eng.matmul(
                     ps_m[:, 0:tl], ones_bf[:], lnb[:, c, sl],
                     start=(c == 0), stop=(c == NCHUNK - 1)),
                 reads=[bkey(lnb, c), bkey(ones_bf)],
                 writes=[bkey(ps_m)], tag="ln_summ")
        for c in range(NCHUNK):
            off = c * 512
            p.op("tensor",
                 lambda eng, c=c, tl=tl, off=off: eng.matmul(
                     ps_v[:, 0:tl], ones_bf[:], scr[:, off:off + tl],
                     start=(c == 0), stop=(c == NCHUNK - 1)),
                 reads=skey(off, tl) + [bkey(ones_bf)],
                 writes=[bkey(ps_v)], tag="ln_sumv")
        p.op("vector", lambda eng, tl=tl: eng.tensor_scalar(
            mea[:, 0:tl], ps_m[:, 0:tl], inv_d, None, ALU.mult),
            reads=[bkey(ps_m)], writes=[bkey(mea)], tag="ln_mean")
        p.op("vector", lambda eng, tl=tl: eng.tensor_scalar(
            var[:, 0:tl], ps_v[:, 0:tl], inv_d, None, ALU.mult),
            reads=[bkey(ps_v)], writes=[bkey(var)], tag="ln_var0")
        p.op("scalar", lambda eng, tl=tl: eng.activation(
            rstd[:, 0:tl], mea[:, 0:tl], AF.Square),
            reads=[bkey(mea)], writes=[bkey(rstd)], tag="ln_m2")
        p.op("vector", lambda eng, tl=tl: eng.tensor_sub(
            var[:, 0:tl], var[:, 0:tl], rstd[:, 0:tl]),
            reads=[bkey(var), bkey(rstd)], writes=[bkey(var)], tag="ln_var")
        eps_ap, eps_k = cp.get(eps)
        p.op("scalar", lambda eng, tl=tl, eps_ap=eps_ap: eng.activation(
            var[:, 0:tl], var[:, 0:tl], AF.Ln, bias=eps_ap),
            reads=[bkey(var), eps_k], writes=[bkey(var)], tag="ln_ln")
        p.op("scalar", lambda eng, tl=tl: eng.activation(
            rstd[:, 0:tl], var[:, 0:tl], AF.Exp, scale=-0.5),
            reads=[bkey(var)], writes=[bkey(rstd)], tag="ln_rstd")
        for c in range(NCHUNK):
            p.op("vector",
                 lambda eng, c=c, sl=sl, tl=tl: eng.tensor_sub(
                     tmpf[:, 0:tl], x_sb[:, c, sl], mea[:, 0:tl]),
                 reads=[kx(c), bkey(mea)], writes=[bkey(tmpf)], tag="ln_sub")
            p.op("vector",
                 lambda eng, c=c, sl=sl, tl=tl: eng.tensor_mul(
                     lnb[:, c, sl], tmpf[:, 0:tl], rstd[:, 0:tl]),
                 reads=[bkey(tmpf), bkey(rstd)], writes=[bkey(lnb, c)],
                 tag="ln_out")
            if lnf is not None:
                p.op("vector",
                     lambda eng, c=c, sl=sl, tl=tl: eng.tensor_mul(
                         lnf[:, c, sl], tmpf[:, 0:tl], rstd[:, 0:tl]),
                     reads=[bkey(tmpf), bkey(rstd)],
                     writes=[bkey(lnf, ("x", c))], tag="ln_outf")


def alloc_common(nc, ctx, T, want=()):
    sb = lambda name, shape, dt=F32: ctx.enter_context(
        nc.sbuf_tensor("sb_" + name, shape, dt))
    bufs = {}
    bufs["x_sb"] = sb("x_sb", [128, NCHUNK, T])
    bufs["lnb"] = sb("lnb", [128, NCHUNK, T], BF16)
    bufs["mea"] = sb("mea", [128, 512])
    bufs["var"] = sb("var", [128, 512])
    bufs["rstd"] = sb("rstd", [128, 512])
    bufs["tmpf"] = sb("tmpf", [128, 512])
    bufs["ones_bf"] = sb("ones_bf", [128, 128], BF16)
    bufs["iden_f32"] = sb("iden_f32", [128, 128])
    bufs["PS"] = [ctx.enter_context(nc.psum_tensor(f"ps{i}", [128, 512], F32))
                  for i in range(8)]
    if "router" in want:
        bufs["wT8"] = sb("wT8", [8, T])
        bufs["gwT_sb"] = sb("gwT_sb", [128, NCHUNK, 8], BF16)
        bufs["ind8"] = sb("ind8", [8, 8 * 128])
        bufs["l_sb"] = sb("l_sb", [128, 8])
        bufs["m1"] = sb("m1", [128, 1])
        bufs["m2"] = sb("m2", [128, 1])
        bufs["negm1"] = sb("negm1", [128, 1])
        bufs["e2t"] = sb("e2t", [128, 1])
        bufs["mask_u8"] = sb("mask_u8", [128, 8], U8)
        bufs["keep"] = sb("keep", [128, 8])
        bufs["neg8"] = sb("neg8", [128, 8])
    return bufs


def load_common(p, nc, T, bufs, aps, want=()):
    dma_load(p, "sync", bufs["x_sb"][:, :, :],
             aps["x"].rearrange("q (c t) -> q c t", c=NCHUNK),
             dst_keys=[bkey(bufs["x_sb"], ("x", c)) for c in range(NCHUNK)],
             tag="x_ld")
    dma_load(p, "sync", bufs["ones_bf"][:], aps["ones_bf"],
             dst_keys=[bkey(bufs["ones_bf"])], tag="c_ld")
    dma_load(p, "sync", bufs["iden_f32"][:], aps["iden_f32"],
             dst_keys=[bkey(bufs["iden_f32"])], tag="c_ld")
    if "router" in want:
        dma_load(p, "sync", bufs["ind8"][:], aps["ind8"],
                 dst_keys=[bkey(bufs["ind8"])], tag="c_ld")
        dma_load(p, "gpsimd", bufs["gwT_sb"][:, :, :],
                 aps["gwT_h"].rearrange("q (c e) -> q c e", c=NCHUNK),
                 dst_keys=[bkey(bufs["gwT_sb"])], tag="c_ld")


def store_x(p, nc, T, bufs, xo_ap, t_lo, t_hi):
    ks = [bkey(bufs["x_sb"], ("x", c)) for c in range(NCHUNK)]
    dma_load(p, "sync",
             xo_ap.rearrange("q (c t) -> q c t", c=NCHUNK),
             bufs["x_sb"][:, :, t_lo:t_hi],
             dst_keys=[("OUT", 0)], src_keys=ks, tag="x_st")


def build_ffn0(T):
    nc = bass.Bass("TRN2", target_bir_lowering=False)
    aps = {}
    aps["x"] = nc.dram_tensor("x", [128, NCHUNK * T], F32,
                              kind="ExternalInput").ap()
    aps["gpT_h"] = nc.dram_tensor("gpT_h", [8, 128, NCHUNK * F_L0], F32,
                                  kind="ExternalInput").ap()
    aps["upT_h"] = nc.dram_tensor("upT_h", [8, 128, NCHUNK * F_L0], F32,
                                  kind="ExternalInput").ap()
    aps["dnT_h"] = nc.dram_tensor("dnT_h", [8, 2, 128, NF_L0 * 1024], F32,
                                  kind="ExternalInput").ap()
    aps["gwT_h"] = nc.dram_tensor("gwT_h", [128, NCHUNK * 8], F32,
                                  kind="ExternalInput").ap()
    aps["ones_bf"] = nc.dram_tensor("ones_bf", [128, 128], BF16,
                                    kind="ExternalInput").ap()
    aps["iden_f32"] = nc.dram_tensor("iden_f32", [128, 128], F32,
                                     kind="ExternalInput").ap()
    aps["ind8"] = nc.dram_tensor("ind8", [8, 8 * 128], F32,
                                 kind="ExternalInput").ap()
    xo = nc.dram_tensor("xo", [128, NCHUNK * T], F32,
                        kind="ExternalOutput").ap()
    with ExitStack() as ctx:
        bufs = alloc_common(nc, ctx, T, want=("router",))
        sb = lambda name, shape, dt=F32: ctx.enter_context(
            nc.sbuf_tensor(name, shape, dt))
        bufs["scr_bf"] = sb("scr_bf", [128, NF_L0 * T], BF16)
        bufs["w_gu"] = sb("w_gu", [128, NCHUNK, F_L0], BF16)
        bufs["w_dn"] = sb("w_dn", [128, NF_L0, 1024], BF16)
        p = Prog(nc)
        bufs["cp"] = ConstPool(p, nc, ctx)
        load_common(p, nc, T, bufs, aps, want=("router",))
        w = {"gpT": [aps["gpT_h"][e] for e in range(8)],
             "upT": [aps["upT_h"][e] for e in range(8)],
             "dnT_h": aps["dnT_h"]}
        emit_moe(p, nc, T, bufs, w, NF=NF_L0)
        store_x(p, nc, T, bufs, xo, 0, T)
        p.emit_all(ctx)
    return nc


def prep_ffn0_weights(inputs):
    gp = np.asarray(inputs["L0_ffn_gate_proj"])
    up = np.asarray(inputs["L0_ffn_up_proj"])
    dn = np.asarray(inputs["L0_ffn_down_proj"])
    gw = np.asarray(inputs["L0_ffn_gate_w"])
    gpT = np.ascontiguousarray(gp.transpose(0, 2, 1))
    upT = np.ascontiguousarray(up.transpose(0, 2, 1))
    dnT = np.ascontiguousarray(dn.transpose(0, 2, 1))
    gpT_h = gpT.reshape(8, NCHUNK, 128, F_L0).transpose(0, 2, 1, 3).reshape(
        8, 128, NCHUNK * F_L0).copy()
    upT_h = upT.reshape(8, NCHUNK, 128, F_L0).transpose(0, 2, 1, 3).reshape(
        8, 128, NCHUNK * F_L0).copy()
    dnT_4d = dnT.reshape(8, NF_L0, 128, 2048)
    dnT_h = np.stack(
        [dnT_4d[:, :, :, 0:1024], dnT_4d[:, :, :, 1024:2048]], axis=1
    ).transpose(0, 1, 3, 2, 4).reshape(8, 2, 128, NF_L0 * 1024).copy()
    gwT = np.ascontiguousarray(gw.T)
    gwT_h = gwT.reshape(NCHUNK, 128, 8).transpose(1, 0, 2).reshape(
        128, NCHUNK * 8).copy()
    return {"gpT_h": gpT_h, "upT_h": upT_h, "dnT_h": dnT_h, "gwT_h": gwT_h}


def consts_common():
    return {
        "ones_bf": np.ones((128, 128), dtype=bf),
        "iden_f32": np.eye(128, dtype=np.float32),
        "ind8": np.repeat(np.eye(8, dtype=np.float32), 128, axis=1),
    }


def to_fm(x_bt_d):
    T = x_bt_d.shape[0]
    return np.ascontiguousarray(
        x_bt_d.T.reshape(NCHUNK, 128, T).transpose(1, 0, 2).reshape(
            128, NCHUNK * T))


def from_fm(dev, T):
    return np.ascontiguousarray(
        dev.reshape(128, NCHUNK, T).transpose(1, 0, 2).reshape(D, T).T)


def emit_l1_ffn(p, nc, T, bufs, w, tbase=0):
    assert tbase % 4 == 0
    x_sb = bufs["x_sb"]; lnb = bufs["lnb"]; PS = bufs["PS"]
    scr = bufs["scr_bf"]; w_gu = bufs["w_gu16"]; cp = bufs["cp"]
    ones_bf = bufs["ones_bf"]
    Te = T // 4
    skey = lambda off, ln: [bkey(scr, i) for i in
                            range(off // 512, (off + ln + 511) // 512)]
    emit_layernorm_flat(p, nc, T, x_sb, lnb, scr, ones_bf, PS[0], PS[1],
                        bufs["mea"], bufs["var"], bufs["rstd"], bufs["tmpf"],
                        skey, cp=cp)
    for e in range(4):
        for q in range(4):
            dma_load(p, "gpsimd", w_gu[:, :, :],
                     w["guT_q"][e, q].rearrange("r (c f) -> r c f", c=NCHUNK),
                     dst_keys=[bkey(w_gu)], tag="gu_ld")
            for j in range(8):
                oc = q * 8 + j
                pg = PS[4 + j % 2]
                for c in range(NCHUNK):
                    p.op("tensor",
                         lambda eng, c=c, j=j, e=e, pg=pg, Te=Te: eng.matmul(
                             pg[:, 0:Te],
                             w_gu[:, c, j * 128:(j + 1) * 128],
                             lnb[:, c, e::4],
                             start=(c == 0), stop=(c == NCHUNK - 1)),
                         reads=[bkey(w_gu), bkey(lnb, c)],
                         writes=[bkey(pg)], tag="gu_mm")
                ic = oc % 16
                off = ic * Te
                if oc < 16:
                    p.op("scalar",
                         lambda eng, pg=pg, off=off, Te=Te: eng.activation(
                             scr[:, off:off + Te], pg[:, 0:Te], AF.Silu),
                         reads=[bkey(pg)], writes=skey(off, Te), tag="gu_silu")
                else:
                    p.op("vector",
                         lambda eng, pg=pg, off=off, Te=Te: eng.tensor_mul(
                             scr[:, off:off + Te], scr[:, off:off + Te],
                             pg[:, 0:Te]),
                         reads=skey(off, Te) + [bkey(pg)],
                         writes=skey(off, Te), tag="gu_h")
        for half in range(2):
            dma_load(p, "gpsimd", w_gu[:, :, :],
                     w["dnT_hh"][e, half].rearrange("r (c f) -> r c f",
                                                    c=NCHUNK),
                     dst_keys=[bkey(w_gu)], tag="dn_ld")
            for j in range(8):
                dc = half * 8 + j
                pd = PS[4 + j % 2]
                for ic in range(NCHUNK):
                    off = ic * Te
                    p.op("tensor",
                         lambda eng, ic=ic, j=j, pd=pd, off=off, Te=Te:
                         eng.matmul(
                             pd[:, 0:Te],
                             w_gu[:, ic, j * 128:(j + 1) * 128],
                             scr[:, off:off + Te],
                             start=(ic == 0), stop=(ic == NCHUNK - 1)),
                         reads=[bkey(w_gu)] + skey(off, Te),
                         writes=[bkey(pd)], tag="dn_mm")
                p.op("vector",
                     lambda eng, dc=dc, e=e, pd=pd, Te=Te:
                     eng.scalar_tensor_tensor(
                         x_sb[:, dc, e::4], pd[:, 0:Te], 1.0,
                         x_sb[:, dc, e::4], ALU.mult, ALU.add),
                     reads=[bkey(pd), bkey(x_sb, ("x", dc))],
                     writes=[bkey(x_sb, ("x", dc))], tag="dn_acc")


def build_l1_ffn(T, tbase=0):
    nc = bass.Bass("TRN2", target_bir_lowering=False)
    aps = {}
    aps["x"] = nc.dram_tensor("x", [128, NCHUNK * T], F32,
                              kind="ExternalInput").ap()
    aps["guT_q"] = nc.dram_tensor("guT_q", [4, 4, 128, NCHUNK * 1024], F32,
                                  kind="ExternalInput").ap()
    aps["dnT_hh"] = nc.dram_tensor("dnT_hh", [4, 2, 128, NCHUNK * 1024], F32,
                                   kind="ExternalInput").ap()
    aps["ones_bf"] = nc.dram_tensor("ones_bf", [128, 128], BF16,
                                    kind="ExternalInput").ap()
    aps["iden_f32"] = nc.dram_tensor("iden_f32", [128, 128], F32,
                                     kind="ExternalInput").ap()
    xo = nc.dram_tensor("xo", [128, NCHUNK * T], F32,
                        kind="ExternalOutput").ap()
    with ExitStack() as ctx:
        bufs = alloc_common(nc, ctx, T)
        sb = lambda name, shape, dt=F32: ctx.enter_context(
            nc.sbuf_tensor("sb2_" + name, shape, dt))
        bufs["scr_bf"] = sb("scr_bf", [128, max(NCHUNK * 512, NCHUNK * (T // 4))], BF16)
        bufs["w_gu16"] = sb("w_gu16", [128, NCHUNK, 1024], BF16)
        p = Prog(nc)
        bufs["cp"] = ConstPool(p, nc, ctx)
        load_common(p, nc, T, bufs, aps)
        w = {"guT_q": aps["guT_q"], "dnT_hh": aps["dnT_hh"]}
        emit_l1_ffn(p, nc, T, bufs, w, tbase)
        store_x(p, nc, T, bufs, xo, 0, T)
        p.emit_all(ctx)
    return nc


def prep_l1_ffn_weights(inputs):
    gu = np.asarray(inputs["L1_ffn_gate_up_proj"])
    dn = np.asarray(inputs["L1_ffn_down_proj"])
    guT_q = gu.reshape(4, NCHUNK, 128, 4, 1024).transpose(0, 3, 2, 1, 4) \
        .reshape(4, 4, 128, NCHUNK * 1024).copy()
    dnT_hh = dn.reshape(4, NCHUNK, 128, 2, 1024).transpose(0, 3, 2, 1, 4) \
        .reshape(4, 2, 128, NCHUNK * 1024).copy()
    return {"guT_q": guT_q, "dnT_hh": dnT_hh}


def emit_rmsnorm_flat(p, nc, T, x_sb, lnb, scr, ones_bf, ps_v, var, rstd,
                      skey, cp, nw_sb, xkey="x", eps=1e-5):
    kx = lambda c: bkey(x_sb, (xkey, c))
    inv_d = 1.0 / D
    for t0, tl in split_chunks(T, 512):
        sl = slice(t0, t0 + tl)
        for c in range(NCHUNK):
            off = c * 512
            p.op("scalar",
                 lambda eng, c=c, sl=sl, tl=tl, off=off: eng.activation(
                     scr[:, off:off + tl], x_sb[:, c, sl], AF.Square),
                 reads=[kx(c)], writes=skey(off, tl), tag="rn_sq")
        for c in range(NCHUNK):
            off = c * 512
            p.op("tensor",
                 lambda eng, c=c, tl=tl, off=off: eng.matmul(
                     ps_v[:, 0:tl], ones_bf[:], scr[:, off:off + tl],
                     start=(c == 0), stop=(c == NCHUNK - 1)),
                 reads=skey(off, tl) + [bkey(ones_bf)],
                 writes=[bkey(ps_v)], tag="rn_sum")
        eps_ap, eps_k = cp.get(eps)
        p.op("vector", lambda eng, tl=tl: eng.tensor_scalar(
            var[:, 0:tl], ps_v[:, 0:tl], inv_d, None, ALU.mult),
            reads=[bkey(ps_v)], writes=[bkey(var)], tag="rn_ms")
        p.op("scalar", lambda eng, tl=tl, eps_ap=eps_ap: eng.activation(
            var[:, 0:tl], var[:, 0:tl], AF.Ln, bias=eps_ap),
            reads=[bkey(var), eps_k], writes=[bkey(var)], tag="rn_ln")
        p.op("scalar", lambda eng, tl=tl: eng.activation(
            rstd[:, 0:tl], var[:, 0:tl], AF.Exp, scale=-0.5),
            reads=[bkey(var)], writes=[bkey(rstd)], tag="rn_rstd")
        for c in range(NCHUNK):
            p.op("vector",
                 lambda eng, c=c, sl=sl, tl=tl: eng.tensor_mul(
                     lnb[:, c, sl], x_sb[:, c, sl], rstd[:, 0:tl]),
                 reads=[kx(c), bkey(rstd)], writes=[bkey(lnb, c)],
                 tag="rn_mul")
            p.op("vector",
                 lambda eng, c=c, sl=sl: eng.tensor_scalar(
                     lnb[:, c, sl], lnb[:, c, sl], nw_sb[:, c:c + 1], None,
                     ALU.mult),
                 reads=[bkey(lnb, c), bkey(nw_sb)], writes=[bkey(lnb, c)],
                 tag="rn_w")


def emit_proj_to_dram(p, nc, T, lnb, w_sb, w_hbm_ap, n_oc, out_dram, stage,
                      stagef, PS, bufs, scale=None, rope=False, rope_t0=0,
                      tag="pj", strip=512):
    wv = w_hbm_ap.rearrange("q (c f) -> q c f", c=NCHUNK)
    halves = split_chunks(n_oc, max(1, n_oc // 2)) if n_oc > 1 else [(0, 1)]
    for h0, hl in halves:
        dma_load_c(p, "gpsimd", w_sb[:, :, 0:hl * 128],
                   wv[:, :, h0 * 128:(h0 + hl) * 128], NCHUNK,
                   dst_keys=[bkey(w_sb)], tag=tag + "_wld")
        for j in range(hl):
            oc = h0 + j
            for t0, tl in split_chunks(T, strip):
                pg = PS[4 + (j + t0 // strip) % 2]
                for c in range(NCHUNK):
                    p.op("tensor",
                         lambda eng, c=c, j=j, t0=t0, tl=tl, pg=pg: eng.matmul(
                             pg[:, 0:tl], w_sb[:, c, j * 128:(j + 1) * 128],
                             lnb[:, c, t0:t0 + tl],
                             start=(c == 0), stop=(c == NCHUNK - 1)),
                         reads=[bkey(w_sb), bkey(lnb, c)],
                         writes=[bkey(pg)], tag=tag + "_mm")
                if scale is not None:
                    p.op("scalar",
                         lambda eng, pg=pg, tl=tl: eng.activation(
                             stage[:, 0:tl], pg[:, 0:tl], AF.Copy,
                             scale=float(scale)),
                         reads=[bkey(pg)], writes=[bkey(stage)], tag=tag)
                else:
                    p.op("vector",
                         lambda eng, pg=pg, tl=tl: eng.tensor_copy(
                             stage[:, 0:tl], pg[:, 0:tl]),
                         reads=[bkey(pg)], writes=[bkey(stage)], tag=tag)
                if rope:
                    CS, SN2, swp, iden_bf = (bufs["CS"], bufs["SN2"],
                                             bufs["swp"], bufs["iden_bf"])
                    ps2 = PS[2]
                    p.op("tensor",
                         lambda eng, tl=tl, ps2=ps2: eng.matmul(
                             ps2[:, 0:tl], swp[:], stage[:, 0:tl],
                             start=True, stop=True),
                         reads=[bkey(stage), bkey(swp)], writes=[bkey(ps2)],
                         tag=tag + "_swp")
                    g0 = rope_t0 + t0
                    p.op("vector",
                         lambda eng, tl=tl, g0=g0, ps2=ps2: eng.tensor_mul(
                             stagef[:, 0:tl], ps2[:, 0:tl],
                             SN2[:, g0:g0 + tl]),
                         reads=[bkey(ps2), bkey(SN2)], writes=[bkey(stagef)],
                         tag=tag)
                    p.op("vector",
                         lambda eng, tl=tl, g0=g0: eng.tensor_mul(
                             stage[:, 0:tl], stage[:, 0:tl],
                             CS[:, g0:g0 + tl]),
                         reads=[bkey(stage), bkey(CS)], writes=[bkey(stage)],
                         tag=tag)
                    p.op("vector",
                         lambda eng, tl=tl: eng.tensor_add(
                             stage[:, 0:tl], stage[:, 0:tl], stagef[:, 0:tl]),
                         reads=[bkey(stage), bkey(stagef)],
                         writes=[bkey(stage)], tag=tag)
                dma_load(p, "sync", out_dram[oc, :, t0:t0 + tl],
                         stage[:, 0:tl], dst_keys=[("dram", out_dram.tensor.name, oc)],
                         src_keys=[bkey(stage)], tag=tag + "_st")


def emit_v_transposed_to_dram(p, nc, T, lnb, w_sb, w_hbm_ap, n_oc, vT_dram,
                              stage, PS, bufs, tag="vj"):
    wv = w_hbm_ap.rearrange("q (c f) -> q c f", c=NCHUNK)
    iden_bf = bufs["iden_bf"]
    halves = split_chunks(n_oc, max(1, n_oc // 2)) if n_oc > 1 else [(0, 1)]
    for h0, hl in halves:
        dma_load_c(p, "gpsimd", w_sb[:, :, 0:hl * 128],
                   wv[:, :, h0 * 128:(h0 + hl) * 128], NCHUNK,
                   dst_keys=[bkey(w_sb)], tag=tag + "_wld")
        for j in range(hl):
            oc = h0 + j
            for t0, tl in split_chunks(T, 512):
                pg = PS[4 + (j + t0 // 512) % 2]
                for c in range(NCHUNK):
                    p.op("tensor",
                         lambda eng, c=c, j=j, t0=t0, tl=tl, pg=pg: eng.matmul(
                             pg[:, 0:tl], w_sb[:, c, j * 128:(j + 1) * 128],
                             lnb[:, c, t0:t0 + tl],
                             start=(c == 0), stop=(c == NCHUNK - 1)),
                         reads=[bkey(w_sb), bkey(lnb, c)],
                         writes=[bkey(pg)], tag=tag + "_mm")
                p.op("vector",
                     lambda eng, pg=pg, tl=tl: eng.tensor_copy(
                         stage[:, 0:tl], pg[:, 0:tl]),
                     reads=[bkey(pg)], writes=[bkey(stage)], tag=tag)
                PT2 = bufs["PT2"]
                for kt in range(tl // 128):
                    ptb = PT2[kt % 2]
                    p.op("tensor",
                         lambda eng, kt=kt, ptb=ptb: eng.transpose(
                             ptb[:, 0:128],
                             stage[:, kt * 128:(kt + 1) * 128], iden_bf[:]),
                         reads=[bkey(stage), bkey(iden_bf)],
                         writes=[bkey(ptb)], tag=tag + "_tr")
                    p.op("vector",
                         lambda eng, kt=kt, ptb=ptb: eng.tensor_copy(
                             stage[:, 2048 + kt * 128:2048 + (kt + 1) * 128],
                             ptb[:, 0:128]),
                         reads=[bkey(ptb)], writes=[bkey(stage, "T")],
                         tag=tag)
                for kt in range(tl // 128):
                    dma_load(p, "sync",
                             vT_dram[oc, t0 // 128 + kt, :, :],
                             stage[:, 2048 + kt * 128:2048 + (kt + 1) * 128],
                             dst_keys=[("dram", vT_dram.tensor.name, oc)],
                             src_keys=[bkey(stage, "T")], tag=tag + "_st")


def emit_attn_phase(p, nc, cfg, bufs, qd, kd, vTd, o_fb, o_off):
    nq, nkv, T, q0, q1 = (cfg["nq"], cfg["nkv"], cfg["T"], cfg["q0"],
                          cfg["q1"])
    Tq = q1 - q0
    PS = bufs["PS"]
    kh = bufs["kh"]
    vh = bufs["vh"]
    qh = bufs["qh"]
    Srow = bufs["Srow"]
    Pb = bufs["Pb"]
    PbT = bufs["PbT"]
    m1 = bufs["am1"]; rs = bufs["ars"]; negm = bufs["anegm"]
    ltri = bufs["ltri"]
    iden_bf = bufs["iden_bf"]
    grp = nq // nkv
    for h in range(nq):
        kv = h // grp
        if h % grp == 0:
            dma_load(p, "sync", kh[:, 0:T], kd[kv, :, 0:T],
                     dst_keys=[bkey(kh)], src_keys=[("dram", kd.tensor.name, kv)],
                     tag="at_kld")
            for kt in range(T // 128):
                dma_load(p, "sync", vh[:, kt * 128:(kt + 1) * 128],
                         vTd[kv, kt, :, :],
                         dst_keys=[bkey(vh)],
                         src_keys=[("dram", vTd.tensor.name, kv)],
                         tag="at_vld")
        for qt in range(Tq // 128):
            gq = q0 + qt * 128
            kend = gq + 128
            nk = kend // 128
            dma_load(p, "sync", qh[:, 0:128], qd[h, :, qt * 128:(qt + 1) * 128],
                     dst_keys=[bkey(qh)], src_keys=[("dram", qd.tensor.name, h)],
                     tag="at_qld")
            for s0, sl2 in split_chunks(kend, 512):
                pg = PS[4 + (s0 // 512) % 2]
                p.op("tensor",
                     lambda eng, s0=s0, sl2=sl2, pg=pg: eng.matmul(
                         pg[:, 0:sl2], qh[:, 0:128], kh[:, s0:s0 + sl2],
                         start=True, stop=True),
                     reads=[bkey(qh), bkey(kh)], writes=[bkey(pg)],
                     tag="at_smm")
                p.op("vector",
                     lambda eng, s0=s0, sl2=sl2, pg=pg: eng.tensor_copy(
                         Srow[:, s0:s0 + sl2], pg[:, 0:sl2]),
                     reads=[bkey(pg)], writes=[bkey(Srow)], tag="at_sc")
            p.op("vector",
                 lambda eng, kend=kend: eng.tensor_add(
                     Srow[:, kend - 128:kend], Srow[:, kend - 128:kend],
                     ltri[:, 0:128]),
                 reads=[bkey(Srow), bkey(ltri)], writes=[bkey(Srow)],
                 tag="at_mask")
            p.op("vector",
                 lambda eng, kend=kend: eng.reduce_max(
                     m1[:, 0:1], Srow[:, 0:kend], axis=AX.X),
                 reads=[bkey(Srow)], writes=[bkey(m1)], tag="at_max")
            p.op("vector", lambda eng: eng.tensor_scalar(
                negm[:, 0:1], m1[:, 0:1], -1.0, None, ALU.mult),
                reads=[bkey(m1)], writes=[bkey(negm)], tag="at_neg")
            p.op("scalar",
                 lambda eng, kend=kend: eng.activation(
                     Pb[:, 0:kend], Srow[:, 0:kend], AF.Exp,
                     bias=negm[:, 0:1], accum_out=rs[:, 0:1]),
                 reads=[bkey(Srow), bkey(negm)],
                 writes=[bkey(Pb), bkey(rs)], tag="at_exp")
            p.op("vector", lambda eng: eng.reciprocal(rs[:, 0:1], rs[:, 0:1]),
                 reads=[bkey(rs)], writes=[bkey(rs)], tag="at_rr")
            p.op("vector",
                 lambda eng, kend=kend: eng.tensor_scalar(
                     Pb[:, 0:kend], Pb[:, 0:kend], rs[:, 0:1], None, ALU.mult),
                 reads=[bkey(Pb), bkey(rs)], writes=[bkey(Pb)], tag="at_pn")
            po = PS[3]
            PT2 = bufs["PT2"]
            for kt in range(nk):
                ptb = PT2[kt % 2]
                p.op("tensor",
                     lambda eng, kt=kt, ptb=ptb: eng.transpose(
                         ptb[:, 0:128], Pb[:, kt * 128:(kt + 1) * 128],
                         iden_bf[:]),
                     reads=[bkey(Pb), bkey(iden_bf)], writes=[bkey(ptb)],
                     tag="at_ptr")
                p.op("vector",
                     lambda eng, kt=kt, ptb=ptb: eng.tensor_copy(
                         PbT[:, kt * 128:(kt + 1) * 128],
                         ptb[:, 0:128]),
                     reads=[bkey(ptb)], writes=[bkey(PbT, kt)],
                     tag="at_ptc")
            for kt in range(nk):
                p.op("tensor",
                     lambda eng, kt=kt, po=po, nk=nk: eng.matmul(
                         po[:, 0:128], vh[:, kt * 128:(kt + 1) * 128],
                         PbT[:, kt * 128:(kt + 1) * 128],
                         start=(kt == 0), stop=(kt == nk - 1)),
                     reads=[bkey(vh), bkey(PbT, kt)], writes=[bkey(po)],
                     tag="at_av")
            off = o_off + h * Tq + qt * 128
            p.op("vector",
                 lambda eng, off=off, po=po: eng.tensor_copy(
                     o_fb[:, off:off + 128], po[:, 0:128]),
                 reads=[bkey(po)], writes=[bkey(o_fb, off // 512)],
                 tag="at_oc")


def emit_out_proj(p, nc, cfg, bufs, woT_ap, o_fb, o_off, xin_ap, xout_ap,
                  has_bias=None, res_key="OUTX", st_sz=256):
    T_q = cfg["q1"] - cfg["q0"]
    PS = bufs["PS"]; w_sb = bufs["w_sb"]; xtmp = bufs["xtmp"]
    wv = woT_ap.rearrange("q (c f) -> q c f", c=NCHUNK)
    for t0, tl in split_chunks(T_q, st_sz):
        dma_load_c(p, "sync", xtmp[:, :, 0:tl], xin_ap[:, :, t0:t0 + tl],
                   NCHUNK,
                   dst_keys=[bkey(xtmp, ("x", c)) for c in range(NCHUNK)],
                   tag="wo_xld")
        for wh in range(2):
            dma_load_c(p, "gpsimd", w_sb[:, :, 0:1024],
                       wv[:, :, wh * 1024:(wh + 1) * 1024], NCHUNK,
                       dst_keys=[bkey(w_sb)], tag="wo_ld")
            for j in range(8):
                oc = wh * 8 + j
                pg = PS[4 + j % 2]
                for c in range(NCHUNK):
                    off = o_off + c * T_q + t0
                    p.op("tensor",
                         lambda eng, c=c, j=j, tl=tl, pg=pg, off=off:
                         eng.matmul(
                             pg[:, 0:tl], w_sb[:, c, j * 128:(j + 1) * 128],
                             o_fb[:, off:off + tl],
                             start=(c == 0), stop=(c == NCHUNK - 1)),
                         reads=[bkey(w_sb), bkey(o_fb, off // 512),
                                bkey(o_fb, (off + tl - 1) // 512)],
                         writes=[bkey(pg)], tag="wo_mm")
                p.op("vector",
                     lambda eng, oc=oc, tl=tl, pg=pg: eng.scalar_tensor_tensor(
                         xtmp[:, oc, 0:tl], pg[:, 0:tl], 1.0,
                         xtmp[:, oc, 0:tl], ALU.mult, ALU.add),
                     reads=[bkey(pg), bkey(xtmp, ("x", oc))],
                     writes=[bkey(xtmp, ("x", oc))], tag="wo_acc")
        dma_load_c(p, "sync", xout_ap[:, :, t0:t0 + tl], xtmp[:, :, 0:tl],
                   NCHUNK, dst_keys=[(res_key, t0)],
                   src_keys=[bkey(xtmp, ("x", c)) for c in range(NCHUNK)],
                   tag="wo_st")


def alloc_attn_bufs(nc, ctx, T, Tq, rope=False):
    sb = lambda name, shape, dt=F32: ctx.enter_context(
        nc.sbuf_tensor("sa_" + name, shape, dt))
    bufs = {}
    bufs["lnb"] = sb("lnb", [128, NCHUNK, T], BF16)
    bufs["scr_bf"] = sb("scr_bf", [128, NCHUNK * 512], BF16)
    bufs["o_fb"] = sb("o_fb", [128, 16 * Tq], BF16)
    bufs["w_sb"] = sb("w_sb", [128, NCHUNK, 1024], BF16)
    bufs["xtmp"] = sb("xtmp", [128, NCHUNK, 128 if rope else 256])
    bufs["kh"] = sb("kh", [128, T], BF16)
    bufs["vh"] = sb("vh", [128, T], BF16)
    bufs["qh"] = sb("qh", [128, 128], BF16)
    bufs["Srow"] = sb("Srow", [128, T])
    bufs["Pb"] = sb("Pb", [128, T], BF16)
    bufs["PbT"] = sb("PbT", [128, 2048], BF16)
    bufs["am1"] = sb("am1", [128, 1])
    bufs["ars"] = sb("ars", [128, 1])
    bufs["anegm"] = sb("anegm", [128, 1])
    bufs["ltri"] = sb("ltri", [128, 128])
    bufs["iden_bf"] = sb("iden_bf", [128, 128], BF16)
    bufs["ones_bf"] = sb("ones_bf", [128, 128], BF16)
    bufs["stage"] = sb("stage", [128, 2560], BF16)
    bufs["stagef"] = sb("stagef", [128, 512])
    bufs["var"] = sb("var", [128, 512])
    bufs["rstd"] = sb("rstd", [128, 512])
    if rope:
        bufs["CS"] = sb("CS", [128, T], BF16)
        bufs["SN2"] = sb("SN2", [128, T], BF16)
        bufs["swp"] = sb("swp", [128, 128], BF16)
    else:
        bufs["mea"] = sb("mea", [128, 512])
        bufs["tmpf"] = sb("tmpf", [128, 512])
    bufs["PS"] = [ctx.enter_context(nc.psum_tensor(f"ps{i}", [128, 512], F32))
                  for i in range(6)]
    bufs["PT2"] = [ctx.enter_context(nc.psum_tensor(f"pst{i}", [128, 512],
                                                    BF16)) for i in range(2)]
    return bufs


def build_attn_l1(T, q0, q1, upto="all"):
    nc = bass.Bass("TRN2", target_bir_lowering=False)
    Tq = q1 - q0
    aps = {}
    aps["x"] = nc.dram_tensor("x", [128, NCHUNK * T], F32,
                              kind="ExternalInput").ap()
    for wn in ("wqT", "wkT", "wvT", "woT"):
        aps[wn] = nc.dram_tensor(wn, [128, NCHUNK * 2048], F32,
                                 kind="ExternalInput").ap()
    aps["ones_bf"] = nc.dram_tensor("ones_bf", [128, 128], BF16,
                                    kind="ExternalInput").ap()
    aps["iden_bf"] = nc.dram_tensor("iden_bf", [128, 128], BF16,
                                    kind="ExternalInput").ap()
    aps["ltri"] = nc.dram_tensor("ltri", [128, 128], F32,
                                 kind="ExternalInput").ap()
    xo = nc.dram_tensor("xo", [128, NCHUNK * Tq], F32,
                        kind="ExternalOutput").ap()
    qd = nc.dram_tensor("qd", [16, 128, Tq], BF16, kind="Internal").ap()
    kd = nc.dram_tensor("kd", [16, 128, T], BF16, kind="Internal").ap()
    vTd = nc.dram_tensor("vTd", [16, T // 128, 128, 128], BF16,
                         kind="Internal").ap()
    cfg = {"nq": 16, "nkv": 16, "T": T, "q0": q0, "q1": q1}
    with ExitStack() as ctx:
        bufs = alloc_attn_bufs(nc, ctx, T, Tq)
        p = Prog(nc)
        cp = ConstPool(p, nc, ctx)
        bufs["cp"] = cp
        for n, b in (("ones_bf", "ones_bf"), ("iden_bf", "iden_bf"),
                     ("ltri", "ltri")):
            dma_load(p, "sync", bufs[b][:], aps[n], dst_keys=[bkey(bufs[b])],
                     tag="c_ld")
        skey = lambda off, ln: [bkey(bufs["scr_bf"], i) for i in
                                range(off // 512, (off + ln + 511) // 512)]
        xv = aps["x"].rearrange("q (c t) -> q c t", c=NCHUNK)
        for t0, tl in split_chunks(T, 256):
            dma_load_c(p, "sync", bufs["xtmp"][:, :, 0:tl],
                       xv[:, :, t0:t0 + tl], NCHUNK,
                       dst_keys=[bkey(bufs["xtmp"], ("x", c))
                                 for c in range(NCHUNK)],
                       tag="x_ld")
            emit_layernorm_flat(p, nc, tl, bufs["xtmp"],
                                LnbView(bufs["lnb"], t0), bufs["scr_bf"],
                                bufs["ones_bf"], bufs["PS"][0], bufs["PS"][1],
                                bufs["mea"], bufs["var"], bufs["rstd"],
                                bufs["tmpf"], skey, cp=cp)
        if upto in ("proj", "attn", "all", "q"):
            emit_proj_to_dram(p, nc, Tq, SliceLnb(bufs["lnb"], q0),
                              bufs["w_sb"], aps["wqT"], 16, qd, bufs["stage"],
                              bufs["stagef"], bufs["PS"], bufs,
                              scale=1.0 / np.sqrt(128.0), tag="q")
        if upto in ("proj", "attn", "all", "k"):
            emit_proj_to_dram(p, nc, T, bufs["lnb"], bufs["w_sb"], aps["wkT"],
                              16, kd, bufs["stage"], bufs["stagef"],
                              bufs["PS"], bufs, tag="k")
        if upto in ("proj", "attn", "all", "v"):
            emit_v_transposed_to_dram(p, nc, T, bufs["lnb"], bufs["w_sb"],
                                      aps["wvT"], 16, vTd, bufs["stage"],
                                      bufs["PS"], bufs, tag="v")
        if upto in ("attn", "all"):
            emit_attn_phase(p, nc, cfg, bufs, qd, kd, vTd, bufs["o_fb"], 0)
        if upto == "all":
            xin = aps["x"].rearrange("q (c t) -> q c t", c=NCHUNK)[:, :, q0:q1]
            xout = xo.rearrange("q (c t) -> q c t", c=NCHUNK)
            emit_out_proj(p, nc, cfg, bufs, aps["woT"], bufs["o_fb"], 0,
                          xin, xout)
        else:
            for c in range(NCHUNK):
                p.op("vector", lambda eng, c=c: eng.memset(
                    bufs["xtmp"][:, c, 0:256], 0.0),
                    writes=[bkey(bufs["xtmp"], ("x", c))], tag="z")
            for t0, tl in split_chunks(Tq, 256):
                dma_load_c(p, "sync",
                           xo.rearrange("q (c t) -> q c t", c=NCHUNK)[
                               :, :, t0:t0 + tl],
                           bufs["xtmp"][:, :, 0:tl], NCHUNK,
                           dst_keys=[("OUT", t0)],
                           src_keys=[bkey(bufs["xtmp"], ("x", c))
                                     for c in range(NCHUNK)], tag="z_st")
        p.emit_all(ctx)
    return nc


class LnbView:

    def __init__(self, h, t0):
        self.h = h
        self.t0 = t0
        self.name = h.name

    def __getitem__(self, idx):
        q, c, sl = idx
        return self.h[q, c, slice(sl.start + self.t0, sl.stop + self.t0)]


class SliceLnb:

    def __init__(self, h, q0):
        self.h = h
        self.q0 = q0
        self.name = h.name

    def __getitem__(self, idx):
        q, c, sl = idx
        return self.h[q, c, slice(sl.start + self.q0, sl.stop + self.q0)]


def prep_attn_l1_weights(inputs):
    out = {}
    for src, dst in (("L1_attn_b_Wq", "wqT"), ("L1_attn_b_Wk", "wkT"),
                     ("L1_attn_b_Wv", "wvT"), ("L1_attn_b_Wo", "woT")):
        w = np.asarray(inputs[src])
        wT = np.ascontiguousarray(w.T)
        out[dst] = wT.reshape(NCHUNK, 128, 2048).transpose(1, 0, 2).reshape(
            128, NCHUNK * 2048).copy()
    return out


def ltri_mask():
    m = np.zeros((128, 128), np.float32)
    m[np.triu_indices(128, 1)] = -1e30
    return m


def build_attn_l0(T, q0, q1, upto="all"):
    nc = bass.Bass("TRN2", target_bir_lowering=False)
    Tq = q1 - q0
    aps = {}
    aps["x"] = nc.dram_tensor("x", [128, NCHUNK * T], F32,
                              kind="ExternalInput").ap()
    aps["wqT"] = nc.dram_tensor("wqT", [128, NCHUNK * 2048], F32,
                                kind="ExternalInput").ap()
    aps["wkT"] = nc.dram_tensor("wkT", [128, NCHUNK * 512], F32,
                                kind="ExternalInput").ap()
    aps["wvT"] = nc.dram_tensor("wvT", [128, NCHUNK * 512], F32,
                                kind="ExternalInput").ap()
    aps["woT"] = nc.dram_tensor("woT", [128, NCHUNK * 2048], F32,
                                kind="ExternalInput").ap()
    aps["anw"] = nc.dram_tensor("anw", [128, NCHUNK], F32,
                                kind="ExternalInput").ap()
    aps["CS"] = nc.dram_tensor("CS", [128, T], BF16,
                               kind="ExternalInput").ap()
    aps["SN2"] = nc.dram_tensor("SN2", [128, T], BF16,
                                kind="ExternalInput").ap()
    aps["swp"] = nc.dram_tensor("swp", [128, 128], BF16,
                                kind="ExternalInput").ap()
    aps["ones_bf"] = nc.dram_tensor("ones_bf", [128, 128], BF16,
                                    kind="ExternalInput").ap()
    aps["iden_bf"] = nc.dram_tensor("iden_bf", [128, 128], BF16,
                                    kind="ExternalInput").ap()
    aps["ltri"] = nc.dram_tensor("ltri", [128, 128], F32,
                                 kind="ExternalInput").ap()
    xo = nc.dram_tensor("xo", [128, NCHUNK * Tq], F32,
                        kind="ExternalOutput").ap()
    qd = nc.dram_tensor("qd", [16, 128, Tq], BF16, kind="Internal").ap()
    kd = nc.dram_tensor("kd", [4, 128, T], BF16, kind="Internal").ap()
    vTd = nc.dram_tensor("vTd", [4, T // 128, 128, 128], BF16,
                         kind="Internal").ap()
    lnfd = nc.dram_tensor("lnfd", [128, NCHUNK * T], F32,
                          kind="Internal").ap()
    cfg = {"nq": 16, "nkv": 4, "T": T, "q0": q0, "q1": q1}
    with ExitStack() as ctx:
        bufs = alloc_attn_bufs(nc, ctx, T, Tq, rope=True)
        sb = lambda name, shape, dt=F32: ctx.enter_context(
            nc.sbuf_tensor("sl0_" + name, shape, dt))
        bufs["anw"] = sb("anw", [128, NCHUNK])
        bufs["lnf"] = sb("lnf", [128, NCHUNK, 128])
        bufs["mea"] = sb("mea", [128, 256])
        bufs["tmpf"] = sb("tmpf", [128, 256])
        p = Prog(nc)
        cp = ConstPool(p, nc, ctx)
        bufs["cp"] = cp
        for n in ("ones_bf", "iden_bf", "ltri", "CS", "SN2", "swp", "anw"):
            dma_load(p, "sync", bufs[n][:], aps[n], dst_keys=[bkey(bufs[n])],
                     tag="c_ld")
        skey = lambda off, ln: [bkey(bufs["scr_bf"], i) for i in
                                range(off // 512, (off + ln + 511) // 512)]
        xv = aps["x"].rearrange("q (c t) -> q c t", c=NCHUNK)
        lnfv = lnfd.rearrange("q (c t) -> q c t", c=NCHUNK)
        for t0, tl in split_chunks(T, 128):
            dma_load_c(p, "sync", bufs["xtmp"][:, :, 0:tl],
                       xv[:, :, t0:t0 + tl], NCHUNK,
                       dst_keys=[bkey(bufs["xtmp"], ("x", c))
                                 for c in range(NCHUNK)],
                       tag="x_ld")
            emit_layernorm_flat(p, nc, tl, bufs["xtmp"],
                                LnbView(bufs["lnb"], t0), bufs["scr_bf"],
                                bufs["ones_bf"], bufs["PS"][0], bufs["PS"][1],
                                bufs["mea"], bufs["var"], bufs["rstd"],
                                bufs["tmpf"], skey, cp=cp,
                                lnf=bufs["lnf"])
            dma_load_c(p, "sync", lnfv[:, :, t0:t0 + tl],
                       bufs["lnf"][:, :, 0:tl], NCHUNK,
                       dst_keys=[("dram", "lnfd", t0)],
                       src_keys=[bkey(bufs["lnf"], ("x", c))
                                 for c in range(NCHUNK)], tag="lnf_st")
            emit_rmsnorm_flat(p, nc, tl, bufs["lnf"],
                              LnbView(bufs["lnb"], t0), bufs["scr_bf"],
                              bufs["ones_bf"], bufs["PS"][1], bufs["var"],
                              bufs["rstd"], skey, cp, bufs["anw"])
        if upto in ("proj", "attn", "all"):
            emit_proj_to_dram(p, nc, Tq, SliceLnb(bufs["lnb"], q0),
                              bufs["w_sb"], aps["wqT"], 16, qd, bufs["stage"],
                              bufs["stagef"], bufs["PS"], bufs,
                              scale=1.0 / np.sqrt(128.0), rope=True,
                              rope_t0=q0, tag="q")
            emit_proj_to_dram(p, nc, T, bufs["lnb"], bufs["w_sb"], aps["wkT"],
                              4, kd, bufs["stage"], bufs["stagef"],
                              bufs["PS"], bufs, rope=True, rope_t0=0, tag="k")
            emit_v_transposed_to_dram(p, nc, T, bufs["lnb"], bufs["w_sb"],
                                      aps["wvT"], 4, vTd, bufs["stage"],
                                      bufs["PS"], bufs, tag="v")
        if upto in ("attn", "all"):
            emit_attn_phase(p, nc, cfg, bufs, qd, kd, vTd, bufs["o_fb"], 0)
        if upto == "all":
            xin = lnfv[:, :, q0:q1]
            xout = xo.rearrange("q (c t) -> q c t", c=NCHUNK)
            emit_out_proj(p, nc, cfg, bufs, aps["woT"], bufs["o_fb"], 0,
                          xin, xout, st_sz=128)
        p.emit_all(ctx)
    return nc


F_AB = 1024
NF_AB = 8


def build_moe_attnb(T):
    nc = bass.Bass("TRN2", target_bir_lowering=False)
    aps = {}
    aps["x"] = nc.dram_tensor("x", [128, NCHUNK * T], F32,
                              kind="ExternalInput").ap()
    aps["gpT_h"] = nc.dram_tensor("gpT_h", [8, 128, NCHUNK * F_AB], F32,
                                  kind="ExternalInput").ap()
    aps["upT_h"] = nc.dram_tensor("upT_h", [8, 128, NCHUNK * F_AB], F32,
                                  kind="ExternalInput").ap()
    aps["dnT_h"] = nc.dram_tensor("dnT_h", [8, 2, 128, NF_AB * 1024], F32,
                                  kind="ExternalInput").ap()
    aps["gwT_h"] = nc.dram_tensor("gwT_h", [128, NCHUNK * 8], F32,
                                  kind="ExternalInput").ap()
    aps["mnw"] = nc.dram_tensor("mnw", [128, NCHUNK], F32,
                                kind="ExternalInput").ap()
    aps["xres"] = nc.dram_tensor("xres", [128, NCHUNK * T], F32,
                                 kind="ExternalInput").ap()
    aps["ones_bf"] = nc.dram_tensor("ones_bf", [128, 128], BF16,
                                    kind="ExternalInput").ap()
    aps["iden_f32"] = nc.dram_tensor("iden_f32", [128, 128], F32,
                                     kind="ExternalInput").ap()
    aps["ind8"] = nc.dram_tensor("ind8", [8, 8 * 128], F32,
                                 kind="ExternalInput").ap()
    xo = nc.dram_tensor("xo", [128, NCHUNK * T], F32,
                        kind="ExternalOutput").ap()
    with ExitStack() as ctx:
        bufs = alloc_common(nc, ctx, T, want=("router",))
        sb = lambda name, shape, dt=F32: ctx.enter_context(
            nc.sbuf_tensor("sm_" + name, shape, dt))
        bufs["scr_bf"] = sb("scr_bf", [128, max(NF_AB * T, NCHUNK * 512)],
                            BF16)
        bufs["w_gu"] = sb("w_gu", [128, NCHUNK, F_AB], BF16)
        bufs["w_dn"] = sb("w_dn", [128, NF_AB, 1024], BF16)
        bufs["mnw"] = sb("mnw", [128, NCHUNK])
        p = Prog(nc)
        bufs["cp"] = ConstPool(p, nc, ctx)
        load_common(p, nc, T, bufs, aps, want=("router",))
        dma_load(p, "sync", bufs["mnw"][:], aps["mnw"],
                 dst_keys=[bkey(bufs["mnw"])], tag="c_ld")
        w = {"gpT": [aps["gpT_h"][e] for e in range(8)],
             "upT": [aps["upT_h"][e] for e in range(8)],
             "dnT_h": aps["dnT_h"]}
        bufs["xres_t"] = sb("xres_t", [128, NCHUNK, 256])
        emit_moe(p, nc, T, bufs, w, NF=NF_AB, norm="rms", nw_sb=bufs["mnw"],
                 xres_ap=aps["xres"].rearrange("q (c t) -> q c t", c=NCHUNK),
                 xres_t=bufs["xres_t"])
        store_x(p, nc, T, bufs, xo, 0, T)
        p.emit_all(ctx)
    return nc


def deinterleave_cols(w2d, n_heads, dh=128):
    D_in, out = w2d.shape
    w = w2d.reshape(D_in, n_heads, dh)
    perm = np.concatenate([np.arange(0, dh, 2), np.arange(1, dh, 2)])
    return np.ascontiguousarray(w[:, :, perm].reshape(D_in, out))


def chunk128(wT):
    D_in, out = wT.shape
    return np.ascontiguousarray(
        wT.reshape(NCHUNK, 128, out).transpose(1, 0, 2).reshape(
            128, NCHUNK * out))


def rope_tables(T):
    half = 64
    inv = 1.0 / (10000.0 ** (np.arange(half, dtype=np.float64) / half))
    ang = np.arange(T, dtype=np.float64)[:, None] * inv[None, :]
    cos = np.cos(ang).T
    sin = np.sin(ang).T
    CS = np.concatenate([cos, cos], axis=0).astype(bf)
    SN2 = np.concatenate([-sin, sin], axis=0).astype(bf)
    swp = np.zeros((128, 128), np.float32)
    for pcol in range(128):
        swp[(pcol + 64) % 128, pcol] = 1.0
    return CS, SN2, swp.astype(bf)


def prep_attn_l0_weights(inputs):
    out = {}
    wq = np.asarray(inputs["L0_attn_b_w_q"])
    wk = np.asarray(inputs["L0_attn_b_w_k"])
    wv = np.asarray(inputs["L0_attn_b_w_v"])
    wo = np.asarray(inputs["L0_attn_b_w_o"])
    out["wqT"] = chunk128(deinterleave_cols(wq, 16))
    out["wkT"] = chunk128(deinterleave_cols(wk, 4))
    out["wvT"] = chunk128(wv)
    out["woT"] = chunk128(wo)
    anw = np.asarray(inputs["L0_attn_b_attn_norm_w"]).reshape(NCHUNK, 128).T
    out["anw"] = np.ascontiguousarray(anw)
    return out


def prep_moe_attnb_weights(inputs):
    w1 = np.asarray(inputs["L0_attn_b_w1"])
    w3 = np.asarray(inputs["L0_attn_b_w3"])
    w2 = np.asarray(inputs["L0_attn_b_w2"])
    gw = np.asarray(inputs["L0_attn_b_gate_w"])
    gpT_h = w1.reshape(8, NCHUNK, 128, F_AB).transpose(0, 2, 1, 3).reshape(
        8, 128, NCHUNK * F_AB).copy()
    upT_h = w3.reshape(8, NCHUNK, 128, F_AB).transpose(0, 2, 1, 3).reshape(
        8, 128, NCHUNK * F_AB).copy()
    w2_4d = w2.reshape(8, NF_AB, 128, 2048)
    dnT_h = np.stack(
        [w2_4d[:, :, :, 0:1024], w2_4d[:, :, :, 1024:2048]], axis=1
    ).transpose(0, 1, 3, 2, 4).reshape(8, 2, 128, NF_AB * 1024).copy()
    gwT_h = gw.reshape(NCHUNK, 128, 8).transpose(1, 0, 2).reshape(
        128, NCHUNK * 8).copy()
    mnw = np.asarray(inputs["L0_attn_b_moe_norm_w"]).reshape(NCHUNK, 128).T
    return {"gpT_h": gpT_h, "upT_h": upT_h, "dnT_h": dnT_h, "gwT_h": gwT_h,
            "mnw": np.ascontiguousarray(mnw)}


def build_rglru(T, q0, q1):
    nc = bass.Bass("TRN2", target_bir_lowering=False)
    Tq = q1 - q0
    aps = {}
    aps["x"] = nc.dram_tensor("x", [128, NCHUNK * T], F32,
                              kind="ExternalInput").ap()
    aps["winT"] = nc.dram_tensor("winT", [128, NCHUNK * 2048], F32,
                                 kind="ExternalInput").ap()
    aps["woutT"] = nc.dram_tensor("woutT", [128, NCHUNK * 2048], F32,
                                  kind="ExternalInput").ap()
    aps["wg"] = nc.dram_tensor("wg", [128, 2 * 2048], F32,
                               kind="ExternalInput").ap()
    aps["pv"] = nc.dram_tensor("pv", [128, 64], F32,
                               kind="ExternalInput").ap()
    aps["ones_bf"] = nc.dram_tensor("ones_bf", [128, 128], BF16,
                                    kind="ExternalInput").ap()
    xo = nc.dram_tensor("xo", [128, NCHUNK * Tq], F32,
                        kind="ExternalOutput").ap()
    S = 512
    nstrip = T // S
    with ExitStack() as ctx:
        sb = lambda name, shape, dt=F32: ctx.enter_context(
            nc.sbuf_tensor("sg_" + name, shape, dt))
        xtmp = sb("xtmp", [128, NCHUNK, S])
        lnbS = sb("lnbS", [128, NCHUNK, S], BF16)
        scrq = sb("scrq", [128, NCHUNK * 512], BF16)
        xh = sb("xh", [128, NCHUNK, S], BF16)
        hS = sb("hS", [128, NCHUNK, S], BF16)
        w_sb = sb("w_sb", [128, NCHUNK, 1024], BF16)
        wg_sb = sb("wg_sb", [128, 2 * 2048], BF16)
        pv_sb = sb("pv", [128, 64])
        m8sp = sb("m8sp", [128, 8])
        carry = sb("carry", [128, 24])
        zer = sb("zer", [128, S])
        SC = sb("SC", [128, 12, S])
        ki = sb("ki", [128, S], I32)
        ones_bf = sb("ones_bf", [128, 128], BF16)
        mea = sb("mea", [128, 256]); var = sb("var", [128, 256])
        rstd = sb("rstd", [128, 256]); tmpf = sb("tmpf", [128, 256])
        PS = [ctx.enter_context(nc.psum_tensor(f"ps{i}", [128, 512], F32))
              for i in range(8)]
        p = Prog(nc)
        cp = ConstPool(p, nc, ctx)
        dma_load(p, "sync", ones_bf[:], aps["ones_bf"],
                 dst_keys=[bkey(ones_bf)], tag="c_ld")
        dma_load(p, "sync", pv_sb[:], aps["pv"], dst_keys=[bkey(pv_sb)],
                 tag="c_ld")
        dma_load(p, "gpsimd", wg_sb[:], aps["wg"], dst_keys=[bkey(wg_sb)],
                 tag="c_ld")
        p.op("vector", lambda eng: eng.memset(zer[:], 0.0),
             writes=[bkey(zer)])
        p.op("vector", lambda eng: eng.memset(carry[:], 0.0),
             writes=[bkey(carry, "ph"), bkey(carry, "gr"),
                     bkey(carry, "gi")])
        arp = pv_sb[:, 48:56]
        one_ap, one_k = cp.get(1.0)
        p.op("scalar", lambda eng: eng.activation(SC[:, 0, 0:8], arp,
                                                  AF.Abs),
             reads=[bkey(pv_sb)], writes=[bkey(SC, 0)], tag="sp")
        p.op("scalar", lambda eng: eng.activation(SC[:, 0, 8:16],
                                                  SC[:, 0, 0:8], AF.Exp,
                                                  scale=-1.0),
             reads=[bkey(SC, 0)], writes=[bkey(SC, 0)], tag="sp")
        p.op("scalar", lambda eng, one_ap=one_ap: eng.activation(
            SC[:, 0, 8:16], SC[:, 0, 8:16], AF.Ln, bias=one_ap),
            reads=[bkey(SC, 0), one_k], writes=[bkey(SC, 0)], tag="sp")
        p.op("vector", lambda eng: eng.tensor_relu(SC[:, 0, 16:24], arp),
             reads=[bkey(pv_sb)], writes=[bkey(SC, 0)], tag="sp")
        p.op("vector", lambda eng: eng.tensor_add(m8sp[:, 0:8],
                                                  SC[:, 0, 8:16],
                                                  SC[:, 0, 16:24]),
             reads=[bkey(SC, 0)], writes=[bkey(m8sp)], tag="sp")
        p.op("vector", lambda eng: eng.tensor_scalar(m8sp[:, 0:8],
                                                     m8sp[:, 0:8], -8.0,
                                                     None, ALU.mult),
             reads=[bkey(m8sp)], writes=[bkey(m8sp)], tag="sp")

        xv = aps["x"].rearrange("q (c t) -> q c t", c=NCHUNK)
        xov = xo.rearrange("q (c t) -> q c t", c=NCHUNK)
        wiv = aps["winT"].rearrange("q (c f) -> q c f", c=NCHUNK)
        wov = aps["woutT"].rearrange("q (c f) -> q c f", c=NCHUNK)
        skey = lambda off, ln: [bkey(scrq, i) for i in
                                range(off // 512, (off + ln + 511) // 512)]
        inv2pi = 1.0 / (2 * np.pi)
        for st in range(nstrip):
            t0 = st * S
            own = (t0 >= q0) and (t0 < q1)
            for u0, ul in split_chunks(S, 256):
                dma_load_c(p, "sync", xtmp[:, :, u0:u0 + ul],
                           xv[:, :, t0 + u0:t0 + u0 + ul], NCHUNK,
                           dst_keys=[bkey(xtmp, ("x", c))
                                     for c in range(NCHUNK)], tag="x_ld")
            for u0, ul in split_chunks(S, 256):
                emit_layernorm_flat(p, nc, ul, XOffView(xtmp, u0),
                                    LnbView(lnbS, u0), scrq, ones_bf,
                                    PS[0], PS[1], mea, var, rstd, tmpf,
                                    skey, cp=cp)
            for wh in range(2):
                dma_load_c(p, "gpsimd", w_sb[:, :, 0:1024],
                           wiv[:, :, wh * 1024:(wh + 1) * 1024], NCHUNK,
                           dst_keys=[bkey(w_sb)], tag="wi_ld")
                for j in range(8):
                    oc = wh * 8 + j
                    pg = PS[4 + j % 2]
                    for c in range(NCHUNK):
                        p.op("tensor",
                             lambda eng, c=c, j=j, pg=pg: eng.matmul(
                                 pg[:, 0:S], w_sb[:, c, j * 128:(j + 1) * 128],
                                 lnbS[:, c, 0:S],
                                 start=(c == 0), stop=(c == NCHUNK - 1)),
                             reads=[bkey(w_sb), bkey(lnbS, c)],
                             writes=[bkey(pg)], tag="wi_mm")
                    p.op("vector",
                         lambda eng, oc=oc, pg=pg: eng.tensor_scalar(
                             xh[:, oc, 0:S], pg[:, 0:S],
                             pv_sb[:, oc:oc + 1], None, ALU.add),
                         reads=[bkey(pg), bkey(pv_sb)],
                         writes=[bkey(xh, oc)], tag="wi_ev")
            for j in range(8):
                GA, GX, PH, MG, CC, SS2, T1, T2, UR, UI, GR, GI = range(12)
                sc_ = lambda i: SC[:, i, 0:S]
                kS = lambda i: bkey(SC, i)
                for gi_, (base, buf) in enumerate(
                        (((0, GA)), ((1, GX)))):
                    pg = PS[4 + gi_]
                    for kc in range(2):
                        p.op("tensor",
                             lambda eng, j=j, kc=kc, base=base, pg=pg:
                             eng.matmul(
                                 pg[:, 0:S],
                                 wg_sb[:, (base * 8 + j) * 256 + kc * 128:
                                       (base * 8 + j) * 256 + (kc + 1) * 128],
                                 xh[:, 2 * j + kc, 0:S],
                                 start=(kc == 0), stop=(kc == 1)),
                             reads=[bkey(wg_sb), bkey(xh, 2 * j + kc)],
                             writes=[bkey(pg)], tag="g_mm")
                    bcol = 40 + j if base == 0 else 32 + j
                    p.op("scalar",
                         lambda eng, pg=pg, buf=buf, bcol=bcol:
                         eng.activation(SC[:, buf, 0:S], pg[:, 0:S],
                                        AF.Sigmoid,
                                        bias=pv_sb[:, bcol:bcol + 1]),
                         reads=[bkey(pg), bkey(pv_sb)], writes=[kS(buf)],
                         tag="g_sig")
                p.op("vector", lambda eng, j=j: eng.tensor_scalar(
                    sc_(PH), sc_(GA), pv_sb[:, 56 + j:57 + j], None,
                    ALU.mult),
                    reads=[kS(GA), bkey(pv_sb)], writes=[kS(PH)], tag="th")
                p.op("vector", lambda eng, j=j: eng.tensor_scalar(
                    sc_(GA), sc_(GA), m8sp[:, j:j + 1], None, ALU.mult),
                    reads=[kS(GA), bkey(m8sp)], writes=[kS(GA)], tag="lm")
                p.op("scalar", lambda eng: eng.activation(sc_(MG), sc_(GA),
                                                          AF.Exp),
                     reads=[kS(GA)], writes=[kS(MG)], tag="mag")
                p.op("scalar", lambda eng: eng.activation(sc_(T1), sc_(GA),
                                                          AF.Exp, scale=2.0),
                     reads=[kS(GA)], writes=[kS(T1)], tag="asq")
                p.op("vector", lambda eng: eng.tensor_scalar(
                    sc_(T1), sc_(T1), -1.0, 1.0, ALU.mult, ALU.add),
                    reads=[kS(T1)], writes=[kS(T1)], tag="nr1")
                p.op("scalar", lambda eng: eng.activation(sc_(T1), sc_(T1),
                                                          AF.Sqrt),
                     reads=[kS(T1)], writes=[kS(T1)], tag="nr2")
                p.op("vector", lambda eng: eng.tensor_mul(sc_(GX), sc_(GX),
                                                          sc_(T1)),
                     reads=[kS(GX), kS(T1)], writes=[kS(GX)], tag="sc")
                p.op("vector", lambda eng, j=j: eng.tensor_tensor_scan(
                    sc_(PH), sc_(PH), zer[:, 0:S], carry[:, j:j + 1],
                    ALU.add, ALU.add),
                    reads=[kS(PH), bkey(zer), bkey(carry, "ph")],
                    writes=[kS(PH)], tag="phs")
                p.op("vector", lambda eng, j=j: eng.tensor_copy(
                    carry[:, j:j + 1], SC[:, PH, S - 1:S]),
                    reads=[kS(PH)], writes=[bkey(carry, "ph")], tag="phc")
                p.op("vector", lambda eng: eng.tensor_scalar(
                    sc_(T1), sc_(PH), inv2pi, None, ALU.mult),
                    reads=[kS(PH)], writes=[kS(T1)], tag="rr1")
                p.op("vector", lambda eng: eng.tensor_copy(ki[:, 0:S],
                                                           sc_(T1)),
                     reads=[kS(T1)], writes=[bkey(ki)], tag="rr2")
                p.op("vector", lambda eng: eng.tensor_copy(sc_(T1),
                                                           ki[:, 0:S]),
                     reads=[bkey(ki)], writes=[kS(T1)], tag="rr3")
                p.op("vector", lambda eng: eng.scalar_tensor_tensor(
                    sc_(T1), sc_(T1), -2 * np.pi, sc_(PH), ALU.mult,
                    ALU.add),
                    reads=[kS(T1), kS(PH)], writes=[kS(T1)], tag="rr4")
                p.op("scalar", lambda eng: eng.activation(sc_(SS2), sc_(T1),
                                                          AF.Sin),
                     reads=[kS(T1)], writes=[kS(SS2)], tag="sin")
                p.op("vector", lambda eng: eng.tensor_scalar(
                    sc_(T1), sc_(PH), inv2pi, 0.25, ALU.mult, ALU.add),
                    reads=[kS(PH)], writes=[kS(T1)], tag="rc1")
                p.op("vector", lambda eng: eng.tensor_copy(ki[:, 0:S],
                                                           sc_(T1)),
                     reads=[kS(T1)], writes=[bkey(ki)], tag="rc2")
                p.op("vector", lambda eng: eng.tensor_copy(sc_(T1),
                                                           ki[:, 0:S]),
                     reads=[bkey(ki)], writes=[kS(T1)], tag="rc3")
                p.op("vector", lambda eng: eng.tensor_scalar(
                    sc_(T2), sc_(PH), np.pi / 2, None, ALU.add),
                    reads=[kS(PH)], writes=[kS(T2)], tag="rc4")
                p.op("vector", lambda eng: eng.scalar_tensor_tensor(
                    sc_(T1), sc_(T1), -2 * np.pi, sc_(T2), ALU.mult,
                    ALU.add),
                    reads=[kS(T1), kS(T2)], writes=[kS(T1)], tag="rc5")
                p.op("scalar", lambda eng: eng.activation(sc_(CC), sc_(T1),
                                                          AF.Sin),
                     reads=[kS(T1)], writes=[kS(CC)], tag="cos")
                p.op("vector", lambda eng, j=j: eng.tensor_mul(
                    sc_(T1), sc_(GX), xh[:, j, 0:S]),
                    reads=[kS(GX), bkey(xh, j)], writes=[kS(T1)], tag="xr")
                p.op("vector", lambda eng, j=j: eng.tensor_mul(
                    sc_(T2), sc_(GX), xh[:, j + 8, 0:S]),
                    reads=[kS(GX), bkey(xh, j + 8)], writes=[kS(T2)],
                    tag="xi")
                p.op("vector", lambda eng: eng.tensor_mul(sc_(UR), sc_(CC),
                                                          sc_(T1)),
                     reads=[kS(CC), kS(T1)], writes=[kS(UR)], tag="u1")
                p.op("vector", lambda eng: eng.tensor_mul(sc_(UI), sc_(SS2),
                                                          sc_(T2)),
                     reads=[kS(SS2), kS(T2)], writes=[kS(UI)], tag="u2")
                p.op("vector", lambda eng: eng.tensor_add(sc_(UR), sc_(UR),
                                                          sc_(UI)),
                     reads=[kS(UR), kS(UI)], writes=[kS(UR)], tag="u3")
                p.op("vector", lambda eng: eng.tensor_mul(sc_(UI), sc_(CC),
                                                          sc_(T2)),
                     reads=[kS(CC), kS(T2)], writes=[kS(UI)], tag="u4")
                p.op("vector", lambda eng: eng.tensor_mul(sc_(T2), sc_(SS2),
                                                          sc_(T1)),
                     reads=[kS(SS2), kS(T1)], writes=[kS(T2)], tag="u5")
                p.op("vector", lambda eng: eng.tensor_sub(sc_(UI), sc_(UI),
                                                          sc_(T2)),
                     reads=[kS(UI), kS(T2)], writes=[kS(UI)], tag="u6")
                p.op("vector", lambda eng, j=j: eng.tensor_tensor_scan(
                    sc_(GR), sc_(MG), sc_(UR), carry[:, 8 + j:9 + j],
                    ALU.mult, ALU.add),
                    reads=[kS(MG), kS(UR), bkey(carry, "gr")],
                    writes=[kS(GR)], tag="grs")
                p.op("vector", lambda eng, j=j: eng.tensor_tensor_scan(
                    sc_(GI), sc_(MG), sc_(UI), carry[:, 16 + j:17 + j],
                    ALU.mult, ALU.add),
                    reads=[kS(MG), kS(UI), bkey(carry, "gi")],
                    writes=[kS(GI)], tag="gis")
                p.op("vector", lambda eng, j=j: eng.tensor_copy(
                    carry[:, 8 + j:9 + j], SC[:, GR, S - 1:S]),
                    reads=[kS(GR)], writes=[bkey(carry, "gr")], tag="grc")
                p.op("vector", lambda eng, j=j: eng.tensor_copy(
                    carry[:, 16 + j:17 + j], SC[:, GI, S - 1:S]),
                    reads=[kS(GI)], writes=[bkey(carry, "gi")], tag="gic")
                p.op("vector", lambda eng: eng.tensor_mul(sc_(T1), sc_(CC),
                                                          sc_(GR)),
                     reads=[kS(CC), kS(GR)], writes=[kS(T1)], tag="h1")
                p.op("vector", lambda eng: eng.tensor_mul(sc_(T2), sc_(SS2),
                                                          sc_(GI)),
                     reads=[kS(SS2), kS(GI)], writes=[kS(T2)], tag="h2")
                p.op("vector", lambda eng, j=j: eng.tensor_sub(
                    hS[:, j, 0:S], sc_(T1), sc_(T2)),
                    reads=[kS(T1), kS(T2)], writes=[bkey(hS, j)], tag="h3")
                p.op("vector", lambda eng: eng.tensor_mul(sc_(T1), sc_(SS2),
                                                          sc_(GR)),
                     reads=[kS(SS2), kS(GR)], writes=[kS(T1)], tag="h4")
                p.op("vector", lambda eng: eng.tensor_mul(sc_(T2), sc_(CC),
                                                          sc_(GI)),
                     reads=[kS(CC), kS(GI)], writes=[kS(T2)], tag="h5")
                p.op("vector", lambda eng, j=j: eng.tensor_add(
                    hS[:, j + 8, 0:S], sc_(T1), sc_(T2)),
                    reads=[kS(T1), kS(T2)], writes=[bkey(hS, j + 8)],
                    tag="h6")
            if own:
                for oc in range(NCHUNK):
                    p.op("vector",
                         lambda eng, oc=oc: eng.tensor_scalar(
                             xtmp[:, oc, 0:S], xtmp[:, oc, 0:S],
                             pv_sb[:, 16 + oc:17 + oc], None, ALU.add),
                         reads=[bkey(xtmp, ("x", oc)), bkey(pv_sb)],
                         writes=[bkey(xtmp, ("x", oc))], tag="bo")
                for wh in range(2):
                    dma_load_c(p, "gpsimd", w_sb[:, :, 0:1024],
                               wov[:, :, wh * 1024:(wh + 1) * 1024], NCHUNK,
                               dst_keys=[bkey(w_sb)], tag="wo_ld")
                    for jj in range(8):
                        oc = wh * 8 + jj
                        pg = PS[4 + jj % 2]
                        for c in range(NCHUNK):
                            p.op("tensor",
                                 lambda eng, c=c, jj=jj, pg=pg: eng.matmul(
                                     pg[:, 0:S],
                                     w_sb[:, c, jj * 128:(jj + 1) * 128],
                                     hS[:, c, 0:S],
                                     start=(c == 0), stop=(c == NCHUNK - 1)),
                                 reads=[bkey(w_sb), bkey(hS, c)],
                                 writes=[bkey(pg)], tag="wo_mm")
                        p.op("vector",
                             lambda eng, oc=oc, pg=pg:
                             eng.scalar_tensor_tensor(
                                 xtmp[:, oc, 0:S], pg[:, 0:S], 1.0,
                                 xtmp[:, oc, 0:S], ALU.mult, ALU.add),
                             reads=[bkey(pg), bkey(xtmp, ("x", oc))],
                             writes=[bkey(xtmp, ("x", oc))], tag="wo_acc")
                dma_load_c(p, "sync", xov[:, :, t0 - q0:t0 - q0 + S],
                           xtmp[:, :, 0:S], NCHUNK,
                           dst_keys=[("OUT", t0)],
                           src_keys=[bkey(xtmp, ("x", c))
                                     for c in range(NCHUNK)], tag="x_st")
        p.emit_all(ctx)
    return nc


class XOffView:

    def __init__(self, h, u0):
        self.h = h
        self.u0 = u0
        self.name = h.name

    def __getitem__(self, idx):
        q, c, sl = idx
        return self.h[q, c, slice(sl.start + self.u0, sl.stop + self.u0)]


def prep_rglru_weights(inputs):
    W_in = np.asarray(inputs["L0_attn_a_W_in"])
    W_out = np.asarray(inputs["L0_attn_a_W_out"])
    Wg_x = np.asarray(inputs["L0_attn_a_Wg_x"])
    Wg_a = np.asarray(inputs["L0_attn_a_Wg_a"])
    out = {"winT": chunk128(W_in), "woutT": chunk128(W_out)}
    wgx = Wg_x.reshape(8, 2, 128, 128).transpose(2, 0, 1, 3).reshape(128, 2048)
    wga = Wg_a.reshape(8, 2, 128, 128).transpose(2, 0, 1, 3).reshape(128, 2048)
    out["wg"] = np.ascontiguousarray(np.concatenate([wga, wgx], axis=1))
    pv = np.zeros((128, 64), np.float32)
    pv[:, 0:16] = np.asarray(inputs["L0_attn_a_b_in"]).reshape(16, 128).T
    pv[:, 16:32] = np.asarray(inputs["L0_attn_a_b_out"]).reshape(16, 128).T
    pv[:, 32:40] = np.asarray(inputs["L0_attn_a_bg_x"]).reshape(8, 128).T
    pv[:, 40:48] = np.asarray(inputs["L0_attn_a_bg_a"]).reshape(8, 128).T
    pv[:, 48:56] = np.asarray(
        inputs["L0_attn_a_a_real_param"]).reshape(8, 128).T
    pv[:, 56:64] = np.asarray(
        inputs["L0_attn_a_a_imag_param"]).reshape(8, 128).T
    out["pv"] = pv
    return out


def build_mamba(T, q0, q1):
    nc = bass.Bass("TRN2", target_bir_lowering=False)
    Tq = q1 - q0
    CH = 256
    ncc = T // CH
    aps = {}
    aps["x"] = nc.dram_tensor("x", [128, NCHUNK * T], F32,
                              kind="ExternalInput").ap()
    aps["inT"] = nc.dram_tensor("inT", [128, NCHUNK * 8352], F32,
                                kind="ExternalInput").ap()
    aps["oT"] = nc.dram_tensor("oT", [128, 32 * 2048], F32,
                               kind="ExternalInput").ap()
    aps["cw"] = nc.dram_tensor("cw", [128, 33 * 4], F32,
                               kind="ExternalInput").ap()
    aps["cb"] = nc.dram_tensor("cb", [128, 33], F32,
                               kind="ExternalInput").ap()
    aps["hp"] = nc.dram_tensor("hp", [32, 4], F32,
                               kind="ExternalInput").ap()
    aps["nw"] = nc.dram_tensor("nw", [128, 32], F32,
                               kind="ExternalInput").ap()
    aps["ds_row"] = nc.dram_tensor("ds_row", [1, 32], F32,
                                   kind="ExternalInput").ap()
    aps["ones_bf"] = nc.dram_tensor("ones_bf", [128, 128], BF16,
                                    kind="ExternalInput").ap()
    aps["iden_bf"] = nc.dram_tensor("iden_bf", [128, 128], BF16,
                                    kind="ExternalInput").ap()
    aps["iden_f32"] = nc.dram_tensor("iden_f32", [128, 128], F32,
                                     kind="ExternalInput").ap()
    aps["mks"] = nc.dram_tensor("mks", [128, 2 * CH], F32,
                                kind="ExternalInput").ap()
    xo = nc.dram_tensor("xo", [128, NCHUNK * Tq], F32,
                        kind="ExternalOutput").ap()
    zxd = nc.dram_tensor("zxd", [66, 128, T], BF16, kind="Internal").ap()
    convd = nc.dram_tensor("convd", [33, 128, T], BF16, kind="Internal").ap()
    aD = nc.dram_tensor("aD", [32, T], F32, kind="Internal").ap()
    S = 512
    with ExitStack() as ctx:
        sb = lambda name, shape, dt=F32: ctx.enter_context(
            nc.sbuf_tensor("sm2_" + name, shape, dt))
        xtmp = sb("xtmp", [128, NCHUNK, CH])
        lnbS = sb("lnbS", [128, NCHUNK, S], BF16)
        scrq = sb("scrq", [128, NCHUNK * 512], BF16)
        w_sb = sb("w_sb", [128, NCHUNK, 1024], BF16)
        stage = sb("stage", [128, S], BF16)
        mea = sb("mea", [128, 256]); var = sb("var", [128, 256])
        rstd = sb("rstd", [128, 256]); tmpf = sb("tmpf", [128, 256])
        ones_bf = sb("ones_bf", [128, 128], BF16)
        iden_bf = sb("iden_bf", [128, 128], BF16)
        iden_f32 = sb("iden_f32", [128, 128])
        cw_sb = sb("cw", [128, 33 * 4]); cb_sb = sb("cb", [128, 33])
        cbuf = sb("cbuf", [128, T + 4], BF16)
        cacc = sb("cacc", [128, T], BF16)
        hp_sb = sb("hp", [32, 8])
        nw_sb = sb("nw", [128, 32])
        dsr = sb("dsr", [1, 32])
        dbc = sb("dbc", [128, 32])
        dt_sb = sb("dt", [32, T])
        a_sb = sb("a", [32, T])
        t32 = sb("t32", [32, T])
        arow = sb("arow", [1, CH])
        onesf = sb("onesf", [1, 128])
        B_sb = sb("B", [64, T], BF16)
        C_sb = sb("C", [64, T], BF16)
        xT = sb("xT", [128, 32, 2, 128], BF16)
        BT = sb("BT", [128, 2, 64], BF16)
        Msb = sb("M", [128, 2, CH], BF16)
        Ldif = sb("Ldif", [128, CH])
        Lh = sb("Lh", [128, CH])
        Mh = sb("Mh", [128, 2, CH], BF16)
        aT = sb("aT", [128, 2, 32]); naT = sb("naT", [128, 2, 32])
        eaT = sb("eaT", [128, 2, 32]); dtT = sb("dtT", [128, 2, 32])
        ddT = sb("ddT", [128, 2, 32])
        prev = sb("prev", [64, 32, 128])
        Btl = sb("Btl", [128, 64], BF16)
        Epair = sb("E", [64, 1])
        yfm = sb("yfm", [128, 32, CH], BF16)
        zc = sb("zc", [128, 8, CH], BF16)
        prevb = sb("prevb", [64, 128], BF16)
        mks = sb("mks", [128, 2, CH])
        PS = [ctx.enter_context(nc.psum_tensor(f"ps{i}", [128, 512], F32))
              for i in range(6)]
        PT2 = [ctx.enter_context(nc.psum_tensor(f"pst{i}", [128, 512], BF16))
               for i in range(2)]
        p = Prog(nc)
        cp = ConstPool(p, nc, ctx)
        for n, b in (("ones_bf", ones_bf), ("iden_bf", iden_bf),
                     ("iden_f32", iden_f32), ("cw", cw_sb), ("cb", cb_sb),
                     ("nw", nw_sb), ("ds_row", dsr), ("mks", mks)):
            dma_load(p, "sync", b[:], aps[n], dst_keys=[bkey(b)], tag="c_ld")
        dma_load(p, "sync", hp_sb[:, 0:4], aps["hp"], dst_keys=[bkey(hp_sb)],
                 tag="c_ld")
        p.op("vector", lambda eng: eng.memset(onesf[:], 1.0),
             writes=[bkey(onesf)], tag="c1")
        xv = aps["x"].rearrange("q (c t) -> q c t", c=NCHUNK)
        xov = xo.rearrange("q (c t) -> q c t", c=NCHUNK)
        wiv = aps["inT"].rearrange("q (c f) -> q c f", c=NCHUNK)
        wov = aps["oT"].rearrange("q (c f) -> q c f", c=32)
        skey = lambda off, ln: [bkey(scrq, i) for i in
                                range(off // 512, (off + ln + 511) // 512)]
        for st in range(T // S):
            t0 = st * S
            for u0, ul in split_chunks(S, 256):
                dma_load_c(p, "sync", xtmp[:, :, 0:ul],
                           xv[:, :, t0 + u0:t0 + u0 + ul], NCHUNK,
                           dst_keys=[bkey(xtmp, ("x", c))
                                     for c in range(NCHUNK)], tag="x_ld")
                emit_layernorm_flat(p, nc, ul, xtmp,
                                    LnbView(lnbS, u0), scrq, ones_bf,
                                    PS[0], PS[1], mea, var, rstd, tmpf,
                                    skey, cp=cp)
            for wh in range(9):
                oc0 = wh * 8
                ocn = min(8, 66 - oc0)
                wend = min((oc0 + ocn) * 128, 8352)
                dma_load_c(p, "gpsimd", w_sb[:, :, 0:wend - oc0 * 128],
                           wiv[:, :, oc0 * 128:wend], NCHUNK,
                           dst_keys=[bkey(w_sb)], tag="wi_ld")
                for j in range(ocn):
                    oc = oc0 + j
                    ow = 128 if oc < 65 else 32
                    pg = PS[4 + j % 2]
                    for c in range(NCHUNK):
                        p.op("tensor",
                             lambda eng, c=c, j=j, pg=pg, ow=ow: eng.matmul(
                                 pg[0:ow, 0:S],
                                 w_sb[:, c, j * 128:j * 128 + ow],
                                 lnbS[:, c, 0:S],
                                 start=(c == 0), stop=(c == NCHUNK - 1)),
                             reads=[bkey(w_sb), bkey(lnbS, c)],
                             writes=[bkey(pg)], tag="wi_mm")
                    p.op("vector",
                         lambda eng, pg=pg, ow=ow: eng.tensor_copy(
                             stage[0:ow, 0:S], pg[0:ow, 0:S]),
                         reads=[bkey(pg)], writes=[bkey(stage)], tag="wi_ev")
                    dma_load(p, "sync", zxd[oc, 0:ow, t0:t0 + S],
                             stage[0:ow, 0:S],
                             dst_keys=[("dram", "zxd", oc)],
                             src_keys=[bkey(stage)], tag="wi_st")
        for ch in range(33):
            p.op("vector", lambda eng: eng.memset(cbuf[:, 0:4], 0.0),
                 writes=[bkey(cbuf)], tag="cv0")
            dma_load(p, "sync", cbuf[:, 4:4 + T], zxd[32 + ch, :, 0:T],
                     dst_keys=[bkey(cbuf)], src_keys=[("dram", "zxd", 32 + ch)],
                     tag="cv_ld")
            p.op("vector",
                 lambda eng, ch=ch: eng.tensor_scalar(
                     cacc[:, 0:T], cbuf[:, 1:1 + T],
                     cw_sb[:, ch * 4:ch * 4 + 1], None, ALU.mult),
                 reads=[bkey(cbuf), bkey(cw_sb)], writes=[bkey(cacc)],
                 tag="cv1")
            for k in range(1, 4):
                p.op("vector",
                     lambda eng, ch=ch, k=k: eng.scalar_tensor_tensor(
                         cacc[:, 0:T], cbuf[:, 1 + k:1 + k + T],
                         cw_sb[:, ch * 4 + k:ch * 4 + k + 1], cacc[:, 0:T],
                         ALU.mult, ALU.add),
                     reads=[bkey(cbuf), bkey(cw_sb), bkey(cacc)],
                     writes=[bkey(cacc)], tag="cv2")
            p.op("scalar",
                 lambda eng, ch=ch: eng.activation(
                     cbuf[:, 4:4 + T], cacc[:, 0:T], AF.Silu,
                     bias=cb_sb[:, ch:ch + 1]),
                 reads=[bkey(cacc), bkey(cb_sb)], writes=[bkey(cbuf)],
                 tag="cv3")
            dma_load(p, "sync", convd[ch, :, 0:T], cbuf[:, 4:4 + T],
                     dst_keys=[("dram", "convd", ch)], src_keys=[bkey(cbuf)],
                     tag="cv_st")
        dma_load(p, "gpsimd", dt_sb[:, 0:T], zxd[65, 0:32, 0:T],
                 dst_keys=[bkey(dt_sb)], src_keys=[("dram", "zxd", 65)],
                 tag="dt_ld")
        p.op("scalar", lambda eng: eng.activation(hp_sb[:, 4:5],
                                                  hp_sb[:, 0:1], AF.Exp),
             reads=[bkey(hp_sb)], writes=[bkey(hp_sb, "A")], tag="hp")
        p.op("vector", lambda eng: eng.tensor_scalar(
            hp_sb[:, 4:5], hp_sb[:, 4:5], -1.0, None, ALU.mult),
            reads=[bkey(hp_sb, "A")], writes=[bkey(hp_sb, "A")], tag="hp")
        p.op("vector", lambda eng: eng.tensor_scalar(
            dt_sb[:, 0:T], dt_sb[:, 0:T], hp_sb[:, 1:2], None, ALU.add),
            reads=[bkey(dt_sb), bkey(hp_sb)], writes=[bkey(dt_sb)], tag="dt")
        p.op("scalar", lambda eng: eng.activation(t32[:, 0:T], dt_sb[:, 0:T],
                                                  AF.Abs),
             reads=[bkey(dt_sb)], writes=[bkey(t32)], tag="dt")
        p.op("scalar", lambda eng: eng.activation(t32[:, 0:T], t32[:, 0:T],
                                                  AF.Exp, scale=-1.0),
             reads=[bkey(t32)], writes=[bkey(t32)], tag="dt")
        one_ap, one_k = cp.get_p(1.0, 32)
        p.op("scalar", lambda eng, one_ap=one_ap: eng.activation(
            t32[:, 0:T], t32[:, 0:T], AF.Ln, bias=one_ap),
            reads=[bkey(t32), one_k], writes=[bkey(t32)], tag="dt")
        p.op("vector", lambda eng: eng.tensor_relu(a_sb[:, 0:T],
                                                   dt_sb[:, 0:T]),
             reads=[bkey(dt_sb)], writes=[bkey(a_sb)], tag="dt")
        p.op("vector", lambda eng: eng.tensor_add(dt_sb[:, 0:T], a_sb[:, 0:T],
                                                  t32[:, 0:T]),
             reads=[bkey(a_sb), bkey(t32)], writes=[bkey(dt_sb)], tag="dt")
        p.op("vector", lambda eng: eng.tensor_scalar(
            t32[:, 0:T], dt_sb[:, 0:T], hp_sb[:, 4:5], None, ALU.mult),
            reads=[bkey(dt_sb), bkey(hp_sb, "A")], writes=[bkey(t32)],
            tag="da")
        p.op("vector", lambda eng: eng.memset(a_sb[:, 0:T], 1.0),
             writes=[bkey(a_sb)], tag="msk")
        for cc in range(ncc):
            p.op("vector", lambda eng, cc=cc: eng.memset(
                a_sb[:, cc * CH:cc * CH + 1], 0.0),
                writes=[bkey(a_sb)], tag="msk")
        p.op("vector", lambda eng: eng.tensor_tensor_scan(
            a_sb[:, 0:T], a_sb[:, 0:T], t32[:, 0:T], 0.0, ALU.mult, ALU.add),
            reads=[bkey(a_sb), bkey(t32)], writes=[bkey(a_sb)], tag="asc")
        dma_load(p, "sync", aD[:, 0:T], a_sb[:, 0:T],
                 dst_keys=[("dram", "aD")], src_keys=[bkey(a_sb)], tag="a_st")
        for cc in range(ncc):
            p.op("vector", lambda eng, cc=cc: eng.tensor_scalar(
                t32[:, cc * CH:(cc + 1) * CH], a_sb[:, cc * CH:(cc + 1) * CH],
                -1.0, None, ALU.mult),
                reads=[bkey(a_sb)], writes=[bkey(t32)], tag="dc1")
            p.op("vector", lambda eng, cc=cc: eng.tensor_scalar(
                t32[:, cc * CH:(cc + 1) * CH], t32[:, cc * CH:(cc + 1) * CH],
                a_sb[:, (cc + 1) * CH - 1:(cc + 1) * CH], None, ALU.add),
                reads=[bkey(t32), bkey(a_sb)], writes=[bkey(t32)], tag="dc2")
        p.op("scalar", lambda eng: eng.activation(t32[:, 0:T], t32[:, 0:T],
                                                  AF.Exp),
             reads=[bkey(t32)], writes=[bkey(t32)], tag="dc3")
        p.op("vector", lambda eng: eng.tensor_mul(t32[:, 0:T], t32[:, 0:T],
                                                  dt_sb[:, 0:T]),
             reads=[bkey(t32), bkey(dt_sb)], writes=[bkey(t32)], tag="dc4")
        p.op("tensor", lambda eng: eng.matmul(
            PS[0][:, 0:32], onesf[0:1, :], dsr[0:1, 0:32],
            start=True, stop=True),
            reads=[bkey(onesf), bkey(dsr)], writes=[bkey(PS[0])], tag="db")
        p.op("vector", lambda eng: eng.tensor_copy(dbc[:, 0:32],
                                                   PS[0][:, 0:32]),
             reads=[bkey(PS[0])], writes=[bkey(dbc)], tag="db2")
        dma_load(p, "sync", B_sb[:, 0:T], convd[32, 0:64, 0:T],
                 dst_keys=[bkey(B_sb)], src_keys=[("dram", "convd", 32)],
                 tag="B_ld")
        dma_load(p, "sync", C_sb[:, 0:T], convd[32, 64:128, 0:T],
                 dst_keys=[bkey(C_sb)], src_keys=[("dram", "convd", 32)],
                 tag="C_ld")
        p.op("vector", lambda eng: eng.memset(prev[:, :, :], 0.0),
             writes=[bkey(prev, h) for h in range(32)], tag="pv0")
        xc_sb = ctx.enter_context(
            nc.sbuf_tensor("sm2_xc", [128, 32, CH], BF16))
        for cc in range(ncc):
            c0 = cc * CH
            own = (c0 >= q0) and (c0 < q1)
            for (src, dst) in ((a_sb, aT), (dt_sb, dtT), (t32, ddT)):
                for qt in range(2):
                    p.op("tensor",
                         lambda eng, src=src, qt=qt, c0=c0: eng.transpose(
                             PS[1][0:128, 0:32],
                             src[0:32, c0 + qt * 128:c0 + (qt + 1) * 128],
                             iden_f32[0:32, 0:32]),
                         reads=[bkey(src), bkey(iden_f32)],
                         writes=[bkey(PS[1])], tag="tr_a")
                    p.op("vector",
                         lambda eng, dst=dst, qt=qt: eng.tensor_copy(
                             dst[:, qt, 0:32], PS[1][0:128, 0:32]),
                         reads=[bkey(PS[1])], writes=[bkey(dst)], tag="trac")
            p.op("vector", lambda eng: eng.tensor_scalar(
                naT[:, :, 0:32], aT[:, :, 0:32], -1.0, None, ALU.mult),
                reads=[bkey(aT)], writes=[bkey(naT)], tag="naT")
            p.op("scalar", lambda eng: eng.activation(
                eaT[:, :, 0:32], aT[:, :, 0:32], AF.Exp),
                reads=[bkey(aT)], writes=[bkey(eaT)], tag="eaT")
            for h in range(32):
                dma_load(p, "sync", xc_sb[:, h, 0:CH],
                         convd[h, :, c0:c0 + CH],
                         dst_keys=[bkey(xc_sb, h)],
                         src_keys=[("dram", "convd", h)], tag="xc_ld")
            for h in range(32):
                for qt in range(2):
                    ptb = PT2[qt % 2]
                    p.op("tensor",
                         lambda eng, h=h, qt=qt, ptb=ptb: eng.transpose(
                             ptb[:, 0:128],
                             xc_sb[:, h, qt * 128:(qt + 1) * 128],
                             iden_bf[:]),
                         reads=[bkey(xc_sb, h), bkey(iden_bf)],
                         writes=[bkey(ptb)], tag="xT_tr")
                    p.op("vector",
                         lambda eng, h=h, qt=qt, ptb=ptb: eng.tensor_copy(
                             xT[:, h, qt, 0:128], ptb[:, 0:128]),
                         reads=[bkey(ptb)], writes=[bkey(xT, h)], tag="xT_c")
            for qt in range(2):
                ptb = PT2[qt % 2]
                p.op("tensor",
                     lambda eng, qt=qt, c0=c0, ptb=ptb: eng.transpose(
                         ptb[:, 0:64],
                         B_sb[0:64, c0 + qt * 128:c0 + (qt + 1) * 128],
                         iden_bf[0:64, 0:64]),
                     reads=[bkey(B_sb), bkey(iden_bf)], writes=[bkey(ptb)],
                     tag="BT_tr")
                p.op("vector",
                     lambda eng, qt=qt, ptb=ptb: eng.tensor_copy(
                         BT[:, qt, 0:64], ptb[:, 0:64]),
                     reads=[bkey(ptb)], writes=[bkey(BT)], tag="BT_c")
            for jt in range(2):
                pm = PS[2 + jt]
                p.op("tensor",
                     lambda eng, jt=jt, c0=c0, pm=pm: eng.matmul(
                         pm[:, 0:CH],
                         B_sb[0:64, c0 + jt * 128:c0 + (jt + 1) * 128],
                         C_sb[0:64, c0:c0 + CH],
                         start=True, stop=True),
                     reads=[bkey(B_sb), bkey(C_sb)], writes=[bkey(pm)],
                     tag="M_mm")
                p.op("vector",
                     lambda eng, jt=jt, pm=pm: eng.tensor_copy(
                         Msb[:, jt, 0:CH], pm[:, 0:CH]),
                     reads=[bkey(pm)], writes=[bkey(Msb)], tag="M_c")
            for h in range(32):
                dma_load(p, "sync", arow[0:1, 0:CH],
                         aD[h:h + 1, c0:c0 + CH],
                         dst_keys=[bkey(arow)], src_keys=[("dram", "aD")],
                         tag="ar_ld")
                p.op("vector",
                     lambda eng, h=h: eng.tensor_copy(
                         prevb[0:64, 0:128], prev[0:64, h, 0:128]),
                     reads=[bkey(prev, h)], writes=[bkey(prevb)],
                     tag="pvb")
                for jt in range(2):
                    p.op("tensor",
                         lambda eng: eng.matmul(
                             PS[1][:, 0:CH], onesf[0:1, :],
                             arow[0:1, 0:CH], start=True, stop=True),
                         reads=[bkey(onesf), bkey(arow)],
                         writes=[bkey(PS[1])], tag="Lb_mm")
                    p.op("vector",
                         lambda eng, jt=jt: eng.tensor_add(
                             Ldif[:, 0:CH], PS[1][:, 0:CH],
                             mks[:, jt, 0:CH]),
                         reads=[bkey(PS[1]), bkey(mks)], writes=[bkey(Ldif)],
                         tag="Lb_d")
                    p.op("scalar",
                         lambda eng, jt=jt, h=h: eng.activation(
                             Lh[:, 0:CH], Ldif[:, 0:CH], AF.Exp,
                             bias=naT[:, jt, h:h + 1]),
                         reads=[bkey(Ldif), bkey(naT)], writes=[bkey(Lh)],
                         tag="Lb_e")
                    p.op("vector",
                         lambda eng, jt=jt, h=h: eng.scalar_tensor_tensor(
                             Mh[:, jt, 0:CH], Lh[:, 0:CH],
                             dtT[:, jt, h:h + 1], Msb[:, jt, 0:CH],
                             ALU.mult, ALU.mult),
                         reads=[bkey(Lh), bkey(dtT), bkey(Msb)],
                         writes=[bkey(Mh)], tag="Mh")
                for it in range(2):
                    pz = PS[4 + it]
                    for jt in range(it + 1):
                        p.op("tensor",
                             lambda eng, jt=jt, it=it, h=h, pz=pz:
                             eng.matmul(
                                 pz[:, 0:128],
                                 Mh[:, jt, it * 128:(it + 1) * 128],
                                 xT[:, h, jt, 0:128],
                                 start=(jt == 0), stop=(jt == it)),
                             reads=[bkey(Mh), bkey(xT, h)],
                             writes=[bkey(pz)], tag="zi_mm")
                    p.op("tensor",
                         lambda eng, it=it, h=h, c0=c0: eng.matmul(
                             PS[2][:, 0:128],
                             C_sb[0:64, c0 + it * 128:c0 + (it + 1) * 128],
                             prevb[0:64, 0:128],
                             start=True, stop=True),
                         reads=[bkey(C_sb), bkey(prevb)],
                         writes=[bkey(PS[2])], tag="ze_mm")
                    p.op("vector",
                         lambda eng, it=it, h=h: eng.tensor_scalar(
                             Ldif[:, 0:128], PS[2][:, 0:128],
                             eaT[:, it, h:h + 1], None, ALU.mult),
                         reads=[bkey(PS[2]), bkey(eaT)], writes=[bkey(Ldif)],
                         tag="ye1")
                    p.op("vector",
                         lambda eng, pz=pz: eng.scalar_tensor_tensor(
                             Ldif[:, 0:128], pz[:, 0:128], 1.0,
                             Ldif[:, 0:128], ALU.mult, ALU.add),
                         reads=[bkey(pz), bkey(Ldif)], writes=[bkey(Ldif)],
                         tag="ye1b")
                    p.op("vector",
                         lambda eng, it=it, h=h: eng.scalar_tensor_tensor(
                             Lh[:, 0:128], xT[:, h, it, 0:128],
                             dbc[:, h:h + 1], Ldif[:, 0:128],
                             ALU.mult, ALU.add),
                         reads=[bkey(xT, h), bkey(dbc), bkey(Ldif)],
                         writes=[bkey(Lh)], tag="ye2")
                    p.op("vector",
                         lambda eng: eng.tensor_copy(stage[:, 0:128],
                                                     Lh[:, 0:128]),
                         reads=[bkey(Lh)], writes=[bkey(stage)], tag="ye3")
                    ptb = PT2[it % 2]
                    p.op("tensor",
                         lambda eng, ptb=ptb: eng.transpose(
                             ptb[:, 0:128], stage[:, 0:128], iden_bf[:]),
                         reads=[bkey(stage), bkey(iden_bf)],
                         writes=[bkey(ptb)], tag="ye4")
                    p.op("vector",
                         lambda eng, h=h, it=it, ptb=ptb: eng.tensor_copy(
                             yfm[:, h, it * 128:(it + 1) * 128],
                             ptb[:, 0:128]),
                         reads=[bkey(ptb)], writes=[bkey(yfm, h)], tag="ye5")
                for qt in range(2):
                    p.op("vector",
                         lambda eng, qt=qt, h=h: eng.tensor_scalar(
                             Btl[:, 0:64], BT[:, qt, 0:64],
                             ddT[:, qt, h:h + 1], None, ALU.mult),
                         reads=[bkey(BT), bkey(ddT)], writes=[bkey(Btl)],
                         tag="Bt1")
                    p.op("tensor",
                         lambda eng, qt=qt, h=h: eng.matmul(
                             PS[1][0:64, 0:128], Btl[:, 0:64],
                             xT[:, h, qt, 0:128],
                             start=(qt == 0), stop=(qt == 1)),
                         reads=[bkey(Btl), bkey(xT, h)],
                         writes=[bkey(PS[1])], tag="st_mm")
                p.op("tensor",
                     lambda eng: eng.matmul(
                         PS[0][0:64, 0:1], onesf[0:1, 0:64],
                         arow[0:1, CH - 1:CH],
                         start=True, stop=True),
                     reads=[bkey(onesf), bkey(arow)], writes=[bkey(PS[0])],
                     tag="E_mm")
                p.op("scalar", lambda eng: eng.activation(
                    Epair[0:64, 0:1], PS[0][0:64, 0:1], AF.Exp),
                    reads=[bkey(PS[0])], writes=[bkey(Epair)], tag="E_e")
                p.op("vector",
                     lambda eng, h=h: eng.scalar_tensor_tensor(
                         prev[0:64, h, 0:128], prev[0:64, h, 0:128],
                         Epair[0:64, 0:1], PS[1][0:64, 0:128],
                         ALU.mult, ALU.add),
                     reads=[bkey(prev, h), bkey(Epair), bkey(PS[1])],
                     writes=[bkey(prev, h)], tag="st_up")
            if own:
                for h in range(32):
                    if h % 8 == 0:
                        for k in range(8):
                            dma_load(p, "sync", zc[:, k, 0:CH],
                                     zxd[h + k, :, c0:c0 + CH],
                                     dst_keys=[bkey(zc, k)],
                                     src_keys=[("dram", "zxd", h + k)],
                                     tag="z_ld")
                    p.op("scalar",
                         lambda eng, h=h: eng.activation(
                             stage[:, 0:CH], zc[:, h % 8, 0:CH], AF.Sigmoid),
                         reads=[bkey(zc, h % 8)], writes=[bkey(stage)],
                         tag="gz1")
                    p.op("vector",
                         lambda eng, h=h: eng.tensor_mul(
                             yfm[:, h, 0:CH], yfm[:, h, 0:CH],
                             stage[:, 0:CH]),
                         reads=[bkey(yfm, h), bkey(stage)],
                         writes=[bkey(yfm, h)], tag="gz2")
                    p.op("scalar",
                         lambda eng, h=h: eng.activation(
                             stage[:, 0:CH], yfm[:, h, 0:CH], AF.Square),
                         reads=[bkey(yfm, h)], writes=[bkey(stage)],
                         tag="gn1")
                    p.op("tensor",
                         lambda eng: eng.matmul(
                             PS[1][:, 0:CH], ones_bf[:], stage[:, 0:CH],
                             start=True, stop=True),
                         reads=[bkey(stage), bkey(ones_bf)],
                         writes=[bkey(PS[1])], tag="gn2")
                    eps_ap, eps_k = cp.get(1e-6)
                    p.op("vector",
                         lambda eng: eng.tensor_scalar(
                             Ldif[:, 0:CH], PS[1][:, 0:CH], 1.0 / 128,
                             None, ALU.mult),
                         reads=[bkey(PS[1])], writes=[bkey(Ldif)], tag="gn3")
                    p.op("scalar",
                         lambda eng, eps_ap=eps_ap: eng.activation(
                             Ldif[:, 0:CH], Ldif[:, 0:CH], AF.Ln,
                             bias=eps_ap),
                         reads=[bkey(Ldif), eps_k], writes=[bkey(Ldif)],
                         tag="gn4")
                    p.op("scalar",
                         lambda eng: eng.activation(
                             Ldif[:, 0:CH], Ldif[:, 0:CH], AF.Exp,
                             scale=-0.5),
                         reads=[bkey(Ldif)], writes=[bkey(Ldif)], tag="gn5")
                    p.op("vector",
                         lambda eng, h=h: eng.tensor_mul(
                             yfm[:, h, 0:CH], yfm[:, h, 0:CH], Ldif[:, 0:CH]),
                         reads=[bkey(yfm, h), bkey(Ldif)],
                         writes=[bkey(yfm, h)], tag="gn6")
                    p.op("vector",
                         lambda eng, h=h: eng.tensor_scalar(
                             yfm[:, h, 0:CH], yfm[:, h, 0:CH],
                             nw_sb[:, h:h + 1], None, ALU.mult),
                         reads=[bkey(yfm, h), bkey(nw_sb)],
                         writes=[bkey(yfm, h)], tag="gn7")
                dma_load_c(p, "sync", xtmp[:, :, 0:CH],
                           xv[:, :, c0:c0 + CH], NCHUNK,
                           dst_keys=[bkey(xtmp, ("x", c))
                                     for c in range(NCHUNK)], tag="xr_ld")
                for wq in range(4):
                    dma_load_c(p, "gpsimd",
                               w_sb[:, :, :].rearrange(
                                   "q a b -> q (a b)").rearrange(
                                   "q (c f) -> q c f", c=32),
                               wov[:, :, wq * 512:(wq + 1) * 512], 32,
                               dst_keys=[bkey(w_sb)], tag="wo_ld", group=8)
                    for j in range(4):
                        oc = wq * 4 + j
                        pg = PS[4 + j % 2]
                        for c32 in range(32):
                            p.op("tensor",
                                 lambda eng, c32=c32, j=j, pg=pg: eng.matmul(
                                     pg[:, 0:CH],
                                     w_sb[:, c32 // 2,
                                          (c32 % 2) * 512 + j * 128:
                                          (c32 % 2) * 512 + (j + 1) * 128],
                                     yfm[:, c32, 0:CH],
                                     start=(c32 == 0), stop=(c32 == 31)),
                                 reads=[bkey(w_sb), bkey(yfm, c32)],
                                 writes=[bkey(pg)], tag="op_mm")
                        p.op("vector",
                             lambda eng, oc=oc, pg=pg:
                             eng.scalar_tensor_tensor(
                                 xtmp[:, oc, 0:CH], pg[:, 0:CH], 1.0,
                                 xtmp[:, oc, 0:CH], ALU.mult, ALU.add),
                             reads=[bkey(pg), bkey(xtmp, ("x", oc))],
                             writes=[bkey(xtmp, ("x", oc))], tag="op_acc")
                dma_load_c(p, "sync", xov[:, :, c0 - q0:c0 - q0 + CH],
                           xtmp[:, :, 0:CH], NCHUNK,
                           dst_keys=[("OUT", c0)],
                           src_keys=[bkey(xtmp, ("x", c))
                                     for c in range(NCHUNK)], tag="x_st")
        p.emit_all(ctx)
    return nc


def prep_mamba_weights(inputs):
    inw = np.asarray(inputs["L1_attn_a_in_proj_w"])
    ow = np.asarray(inputs["L1_attn_a_o_proj_w"])
    cw = np.asarray(inputs["L1_attn_a_conv_w"])
    cb = np.asarray(inputs["L1_attn_a_conv_b"])
    out = {}
    inT = np.ascontiguousarray(inw.T)
    out["inT"] = inT.reshape(NCHUNK, 128, 8352).transpose(1, 0, 2).reshape(
        128, NCHUNK * 8352).copy()
    oT = np.ascontiguousarray(ow.T)
    out["oT"] = oT.reshape(32, 128, 2048).transpose(1, 0, 2).reshape(
        128, 32 * 2048).copy()
    out["cw"] = cw.reshape(33, 128, 4).transpose(1, 0, 2).reshape(128, 132).copy()
    out["cb"] = cb.reshape(33, 128).T.copy()
    hp = np.zeros((32, 4), np.float32)
    hp[:, 0] = np.asarray(inputs["L1_attn_a_A_log"])
    hp[:, 1] = np.asarray(inputs["L1_attn_a_dt_bias"])
    hp[:, 2] = np.asarray(inputs["L1_attn_a_D_skip"])
    out["hp"] = hp
    out["nw"] = np.asarray(inputs["L1_attn_a_norm_w"]).reshape(32, 128).T.copy()
    out["ds_row"] = np.asarray(inputs["L1_attn_a_D_skip"]).reshape(1, 32).copy()
    CH = 256
    mks = np.zeros((128, 2, CH), np.float32)
    for jt in range(2):
        for j in range(128):
            jg = jt * 128 + j
            mks[j, jt, :jg] = -1e30
    out["mks"] = mks.reshape(128, 2 * CH).copy()
    return out


def _half_cols(xfm, h):
    v = xfm.reshape(128, NCHUNK, SEQ)[:, :, h * 1024:(h + 1) * 1024]
    return np.ascontiguousarray(v).reshape(128, NCHUNK * 1024)


def _merge_halves(h0, h1):
    out = np.empty((128, NCHUNK, SEQ), np.float32)
    out[:, :, 0:1024] = h0.reshape(128, NCHUNK, 1024)
    out[:, :, 1024:2048] = h1.reshape(128, NCHUNK, 1024)
    return out.reshape(128, NCHUNK * SEQ)


def kernel(**inputs):
    cst = consts_common()
    iden_bf = np.eye(128, dtype=bf)
    ltri = ltri_mask()
    CS, SN2, swp = rope_tables(SEQ)

    w_rg = prep_rglru_weights(inputs)
    w_f0 = prep_ffn0_weights(inputs)
    w_a0 = prep_attn_l0_weights(inputs)
    w_m0 = prep_moe_attnb_weights(inputs)
    w_mb = prep_mamba_weights(inputs)
    w_f1 = prep_l1_ffn_weights(inputs)
    w_a1 = prep_attn_l1_weights(inputs)

    x_in = np.asarray(inputs["L0_attn_a_x"])
    xfm = [to_fm(x_in[b]) for b in range(BATCH)]

    def run_split(mk, extra):
        halves = []
        for h in range(2):
            nc = mk(SEQ, h * 1024, (h + 1) * 1024)
            im = [{"x": xfm[b], **extra} for b in range(BATCH)]
            res = run_bass_kernel_spmd(nc, im, core_ids=list(range(BATCH)))
            halves.append([res.results[b]["xo"] for b in range(BATCH)])
        for b in range(BATCH):
            xfm[b] = _merge_halves(halves[0][b], halves[1][b])

    def run_half(nc, extra):
        im = []
        for c in range(NCORES):
            b, h = c // 2, c % 2
            d = {"x": _half_cols(xfm[b], h)}
            d.update(extra)
            im.append(d)
        res = run_bass_kernel_spmd(nc, im, core_ids=list(range(NCORES)))
        for b in range(BATCH):
            xfm[b] = _merge_halves(res.results[2 * b]["xo"],
                                   res.results[2 * b + 1]["xo"])

    def run_half_xres(nc, extra, xres):
        im = []
        for c in range(NCORES):
            b, h = c // 2, c % 2
            d = {"x": _half_cols(xfm[b], h), "xres": _half_cols(xres[b], h)}
            d.update(extra)
            im.append(d)
        res = run_bass_kernel_spmd(nc, im, core_ids=list(range(NCORES)))
        for b in range(BATCH):
            xfm[b] = _merge_halves(res.results[2 * b]["xo"],
                                   res.results[2 * b + 1]["xo"])

    run_split(build_rglru, {**w_rg, "ones_bf": cst["ones_bf"]})

    nc_f0 = build_ffn0(1024)
    run_half(nc_f0, {**w_f0, **cst})

    xfm_prev = [x.copy() for x in xfm]
    run_split(build_attn_l0, {**w_a0, "CS": CS, "SN2": SN2, "swp": swp,
                              "ones_bf": cst["ones_bf"], "iden_bf": iden_bf,
                              "ltri": ltri})

    xres_snapshot = [x.copy() for x in xfm_prev]
    nc = build_moe_attnb(1024)
    run_half_xres(nc, {**w_m0, "ones_bf": cst["ones_bf"],
                       "iden_f32": cst["iden_f32"], "ind8": cst["ind8"]},
                  xres_snapshot)

    run_half(nc_f0, {**w_f0, **cst})

    run_split(build_mamba, {**w_mb, "ones_bf": cst["ones_bf"],
                            "iden_bf": iden_bf, "iden_f32": cst["iden_f32"]})

    nc_f1 = build_l1_ffn(1024)
    run_half(nc_f1, {**w_f1, "ones_bf": cst["ones_bf"],
                     "iden_f32": cst["iden_f32"]})

    run_split(build_attn_l1, {**w_a1, "ones_bf": cst["ones_bf"],
                              "iden_bf": iden_bf, "ltri": ltri})

    run_half(nc_f1, {**w_f1, "ones_bf": cst["ones_bf"],
                     "iden_f32": cst["iden_f32"]})

    out = np.stack([from_fm(xfm[b], SEQ) for b in range(BATCH)], axis=0)
    return out.astype(np.float32)


# version 04
# speedup vs baseline: 1.2049x; 1.2049x over previous
import numpy as np
from contextlib import ExitStack

import ml_dtypes
import concourse.bass as bass
import concourse.mybir as mybir
from concourse.bass_utils import run_bass_kernel_spmd

F32 = mybir.dt.float32
BF16 = mybir.dt.bfloat16
I32 = mybir.dt.int32
U8 = mybir.dt.uint8
AF = mybir.ActivationFunctionType
ALU = mybir.AluOpType
AX = mybir.AxisListType

D = 2048
NCHUNK = D // 128
BATCH = 4
SEQ = 2048
NCORES = 8

bf = ml_dtypes.bfloat16


class Op:
    __slots__ = ("engine", "emit", "reads", "writes", "inc", "waits", "tag",
                 "_need", "token")

    def __init__(self, engine, emit, reads, writes, inc, tag):
        self.engine = engine
        self.emit = emit
        self.reads = reads
        self.writes = writes
        self.inc = inc
        self.waits = []
        self.tag = tag


class Prog:
    ENGINES = ("sync", "gpsimd", "tensor", "vector", "scalar")
    NDMA = 10

    def __init__(self, nc):
        self.nc = nc
        self.ops = {e: [] for e in self.ENGINES}
        self.cnt = {e: 0 for e in self.ENGINES}
        self.dma_cnt = {"sync": 0, "gpsimd": 0}
        self.last_w = {}
        self.readers = {}
        self.wm = {e: {} for e in self.ENGINES}

    def _need(self, o, sem_id, val):
        cur = o._need.get(sem_id, 0)
        if val > cur:
            o._need[sem_id] = val

    def op(self, engine, emit, reads=(), writes=(), dma=False, tag=""):
        o = Op(engine, emit, tuple(reads), tuple(writes), 16 if dma else 1,
               tag)
        o._need = {}
        for k in o.reads:
            lw = self.last_w.get(k)
            if lw is not None:
                self._need(o, *lw)
        for k in o.writes:
            lw = self.last_w.get(k)
            if lw is not None:
                self._need(o, *lw)
            for sem_id, v in self.readers.get(k, {}).items():
                self._need(o, sem_id, v)
        if dma:
            assert engine in self.dma_cnt
            i = self.dma_cnt[engine]
            self.dma_cnt[engine] = i + 1
            slot = i % self.NDMA
            sem_id = ("dma", engine, slot)
            if i >= self.NDMA:
                self._need(o, sem_id, 16 * (i // self.NDMA))
            token = (sem_id, 16 * (i // self.NDMA + 1))
        else:
            self.cnt[engine] += 1
            token = (("eng", engine), self.cnt[engine])
        wmE = self.wm[engine]
        for sem_id, v in sorted(o._need.items(), key=lambda kv: str(kv[0])):
            if wmE.get(sem_id, 0) < v:
                o.waits.append((sem_id, v))
                wmE[sem_id] = v
        for k in o.writes:
            self.last_w[k] = token
            self.readers[k] = {}
        for k in o.reads:
            self.readers.setdefault(k, {})[token[0]] = token[1]
        o.token = token
        self.ops[engine].append(o)
        return o

    def emit_all(self, ctx):
        nc = self.nc
        sems = {}
        for e in self.ENGINES:
            sems[("eng", e)] = ctx.enter_context(nc.semaphore(f"sem_{e}"))
        for q in ("sync", "gpsimd"):
            for s in range(self.NDMA):
                sems[("dma", q, s)] = ctx.enter_context(
                    nc.semaphore(f"dmas_{q}_{s}"))
        block = ctx.enter_context(nc.Block())

        def body(engine_obj, ename):
            for o in self.ops[ename]:
                for sem_id, v in o.waits:
                    engine_obj.wait_ge(sems[sem_id], v)
                inst = o.emit(engine_obj)
                if inst is not None:
                    inst.then_inc(sems[o.token[0]], o.inc)

        finals = []
        for e in self.ENGINES:
            if self.cnt[e] > 0:
                finals.append((("eng", e), self.cnt[e]))
        for q in ("sync", "gpsimd"):
            n = self.dma_cnt[q]
            for s in range(self.NDMA):
                uses = (n - s + self.NDMA - 1) // self.NDMA
                if uses > 0:
                    finals.append((("dma", q, s), 16 * uses))

        @block.sync
        def _(sync):
            body(sync, "sync")
            for sem_id, v in finals:
                sync.wait_ge(sems[sem_id], v)

        @block.gpsimd
        def _(gpsimd):
            body(gpsimd, "gpsimd")

        @block.tensor
        def _(tensor):
            body(tensor, "tensor")

        @block.vector
        def _(vector):
            body(vector, "vector")

        @block.scalar
        def _(scalar):
            body(scalar, "scalar")


def bkey(h, tag=None):
    return (h.name if hasattr(h, "name") else id(h), tag)


def dma_load(p, engine, dst_ap, src_ap, dst_keys, src_keys=(), tag=""):
    def em(eng, dst_ap=dst_ap, src_ap=src_ap):
        return eng.dma_start(out=dst_ap, in_=src_ap)
    p.op(engine, em, reads=src_keys, writes=dst_keys, dma=True, tag=tag)


def dma_load_c(p, engine, dst3, src3, nC, dst_keys, src_keys=(), tag="",
               group=4):
    for c0, cl in split_chunks(nC, group):
        dma_load(p, engine, dst3[:, c0:c0 + cl, :], src3[:, c0:c0 + cl, :],
                 dst_keys=dst_keys, src_keys=src_keys, tag=tag)


def split_chunks(n, step):
    out = []
    i = 0
    while i < n:
        out.append((i, min(step, n - i)))
        i += step
    return out


class ConstPool:

    def __init__(self, p, nc, ctx, n=32, prefix="sb_"):
        self.buf = ctx.enter_context(nc.sbuf_tensor(prefix + "constpool", [128, n], F32))
        self.p = p
        self.vals = {}

    def get(self, v):
        v = float(v)
        if v not in self.vals:
            i = len(self.vals)
            self.vals[v] = i
            self.p.op("vector",
                      lambda eng, i=i, v=v: eng.memset(self.buf[:, i:i + 1], v),
                      writes=[bkey(self.buf, i)], tag="const")
        i = self.vals[v]
        return self.buf[:, i:i + 1], bkey(self.buf, i)

    def get_p(self, v, pn):
        ap, k = self.get(v)
        return self.buf[0:pn, self.vals[float(v)]:self.vals[float(v)] + 1], k


def emit_layernorm(p, nc, T, x_sb, lnb, sq_bf, ones_bf, ps_m, ps_v,
                   mea, var, rstd, tmpf, xkey="x", eps=1e-5, lnf=None):
    kx = lambda c: bkey(x_sb, (xkey, c))
    inv_d = 1.0 / D
    for t0, tl in split_chunks(T, 512):
        sl = slice(t0, t0 + tl)
        for c in range(NCHUNK):
            p.op("vector",
                 lambda eng, c=c, sl=sl, tl=tl: eng.tensor_copy(
                     lnb[:, c, sl], x_sb[:, c, sl]),
                 reads=[kx(c)], writes=[bkey(lnb, c)], tag="ln_cast")
            p.op("scalar",
                 lambda eng, c=c, sl=sl, tl=tl: eng.activation(
                     sq_bf[:, c, 0:tl], x_sb[:, c, sl], AF.Square),
                 reads=[kx(c)], writes=[bkey(sq_bf, c)], tag="ln_sq")
        for c in range(NCHUNK):
            p.op("tensor",
                 lambda eng, c=c, sl=sl, tl=tl: eng.matmul(
                     ps_m[:, 0:tl], ones_bf[:], lnb[:, c, sl],
                     start=(c == 0), stop=(c == NCHUNK - 1)),
                 reads=[bkey(lnb, c), bkey(ones_bf)],
                 writes=[bkey(ps_m)], tag="ln_summ")
        for c in range(NCHUNK):
            p.op("tensor",
                 lambda eng, c=c, tl=tl: eng.matmul(
                     ps_v[:, 0:tl], ones_bf[:], sq_bf[:, c, 0:tl],
                     start=(c == 0), stop=(c == NCHUNK - 1)),
                 reads=[bkey(sq_bf, c), bkey(ones_bf)],
                 writes=[bkey(ps_v)], tag="ln_sumv")
        p.op("vector", lambda eng, tl=tl: eng.tensor_scalar(
            mea[:, 0:tl], ps_m[:, 0:tl], inv_d, None, ALU.mult),
            reads=[bkey(ps_m)], writes=[bkey(mea)], tag="ln_mean")
        p.op("vector", lambda eng, tl=tl: eng.tensor_scalar(
            var[:, 0:tl], ps_v[:, 0:tl], inv_d, None, ALU.mult),
            reads=[bkey(ps_v)], writes=[bkey(var)], tag="ln_var0")
        p.op("scalar", lambda eng, tl=tl: eng.activation(
            rstd[:, 0:tl], mea[:, 0:tl], AF.Square),
            reads=[bkey(mea)], writes=[bkey(rstd)], tag="ln_m2")
        p.op("vector", lambda eng, tl=tl: eng.tensor_sub(
            var[:, 0:tl], var[:, 0:tl], rstd[:, 0:tl]),
            reads=[bkey(var), bkey(rstd)], writes=[bkey(var)], tag="ln_var")
        eps_ap, eps_k = cp.get(eps)
        p.op("scalar", lambda eng, tl=tl, eps_ap=eps_ap: eng.activation(
            var[:, 0:tl], var[:, 0:tl], AF.Ln, bias=eps_ap),
            reads=[bkey(var), eps_k], writes=[bkey(var)], tag="ln_ln")
        p.op("scalar", lambda eng, tl=tl: eng.activation(
            rstd[:, 0:tl], var[:, 0:tl], AF.Exp, scale=-0.5),
            reads=[bkey(var)], writes=[bkey(rstd)], tag="ln_rstd")
        for c in range(NCHUNK):
            p.op("vector",
                 lambda eng, c=c, sl=sl, tl=tl: eng.tensor_sub(
                     tmpf[:, 0:tl], x_sb[:, c, sl], mea[:, 0:tl]),
                 reads=[kx(c), bkey(mea)], writes=[bkey(tmpf)], tag="ln_sub")
            p.op("vector",
                 lambda eng, c=c, sl=sl, tl=tl: eng.tensor_mul(
                     lnb[:, c, sl], tmpf[:, 0:tl], rstd[:, 0:tl]),
                 reads=[bkey(tmpf), bkey(rstd)], writes=[bkey(lnb, c)],
                 tag="ln_out")
            if lnf is not None:
                p.op("vector",
                     lambda eng, c=c, sl=sl, tl=tl: eng.tensor_mul(
                         lnf[:, c, sl], tmpf[:, 0:tl], rstd[:, 0:tl]),
                     reads=[bkey(tmpf), bkey(rstd)],
                     writes=[bkey(lnf, ("x", c))], tag="ln_outf")


def emit_router(p, nc, T, lnb, gwT_sb, wT8, ps_r, ps_t, l_sb, m1, m2, negm1,
                e2t, mask_u8, keep, neg8, iden_f32, tag="rt"):
    ntile = T // 128
    p.op("vector", lambda eng: eng.memset(neg8[:], -1e30), writes=[bkey(neg8)])
    for ti in range(ntile):
        t0 = ti * 128
        for c in range(NCHUNK):
            p.op("tensor",
                 lambda eng, c=c, t0=t0: eng.matmul(
                     ps_r[:, 0:8], lnb[:, c, t0:t0 + 128], gwT_sb[:, c, 0:8],
                     start=(c == 0), stop=(c == NCHUNK - 1)),
                 reads=[bkey(lnb, c), bkey(gwT_sb)],
                 writes=[bkey(ps_r)], tag=tag + "_mm")
        p.op("vector", lambda eng: eng.tensor_copy(l_sb[:, 0:8], ps_r[:, 0:8]),
             reads=[bkey(ps_r)], writes=[bkey(l_sb)], tag=tag)
        p.op("vector", lambda eng: eng.reduce_max(m1[:, 0:1], l_sb[:, 0:8],
                                                  axis=AX.X),
             reads=[bkey(l_sb)], writes=[bkey(m1)], tag=tag)
        p.op("vector", lambda eng: eng.tensor_scalar(mask_u8[:, 0:8],
                                                     l_sb[:, 0:8], m1[:, 0:1],
                                                     None, ALU.is_ge),
             reads=[bkey(l_sb), bkey(m1)], writes=[bkey(mask_u8)], tag=tag)
        p.op("vector", lambda eng: eng.select(keep[:, 0:8], mask_u8[:, 0:8],
                                              neg8[:, 0:8], l_sb[:, 0:8],
                                              add_drain=True),
             reads=[bkey(mask_u8), bkey(neg8), bkey(l_sb)],
             writes=[bkey(keep)], tag=tag)
        p.op("vector", lambda eng: eng.reduce_max(m2[:, 0:1], keep[:, 0:8],
                                                  axis=AX.X),
             reads=[bkey(keep)], writes=[bkey(m2)], tag=tag)
        p.op("vector", lambda eng: eng.tensor_scalar(negm1[:, 0:1], m1[:, 0:1],
                                                     -1.0, None, ALU.mult),
             reads=[bkey(m1)], writes=[bkey(negm1)], tag=tag)
        p.op("scalar", lambda eng: eng.activation(l_sb[:, 0:8], l_sb[:, 0:8],
                                                  AF.Exp, bias=negm1[:, 0:1]),
             reads=[bkey(l_sb), bkey(negm1)], writes=[bkey(l_sb)], tag=tag)
        p.op("scalar", lambda eng: eng.activation(e2t[:, 0:1], m2[:, 0:1],
                                                  AF.Exp, bias=negm1[:, 0:1]),
             reads=[bkey(m2), bkey(negm1)], writes=[bkey(e2t)], tag=tag)
        p.op("vector", lambda eng: eng.tensor_scalar(e2t[:, 0:1], e2t[:, 0:1],
                                                     1.0, None, ALU.add),
             reads=[bkey(e2t)], writes=[bkey(e2t)], tag=tag)
        p.op("vector", lambda eng: eng.reciprocal(e2t[:, 0:1], e2t[:, 0:1]),
             reads=[bkey(e2t)], writes=[bkey(e2t)], tag=tag)
        p.op("vector", lambda eng: eng.tensor_scalar(keep[:, 0:8],
                                                     ps_r[:, 0:8], m2[:, 0:1],
                                                     None, ALU.is_ge),
             reads=[bkey(ps_r), bkey(m2)], writes=[bkey(keep)], tag=tag)
        p.op("vector", lambda eng: eng.tensor_mul(l_sb[:, 0:8], l_sb[:, 0:8],
                                                  keep[:, 0:8]),
             reads=[bkey(l_sb), bkey(keep)], writes=[bkey(l_sb)], tag=tag)
        p.op("vector", lambda eng: eng.tensor_scalar(l_sb[:, 0:8], l_sb[:, 0:8],
                                                     e2t[:, 0:1], None,
                                                     ALU.mult),
             reads=[bkey(l_sb), bkey(e2t)], writes=[bkey(l_sb)], tag=tag)
        p.op("tensor", lambda eng: eng.transpose(ps_t[0:8, 0:128],
                                                 l_sb[:, 0:8], iden_f32[:]),
             reads=[bkey(l_sb), bkey(iden_f32)], writes=[bkey(ps_t)], tag=tag)
        p.op("vector",
             lambda eng, t0=t0: eng.tensor_copy(wT8[0:8, t0:t0 + 128],
                                                ps_t[0:8, 0:128]),
             reads=[bkey(ps_t)], writes=[bkey(wT8, ti)], tag=tag)





F_L0 = 1408
NF_L0 = F_L0 // 128


def emit_moe(p, nc, T, bufs, w, NF=NF_L0, norm="ln", nw_sb=None,
             rn_eps=1e-5, xres_ap=None, xres_t=None):
    x_sb = bufs["x_sb"]; lnb = bufs["lnb"]; PS = bufs["PS"]
    scr = bufs["scr_bf"]
    w_gu = bufs["w_gu"]
    w_dn = bufs["w_dn"]
    wT8 = bufs["wT8"]
    ones_bf = bufs["ones_bf"]; iden_f32 = bufs["iden_f32"]
    ind8 = bufs["ind8"]
    gwT_sb = bufs["gwT_sb"]

    skey = lambda off, ln: [bkey(scr, i) for i in range(off // 512, (off + ln + 511) // 512)]

    if norm == "ln":
        emit_layernorm_flat(p, nc, T, x_sb, lnb, scr, ones_bf, PS[0], PS[1],
                            bufs["mea"], bufs["var"], bufs["rstd"],
                            bufs["tmpf"], skey, cp=bufs["cp"])
    elif norm == "rms":
        emit_rmsnorm_flat(p, nc, T, x_sb, lnb, scr, ones_bf, PS[1],
                          bufs["var"], bufs["rstd"], skey, bufs["cp"], nw_sb,
                          eps=rn_eps)
    if xres_ap is not None:
        for t0, tl in split_chunks(T, 256):
            dma_load_c(p, "sync", xres_t[:, :, 0:tl],
                       xres_ap[:, :, t0:t0 + tl], NCHUNK,
                       dst_keys=[bkey(xres_t, c) for c in range(NCHUNK)],
                       tag="xr_ld")
            for c in range(NCHUNK):
                p.op("vector",
                     lambda eng, c=c, t0=t0, tl=tl: eng.tensor_add(
                         x_sb[:, c, t0:t0 + tl], x_sb[:, c, t0:t0 + tl],
                         xres_t[:, c, 0:tl]),
                     reads=[bkey(x_sb, ("x", c)), bkey(xres_t, c)],
                     writes=[bkey(x_sb, ("x", c))], tag="xr_add")
    emit_router(p, nc, T, lnb, gwT_sb, wT8, PS[2], PS[3], bufs["l_sb"],
                bufs["m1"], bufs["m2"], bufs["negm1"], bufs["e2t"],
                bufs["mask_u8"], bufs["keep"], bufs["neg8"], iden_f32)

    strips = split_chunks(T, 512)
    FG = w_gu.shape[2] // 128
    for e in range(8):
        for f0, fn in split_chunks(NF, FG):
            dma_load_c(p, "gpsimd", w_gu[:, :, 0:fn * 128],
                       w["gpT"][e].rearrange("q (c f) -> q c f", c=NCHUNK)[
                           :, :, f0 * 128:(f0 + fn) * 128], NCHUNK,
                       dst_keys=[bkey(w_gu, j) for j in range(4)],
                       tag="gp_ld")
            for t0, tl in strips:
                for fl in range(fn):
                    fc = f0 + fl
                    pg = PS[4 + fc % 2]
                    for c in range(NCHUNK):
                        p.op("tensor",
                             lambda eng, c=c, fl=fl, t0=t0, tl=tl, pg=pg:
                             eng.matmul(
                                 pg[:, 0:tl],
                                 w_gu[:, c, fl * 128:(fl + 1) * 128],
                                 lnb[:, c, t0:t0 + tl],
                                 start=(c == 0), stop=(c == NCHUNK - 1)),
                             reads=[bkey(w_gu, c // 4), bkey(lnb, c)],
                             writes=[bkey(pg)], tag="g_mm")
                    off = fc * T + t0
                    p.op("scalar",
                         lambda eng, tl=tl, pg=pg, off=off: eng.activation(
                             scr[:, off:off + tl], pg[:, 0:tl], AF.Silu),
                         reads=[bkey(pg)], writes=skey(off, tl),
                         tag="g_silu")
        for f0, fn in split_chunks(NF, FG):
            dma_load_c(p, "gpsimd", w_gu[:, :, 0:fn * 128],
                       w["upT"][e].rearrange("q (c f) -> q c f", c=NCHUNK)[
                           :, :, f0 * 128:(f0 + fn) * 128], NCHUNK,
                       dst_keys=[bkey(w_gu, j) for j in range(4)],
                       tag="up_ld")
            for t0, tl in strips:
                for ti in range(tl // 128):
                    p.op("tensor",
                         lambda eng, e=e, t0=t0, ti=ti: eng.matmul(
                             PS[6][:, ti * 128:(ti + 1) * 128],
                             ind8[0:8, e * 128:(e + 1) * 128],
                             wT8[0:8, t0 + ti * 128:t0 + (ti + 1) * 128],
                             start=True, stop=True),
                         reads=[bkey(ind8),
                                bkey(wT8, (t0 + ti * 128) // 128)],
                         writes=[bkey(PS[6])], tag="wb_mm")
                for fl in range(fn):
                    fc = f0 + fl
                    pu = PS[4 + fc % 2]
                    for c in range(NCHUNK):
                        p.op("tensor",
                             lambda eng, c=c, fl=fl, t0=t0, tl=tl, pu=pu:
                             eng.matmul(
                                 pu[:, 0:tl],
                                 w_gu[:, c, fl * 128:(fl + 1) * 128],
                                 lnb[:, c, t0:t0 + tl],
                                 start=(c == 0), stop=(c == NCHUNK - 1)),
                             reads=[bkey(w_gu, c // 4), bkey(lnb, c)],
                             writes=[bkey(pu)], tag="u_mm")
                    off = fc * T + t0
                    p.op("vector",
                         lambda eng, tl=tl, pu=pu, off=off: eng.tensor_mul(
                             scr[:, off:off + tl], scr[:, off:off + tl],
                             pu[:, 0:tl]),
                         reads=skey(off, tl) + [bkey(pu)],
                         writes=skey(off, tl), tag="h_mul")
                    p.op("vector",
                         lambda eng, tl=tl, off=off: eng.tensor_mul(
                             scr[:, off:off + tl], scr[:, off:off + tl],
                             PS[6][:, 0:tl]),
                         reads=skey(off, tl) + [bkey(PS[6])],
                         writes=skey(off, tl), tag="h_wb")
        for half in range(2):
            dma_load(p, "gpsimd", w_dn[:, 0:NF, :],
                     w["dnT_h"][e, half].rearrange("q (c d) -> q c d",
                                                   c=NF),
                     dst_keys=[bkey(w_dn)], tag="dn_ld")
            for t0, tl in strips:
                for dcl in range(8):
                    dc = half * 8 + dcl
                    pd = PS[4 + dcl % 2]
                    for fc in range(NF):
                        off = fc * T + t0
                        p.op("tensor",
                             lambda eng, fc=fc, dcl=dcl, t0=t0, tl=tl, pd=pd,
                             off=off, NF=NF: eng.matmul(
                                 pd[:, 0:tl],
                                 w_dn[:, fc, dcl * 128:(dcl + 1) * 128],
                                 scr[:, off:off + tl],
                                 start=(fc == 0), stop=(fc == NF - 1)),
                             reads=[bkey(w_dn)] + skey(off, tl),
                             writes=[bkey(pd)], tag="d_mm")
                    p.op("vector",
                         lambda eng, dc=dc, t0=t0, tl=tl, pd=pd:
                         eng.scalar_tensor_tensor(
                             x_sb[:, dc, t0:t0 + tl], pd[:, 0:tl], 1.0,
                             x_sb[:, dc, t0:t0 + tl], ALU.mult, ALU.add),
                         reads=[bkey(pd), bkey(x_sb, ("x", dc))],
                         writes=[bkey(x_sb, ("x", dc))], tag="d_acc")


def emit_layernorm_flat(p, nc, T, x_sb, lnb, scr, ones_bf, ps_m, ps_v,
                        mea, var, rstd, tmpf, skey, cp=None, xkey="x",
                        eps=1e-5, lnf=None):
    kx = lambda c: bkey(x_sb, (xkey, c))
    inv_d = 1.0 / D
    for t0, tl in split_chunks(T, 512):
        sl = slice(t0, t0 + tl)
        for c in range(NCHUNK):
            off = c * 512
            p.op("vector",
                 lambda eng, c=c, sl=sl: eng.tensor_copy(
                     lnb[:, c, sl], x_sb[:, c, sl]),
                 reads=[kx(c)], writes=[bkey(lnb, c)], tag="ln_cast")
            p.op("scalar",
                 lambda eng, c=c, sl=sl, tl=tl, off=off: eng.activation(
                     scr[:, off:off + tl], x_sb[:, c, sl], AF.Square),
                 reads=[kx(c)], writes=skey(off, tl), tag="ln_sq")
        for c in range(NCHUNK):
            p.op("tensor",
                 lambda eng, c=c, sl=sl, tl=tl: eng.matmul(
                     ps_m[:, 0:tl], ones_bf[:], lnb[:, c, sl],
                     start=(c == 0), stop=(c == NCHUNK - 1)),
                 reads=[bkey(lnb, c), bkey(ones_bf)],
                 writes=[bkey(ps_m)], tag="ln_summ")
        for c in range(NCHUNK):
            off = c * 512
            p.op("tensor",
                 lambda eng, c=c, tl=tl, off=off: eng.matmul(
                     ps_v[:, 0:tl], ones_bf[:], scr[:, off:off + tl],
                     start=(c == 0), stop=(c == NCHUNK - 1)),
                 reads=skey(off, tl) + [bkey(ones_bf)],
                 writes=[bkey(ps_v)], tag="ln_sumv")
        p.op("vector", lambda eng, tl=tl: eng.tensor_scalar(
            mea[:, 0:tl], ps_m[:, 0:tl], inv_d, None, ALU.mult),
            reads=[bkey(ps_m)], writes=[bkey(mea)], tag="ln_mean")
        p.op("vector", lambda eng, tl=tl: eng.tensor_scalar(
            var[:, 0:tl], ps_v[:, 0:tl], inv_d, None, ALU.mult),
            reads=[bkey(ps_v)], writes=[bkey(var)], tag="ln_var0")
        p.op("scalar", lambda eng, tl=tl: eng.activation(
            rstd[:, 0:tl], mea[:, 0:tl], AF.Square),
            reads=[bkey(mea)], writes=[bkey(rstd)], tag="ln_m2")
        p.op("vector", lambda eng, tl=tl: eng.tensor_sub(
            var[:, 0:tl], var[:, 0:tl], rstd[:, 0:tl]),
            reads=[bkey(var), bkey(rstd)], writes=[bkey(var)], tag="ln_var")
        eps_ap, eps_k = cp.get(eps)
        p.op("scalar", lambda eng, tl=tl, eps_ap=eps_ap: eng.activation(
            var[:, 0:tl], var[:, 0:tl], AF.Ln, bias=eps_ap),
            reads=[bkey(var), eps_k], writes=[bkey(var)], tag="ln_ln")
        p.op("scalar", lambda eng, tl=tl: eng.activation(
            rstd[:, 0:tl], var[:, 0:tl], AF.Exp, scale=-0.5),
            reads=[bkey(var)], writes=[bkey(rstd)], tag="ln_rstd")
        for c in range(NCHUNK):
            p.op("vector",
                 lambda eng, c=c, sl=sl, tl=tl: eng.tensor_sub(
                     tmpf[:, 0:tl], x_sb[:, c, sl], mea[:, 0:tl]),
                 reads=[kx(c), bkey(mea)], writes=[bkey(tmpf)], tag="ln_sub")
            p.op("vector",
                 lambda eng, c=c, sl=sl, tl=tl: eng.tensor_mul(
                     lnb[:, c, sl], tmpf[:, 0:tl], rstd[:, 0:tl]),
                 reads=[bkey(tmpf), bkey(rstd)], writes=[bkey(lnb, c)],
                 tag="ln_out")
            if lnf is not None:
                p.op("vector",
                     lambda eng, c=c, sl=sl, tl=tl: eng.tensor_mul(
                         lnf[:, c, sl], tmpf[:, 0:tl], rstd[:, 0:tl]),
                     reads=[bkey(tmpf), bkey(rstd)],
                     writes=[bkey(lnf, ("x", c))], tag="ln_outf")


def alloc_common(nc, ctx, T, want=(), prefix="sb_", x_sb=None):
    sb = lambda name, shape, dt=F32: ctx.enter_context(
        nc.sbuf_tensor(prefix + name, shape, dt))
    bufs = {}
    bufs["x_sb"] = x_sb if x_sb is not None else sb("x_sb", [128, NCHUNK, T])
    bufs["lnb"] = sb("lnb", [128, NCHUNK, T], BF16)
    bufs["mea"] = sb("mea", [128, 512])
    bufs["var"] = sb("var", [128, 512])
    bufs["rstd"] = sb("rstd", [128, 512])
    bufs["tmpf"] = sb("tmpf", [128, 512])
    bufs["ones_bf"] = sb("ones_bf", [128, 128], BF16)
    bufs["iden_f32"] = sb("iden_f32", [128, 128])
    bufs["PS"] = [ctx.enter_context(nc.psum_tensor(prefix + f"ps{i}",
                                                   [128, 512], F32))
                  for i in range(8)]
    if "router" in want:
        bufs["wT8"] = sb("wT8", [8, T])
        bufs["gwT_sb"] = sb("gwT_sb", [128, NCHUNK, 8], BF16)
        bufs["ind8"] = sb("ind8", [8, 8 * 128])
        bufs["l_sb"] = sb("l_sb", [128, 8])
        bufs["m1"] = sb("m1", [128, 1])
        bufs["m2"] = sb("m2", [128, 1])
        bufs["negm1"] = sb("negm1", [128, 1])
        bufs["e2t"] = sb("e2t", [128, 1])
        bufs["mask_u8"] = sb("mask_u8", [128, 8], U8)
        bufs["keep"] = sb("keep", [128, 8])
        bufs["neg8"] = sb("neg8", [128, 8])
    return bufs


def load_common(p, nc, T, bufs, aps, want=(), skip_x=False):
    if not skip_x:
        dma_load(p, "sync", bufs["x_sb"][:, :, :],
                 aps["x"].rearrange("q (c t) -> q c t", c=NCHUNK),
                 dst_keys=[bkey(bufs["x_sb"], ("x", c))
                           for c in range(NCHUNK)],
                 tag="x_ld")
    dma_load(p, "sync", bufs["ones_bf"][:], aps["ones_bf"],
             dst_keys=[bkey(bufs["ones_bf"])], tag="c_ld")
    dma_load(p, "sync", bufs["iden_f32"][:], aps["iden_f32"],
             dst_keys=[bkey(bufs["iden_f32"])], tag="c_ld")
    if "router" in want:
        dma_load(p, "sync", bufs["ind8"][:], aps["ind8"],
                 dst_keys=[bkey(bufs["ind8"])], tag="c_ld")
        dma_load(p, "gpsimd", bufs["gwT_sb"][:, :, :],
                 aps["gwT_h"].rearrange("q (c e) -> q c e", c=NCHUNK),
                 dst_keys=[bkey(bufs["gwT_sb"])], tag="c_ld")


def store_x(p, nc, T, bufs, xo_ap, t_lo, t_hi):
    ks = [bkey(bufs["x_sb"], ("x", c)) for c in range(NCHUNK)]
    dma_load(p, "sync",
             xo_ap.rearrange("q (c t) -> q c t", c=NCHUNK),
             bufs["x_sb"][:, :, t_lo:t_hi],
             dst_keys=[("OUT", 0)], src_keys=ks, tag="x_st")


def build_ffn0(T):
    nc = bass.Bass("TRN2", target_bir_lowering=False)
    aps = {}
    aps["x"] = nc.dram_tensor("x", [128, NCHUNK * T], F32,
                              kind="ExternalInput").ap()
    aps["gpT_h"] = nc.dram_tensor("gpT_h", [8, 128, NCHUNK * F_L0], F32,
                                  kind="ExternalInput").ap()
    aps["upT_h"] = nc.dram_tensor("upT_h", [8, 128, NCHUNK * F_L0], F32,
                                  kind="ExternalInput").ap()
    aps["dnT_h"] = nc.dram_tensor("dnT_h", [8, 2, 128, NF_L0 * 1024], F32,
                                  kind="ExternalInput").ap()
    aps["gwT_h"] = nc.dram_tensor("gwT_h", [128, NCHUNK * 8], F32,
                                  kind="ExternalInput").ap()
    aps["ones_bf"] = nc.dram_tensor("ones_bf", [128, 128], BF16,
                                    kind="ExternalInput").ap()
    aps["iden_f32"] = nc.dram_tensor("iden_f32", [128, 128], F32,
                                     kind="ExternalInput").ap()
    aps["ind8"] = nc.dram_tensor("ind8", [8, 8 * 128], F32,
                                 kind="ExternalInput").ap()
    xo = nc.dram_tensor("xo", [128, NCHUNK * T], F32,
                        kind="ExternalOutput").ap()
    with ExitStack() as ctx:
        bufs = alloc_common(nc, ctx, T, want=("router",))
        sb = lambda name, shape, dt=F32: ctx.enter_context(
            nc.sbuf_tensor(name, shape, dt))
        bufs["scr_bf"] = sb("scr_bf", [128, NF_L0 * T], BF16)
        bufs["w_gu"] = sb("w_gu", [128, NCHUNK, F_L0], BF16)
        bufs["w_dn"] = sb("w_dn", [128, NF_L0, 1024], BF16)
        p = Prog(nc)
        bufs["cp"] = ConstPool(p, nc, ctx)
        load_common(p, nc, T, bufs, aps, want=("router",))
        w = {"gpT": [aps["gpT_h"][e] for e in range(8)],
             "upT": [aps["upT_h"][e] for e in range(8)],
             "dnT_h": aps["dnT_h"]}
        emit_moe(p, nc, T, bufs, w, NF=NF_L0)
        store_x(p, nc, T, bufs, xo, 0, T)
        p.emit_all(ctx)
    return nc


def prep_ffn0_weights(inputs):
    gp = np.asarray(inputs["L0_ffn_gate_proj"])
    up = np.asarray(inputs["L0_ffn_up_proj"])
    dn = np.asarray(inputs["L0_ffn_down_proj"])
    gw = np.asarray(inputs["L0_ffn_gate_w"])
    gpT = np.ascontiguousarray(gp.transpose(0, 2, 1))
    upT = np.ascontiguousarray(up.transpose(0, 2, 1))
    dnT = np.ascontiguousarray(dn.transpose(0, 2, 1))
    gpT_h = gpT.reshape(8, NCHUNK, 128, F_L0).transpose(0, 2, 1, 3).reshape(
        8, 128, NCHUNK * F_L0).copy()
    upT_h = upT.reshape(8, NCHUNK, 128, F_L0).transpose(0, 2, 1, 3).reshape(
        8, 128, NCHUNK * F_L0).copy()
    dnT_4d = dnT.reshape(8, NF_L0, 128, 2048)
    dnT_h = np.stack(
        [dnT_4d[:, :, :, 0:1024], dnT_4d[:, :, :, 1024:2048]], axis=1
    ).transpose(0, 1, 3, 2, 4).reshape(8, 2, 128, NF_L0 * 1024).copy()
    gwT = np.ascontiguousarray(gw.T)
    gwT_h = gwT.reshape(NCHUNK, 128, 8).transpose(1, 0, 2).reshape(
        128, NCHUNK * 8).copy()
    return {"gpT_h": gpT_h, "upT_h": upT_h, "dnT_h": dnT_h, "gwT_h": gwT_h}


def consts_common():
    return {
        "ones_bf": np.ones((128, 128), dtype=bf),
        "iden_f32": np.eye(128, dtype=np.float32),
        "ind8": np.repeat(np.eye(8, dtype=np.float32), 128, axis=1),
    }


def to_fm(x_bt_d):
    T = x_bt_d.shape[0]
    return np.ascontiguousarray(
        x_bt_d.T.reshape(NCHUNK, 128, T).transpose(1, 0, 2).reshape(
            128, NCHUNK * T))


def from_fm(dev, T):
    return np.ascontiguousarray(
        dev.reshape(128, NCHUNK, T).transpose(1, 0, 2).reshape(D, T).T)


def emit_l1_ffn(p, nc, T, bufs, w, tbase=0):
    assert tbase % 4 == 0
    x_sb = bufs["x_sb"]; lnb = bufs["lnb"]; PS = bufs["PS"]
    scr = bufs["scr_bf"]; w_gu = bufs["w_gu16"]; cp = bufs["cp"]
    ones_bf = bufs["ones_bf"]
    Te = T // 4
    skey = lambda off, ln: [bkey(scr, i) for i in
                            range(off // 512, (off + ln + 511) // 512)]
    emit_layernorm_flat(p, nc, T, x_sb, lnb, scr, ones_bf, PS[0], PS[1],
                        bufs["mea"], bufs["var"], bufs["rstd"], bufs["tmpf"],
                        skey, cp=cp)
    for e in range(4):
        for q in range(4):
            dma_load(p, "gpsimd", w_gu[:, :, :],
                     w["guT_q"][e, q].rearrange("r (c f) -> r c f", c=NCHUNK),
                     dst_keys=[bkey(w_gu)], tag="gu_ld")
            for j in range(8):
                oc = q * 8 + j
                pg = PS[4 + j % 2]
                for c in range(NCHUNK):
                    p.op("tensor",
                         lambda eng, c=c, j=j, e=e, pg=pg, Te=Te: eng.matmul(
                             pg[:, 0:Te],
                             w_gu[:, c, j * 128:(j + 1) * 128],
                             lnb[:, c, e::4],
                             start=(c == 0), stop=(c == NCHUNK - 1)),
                         reads=[bkey(w_gu), bkey(lnb, c)],
                         writes=[bkey(pg)], tag="gu_mm")
                ic = oc % 16
                off = ic * Te
                if oc < 16:
                    p.op("scalar",
                         lambda eng, pg=pg, off=off, Te=Te: eng.activation(
                             scr[:, off:off + Te], pg[:, 0:Te], AF.Silu),
                         reads=[bkey(pg)], writes=skey(off, Te), tag="gu_silu")
                else:
                    p.op("vector",
                         lambda eng, pg=pg, off=off, Te=Te: eng.tensor_mul(
                             scr[:, off:off + Te], scr[:, off:off + Te],
                             pg[:, 0:Te]),
                         reads=skey(off, Te) + [bkey(pg)],
                         writes=skey(off, Te), tag="gu_h")
        for half in range(2):
            dma_load(p, "gpsimd", w_gu[:, :, :],
                     w["dnT_hh"][e, half].rearrange("r (c f) -> r c f",
                                                    c=NCHUNK),
                     dst_keys=[bkey(w_gu)], tag="dn_ld")
            for j in range(8):
                dc = half * 8 + j
                pd = PS[4 + j % 2]
                for ic in range(NCHUNK):
                    off = ic * Te
                    p.op("tensor",
                         lambda eng, ic=ic, j=j, pd=pd, off=off, Te=Te:
                         eng.matmul(
                             pd[:, 0:Te],
                             w_gu[:, ic, j * 128:(j + 1) * 128],
                             scr[:, off:off + Te],
                             start=(ic == 0), stop=(ic == NCHUNK - 1)),
                         reads=[bkey(w_gu)] + skey(off, Te),
                         writes=[bkey(pd)], tag="dn_mm")
                p.op("vector",
                     lambda eng, dc=dc, e=e, pd=pd, Te=Te:
                     eng.scalar_tensor_tensor(
                         x_sb[:, dc, e::4], pd[:, 0:Te], 1.0,
                         x_sb[:, dc, e::4], ALU.mult, ALU.add),
                     reads=[bkey(pd), bkey(x_sb, ("x", dc))],
                     writes=[bkey(x_sb, ("x", dc))], tag="dn_acc")


def build_l1_ffn(T, tbase=0):
    nc = bass.Bass("TRN2", target_bir_lowering=False)
    aps = {}
    aps["x"] = nc.dram_tensor("x", [128, NCHUNK * T], F32,
                              kind="ExternalInput").ap()
    aps["guT_q"] = nc.dram_tensor("guT_q", [4, 4, 128, NCHUNK * 1024], F32,
                                  kind="ExternalInput").ap()
    aps["dnT_hh"] = nc.dram_tensor("dnT_hh", [4, 2, 128, NCHUNK * 1024], F32,
                                   kind="ExternalInput").ap()
    aps["ones_bf"] = nc.dram_tensor("ones_bf", [128, 128], BF16,
                                    kind="ExternalInput").ap()
    aps["iden_f32"] = nc.dram_tensor("iden_f32", [128, 128], F32,
                                     kind="ExternalInput").ap()
    xo = nc.dram_tensor("xo", [128, NCHUNK * T], F32,
                        kind="ExternalOutput").ap()
    with ExitStack() as ctx:
        bufs = alloc_common(nc, ctx, T)
        sb = lambda name, shape, dt=F32: ctx.enter_context(
            nc.sbuf_tensor("sb2_" + name, shape, dt))
        bufs["scr_bf"] = sb("scr_bf", [128, max(NCHUNK * 512, NCHUNK * (T // 4))], BF16)
        bufs["w_gu16"] = sb("w_gu16", [128, NCHUNK, 1024], BF16)
        p = Prog(nc)
        bufs["cp"] = ConstPool(p, nc, ctx)
        load_common(p, nc, T, bufs, aps)
        w = {"guT_q": aps["guT_q"], "dnT_hh": aps["dnT_hh"]}
        emit_l1_ffn(p, nc, T, bufs, w, tbase)
        store_x(p, nc, T, bufs, xo, 0, T)
        p.emit_all(ctx)
    return nc


def prep_l1_ffn_weights(inputs):
    gu = np.asarray(inputs["L1_ffn_gate_up_proj"])
    dn = np.asarray(inputs["L1_ffn_down_proj"])
    guT_q = gu.reshape(4, NCHUNK, 128, 4, 1024).transpose(0, 3, 2, 1, 4) \
        .reshape(4, 4, 128, NCHUNK * 1024).copy()
    dnT_hh = dn.reshape(4, NCHUNK, 128, 2, 1024).transpose(0, 3, 2, 1, 4) \
        .reshape(4, 2, 128, NCHUNK * 1024).copy()
    return {"guT_q": guT_q, "dnT_hh": dnT_hh}


def emit_rmsnorm_flat(p, nc, T, x_sb, lnb, scr, ones_bf, ps_v, var, rstd,
                      skey, cp, nw_sb, xkey="x", eps=1e-5):
    kx = lambda c: bkey(x_sb, (xkey, c))
    inv_d = 1.0 / D
    for t0, tl in split_chunks(T, 512):
        sl = slice(t0, t0 + tl)
        for c in range(NCHUNK):
            off = c * 512
            p.op("scalar",
                 lambda eng, c=c, sl=sl, tl=tl, off=off: eng.activation(
                     scr[:, off:off + tl], x_sb[:, c, sl], AF.Square),
                 reads=[kx(c)], writes=skey(off, tl), tag="rn_sq")
        for c in range(NCHUNK):
            off = c * 512
            p.op("tensor",
                 lambda eng, c=c, tl=tl, off=off: eng.matmul(
                     ps_v[:, 0:tl], ones_bf[:], scr[:, off:off + tl],
                     start=(c == 0), stop=(c == NCHUNK - 1)),
                 reads=skey(off, tl) + [bkey(ones_bf)],
                 writes=[bkey(ps_v)], tag="rn_sum")
        eps_ap, eps_k = cp.get(eps)
        p.op("vector", lambda eng, tl=tl: eng.tensor_scalar(
            var[:, 0:tl], ps_v[:, 0:tl], inv_d, None, ALU.mult),
            reads=[bkey(ps_v)], writes=[bkey(var)], tag="rn_ms")
        p.op("scalar", lambda eng, tl=tl, eps_ap=eps_ap: eng.activation(
            var[:, 0:tl], var[:, 0:tl], AF.Ln, bias=eps_ap),
            reads=[bkey(var), eps_k], writes=[bkey(var)], tag="rn_ln")
        p.op("scalar", lambda eng, tl=tl: eng.activation(
            rstd[:, 0:tl], var[:, 0:tl], AF.Exp, scale=-0.5),
            reads=[bkey(var)], writes=[bkey(rstd)], tag="rn_rstd")
        for c in range(NCHUNK):
            p.op("vector",
                 lambda eng, c=c, sl=sl, tl=tl: eng.tensor_mul(
                     lnb[:, c, sl], x_sb[:, c, sl], rstd[:, 0:tl]),
                 reads=[kx(c), bkey(rstd)], writes=[bkey(lnb, c)],
                 tag="rn_mul")
            p.op("vector",
                 lambda eng, c=c, sl=sl: eng.tensor_scalar(
                     lnb[:, c, sl], lnb[:, c, sl], nw_sb[:, c:c + 1], None,
                     ALU.mult),
                 reads=[bkey(lnb, c), bkey(nw_sb)], writes=[bkey(lnb, c)],
                 tag="rn_w")


def emit_proj_to_dram(p, nc, T, lnb, w_sb, w_hbm_ap, n_oc, out_dram, stage,
                      stagef, PS, bufs, scale=None, rope=False, rope_t0=0,
                      tag="pj", strip=512):
    wv = w_hbm_ap.rearrange("q (c f) -> q c f", c=NCHUNK)
    wg = w_sb.shape[2] // 128
    halves = split_chunks(n_oc, min(wg, max(1, n_oc // 2)))
    for h0, hl in halves:
        dma_load_c(p, "gpsimd", w_sb[:, :, 0:hl * 128],
                   wv[:, :, h0 * 128:(h0 + hl) * 128], NCHUNK,
                   dst_keys=[bkey(w_sb)], tag=tag + "_wld")
        for j in range(hl):
            oc = h0 + j
            for t0, tl in split_chunks(T, strip):
                pg = PS[4 + (j + t0 // strip) % 2]
                for c in range(NCHUNK):
                    p.op("tensor",
                         lambda eng, c=c, j=j, t0=t0, tl=tl, pg=pg: eng.matmul(
                             pg[:, 0:tl], w_sb[:, c, j * 128:(j + 1) * 128],
                             lnb[:, c, t0:t0 + tl],
                             start=(c == 0), stop=(c == NCHUNK - 1)),
                         reads=[bkey(w_sb), bkey(lnb, c)],
                         writes=[bkey(pg)], tag=tag + "_mm")
                if scale is not None:
                    p.op("scalar",
                         lambda eng, pg=pg, tl=tl: eng.activation(
                             stage[:, 0:tl], pg[:, 0:tl], AF.Copy,
                             scale=float(scale)),
                         reads=[bkey(pg)], writes=[bkey(stage)], tag=tag)
                else:
                    p.op("vector",
                         lambda eng, pg=pg, tl=tl: eng.tensor_copy(
                             stage[:, 0:tl], pg[:, 0:tl]),
                         reads=[bkey(pg)], writes=[bkey(stage)], tag=tag)
                if rope:
                    CS, SN2, swp, iden_bf = (bufs["CS"], bufs["SN2"],
                                             bufs["swp"], bufs["iden_bf"])
                    ps2 = PS[2]
                    p.op("tensor",
                         lambda eng, tl=tl, ps2=ps2: eng.matmul(
                             ps2[:, 0:tl], swp[:], stage[:, 0:tl],
                             start=True, stop=True),
                         reads=[bkey(stage), bkey(swp)], writes=[bkey(ps2)],
                         tag=tag + "_swp")
                    g0 = rope_t0 + t0
                    p.op("vector",
                         lambda eng, tl=tl, g0=g0, ps2=ps2: eng.tensor_mul(
                             stagef[:, 0:tl], ps2[:, 0:tl],
                             SN2[:, g0:g0 + tl]),
                         reads=[bkey(ps2), bkey(SN2)], writes=[bkey(stagef)],
                         tag=tag)
                    p.op("vector",
                         lambda eng, tl=tl, g0=g0: eng.tensor_mul(
                             stage[:, 0:tl], stage[:, 0:tl],
                             CS[:, g0:g0 + tl]),
                         reads=[bkey(stage), bkey(CS)], writes=[bkey(stage)],
                         tag=tag)
                    p.op("vector",
                         lambda eng, tl=tl: eng.tensor_add(
                             stage[:, 0:tl], stage[:, 0:tl], stagef[:, 0:tl]),
                         reads=[bkey(stage), bkey(stagef)],
                         writes=[bkey(stage)], tag=tag)
                dma_load(p, "sync", out_dram[oc, :, t0:t0 + tl],
                         stage[:, 0:tl], dst_keys=[("dram", out_dram.tensor.name, oc)],
                         src_keys=[bkey(stage)], tag=tag + "_st")


def emit_v_transposed_to_dram(p, nc, T, lnb, w_sb, w_hbm_ap, n_oc, vT_dram,
                              stage, PS, bufs, tag="vj"):
    wv = w_hbm_ap.rearrange("q (c f) -> q c f", c=NCHUNK)
    iden_bf = bufs["iden_bf"]
    wg = w_sb.shape[2] // 128
    halves = split_chunks(n_oc, min(wg, max(1, n_oc // 2)))
    for h0, hl in halves:
        dma_load_c(p, "gpsimd", w_sb[:, :, 0:hl * 128],
                   wv[:, :, h0 * 128:(h0 + hl) * 128], NCHUNK,
                   dst_keys=[bkey(w_sb)], tag=tag + "_wld")
        for j in range(hl):
            oc = h0 + j
            for t0, tl in split_chunks(T, 512):
                pg = PS[4 + (j + t0 // 512) % 2]
                for c in range(NCHUNK):
                    p.op("tensor",
                         lambda eng, c=c, j=j, t0=t0, tl=tl, pg=pg: eng.matmul(
                             pg[:, 0:tl], w_sb[:, c, j * 128:(j + 1) * 128],
                             lnb[:, c, t0:t0 + tl],
                             start=(c == 0), stop=(c == NCHUNK - 1)),
                         reads=[bkey(w_sb), bkey(lnb, c)],
                         writes=[bkey(pg)], tag=tag + "_mm")
                p.op("vector",
                     lambda eng, pg=pg, tl=tl: eng.tensor_copy(
                         stage[:, 0:tl], pg[:, 0:tl]),
                     reads=[bkey(pg)], writes=[bkey(stage)], tag=tag)
                PT2 = bufs["PT2"]
                for kt in range(tl // 128):
                    ptb = PT2[kt % 2]
                    p.op("tensor",
                         lambda eng, kt=kt, ptb=ptb: eng.transpose(
                             ptb[:, 0:128],
                             stage[:, kt * 128:(kt + 1) * 128], iden_bf[:]),
                         reads=[bkey(stage), bkey(iden_bf)],
                         writes=[bkey(ptb)], tag=tag + "_tr")
                    p.op("vector",
                         lambda eng, kt=kt, ptb=ptb: eng.tensor_copy(
                             stage[:, 2048 + kt * 128:2048 + (kt + 1) * 128],
                             ptb[:, 0:128]),
                         reads=[bkey(ptb)], writes=[bkey(stage, "T")],
                         tag=tag)
                for kt in range(tl // 128):
                    dma_load(p, "sync",
                             vT_dram[oc, t0 // 128 + kt, :, :],
                             stage[:, 2048 + kt * 128:2048 + (kt + 1) * 128],
                             dst_keys=[("dram", vT_dram.tensor.name, oc)],
                             src_keys=[bkey(stage, "T")], tag=tag + "_st")


def emit_attn_phase(p, nc, cfg, bufs, qd, kd, vTd, o_fb, o_off):
    nq, nkv, T, q0, q1 = (cfg["nq"], cfg["nkv"], cfg["T"], cfg["q0"],
                          cfg["q1"])
    Tq = q1 - q0
    PS = bufs["PS"]
    kh = bufs["kh"]
    vh = bufs["vh"]
    qh = bufs["qh"]
    Srow = bufs["Srow"]
    Pb = bufs["Pb"]
    PbT = bufs["PbT"]
    m1 = bufs["am1"]; rs = bufs["ars"]; negm = bufs["anegm"]
    ltri = bufs["ltri"]
    iden_bf = bufs["iden_bf"]
    grp = nq // nkv
    for h in range(nq):
        kv = h // grp
        if h % grp == 0:
            dma_load(p, "sync", kh[:, 0:T], kd[kv, :, 0:T],
                     dst_keys=[bkey(kh)], src_keys=[("dram", kd.tensor.name, kv)],
                     tag="at_kld")
            for kt in range(T // 128):
                dma_load(p, "sync", vh[:, kt * 128:(kt + 1) * 128],
                         vTd[kv, kt, :, :],
                         dst_keys=[bkey(vh)],
                         src_keys=[("dram", vTd.tensor.name, kv)],
                         tag="at_vld")
        for qt in range(Tq // 128):
            gq = q0 + qt * 128
            kend = gq + 128
            nk = kend // 128
            dma_load(p, "sync", qh[:, 0:128], qd[h, :, qt * 128:(qt + 1) * 128],
                     dst_keys=[bkey(qh)], src_keys=[("dram", qd.tensor.name, h)],
                     tag="at_qld")
            for s0, sl2 in split_chunks(kend, 512):
                pg = PS[4 + (s0 // 512) % 2]
                p.op("tensor",
                     lambda eng, s0=s0, sl2=sl2, pg=pg: eng.matmul(
                         pg[:, 0:sl2], qh[:, 0:128], kh[:, s0:s0 + sl2],
                         start=True, stop=True),
                     reads=[bkey(qh), bkey(kh)], writes=[bkey(pg)],
                     tag="at_smm")
                p.op("vector",
                     lambda eng, s0=s0, sl2=sl2, pg=pg: eng.tensor_copy(
                         Srow[:, s0:s0 + sl2], pg[:, 0:sl2]),
                     reads=[bkey(pg)], writes=[bkey(Srow)], tag="at_sc")
            p.op("vector",
                 lambda eng, kend=kend: eng.tensor_add(
                     Srow[:, kend - 128:kend], Srow[:, kend - 128:kend],
                     ltri[:, 0:128]),
                 reads=[bkey(Srow), bkey(ltri)], writes=[bkey(Srow)],
                 tag="at_mask")
            p.op("vector",
                 lambda eng, kend=kend: eng.reduce_max(
                     m1[:, 0:1], Srow[:, 0:kend], axis=AX.X),
                 reads=[bkey(Srow)], writes=[bkey(m1)], tag="at_max")
            p.op("vector", lambda eng: eng.tensor_scalar(
                negm[:, 0:1], m1[:, 0:1], -1.0, None, ALU.mult),
                reads=[bkey(m1)], writes=[bkey(negm)], tag="at_neg")
            p.op("scalar",
                 lambda eng, kend=kend: eng.activation(
                     Pb[:, 0:kend], Srow[:, 0:kend], AF.Exp,
                     bias=negm[:, 0:1], accum_out=rs[:, 0:1]),
                 reads=[bkey(Srow), bkey(negm)],
                 writes=[bkey(Pb), bkey(rs)], tag="at_exp")
            p.op("vector", lambda eng: eng.reciprocal(rs[:, 0:1], rs[:, 0:1]),
                 reads=[bkey(rs)], writes=[bkey(rs)], tag="at_rr")
            p.op("vector",
                 lambda eng, kend=kend: eng.tensor_scalar(
                     Pb[:, 0:kend], Pb[:, 0:kend], rs[:, 0:1], None, ALU.mult),
                 reads=[bkey(Pb), bkey(rs)], writes=[bkey(Pb)], tag="at_pn")
            po = PS[3]
            PT2 = bufs["PT2"]
            for kt in range(nk):
                ptb = PT2[kt % 2]
                p.op("tensor",
                     lambda eng, kt=kt, ptb=ptb: eng.transpose(
                         ptb[:, 0:128], Pb[:, kt * 128:(kt + 1) * 128],
                         iden_bf[:]),
                     reads=[bkey(Pb), bkey(iden_bf)], writes=[bkey(ptb)],
                     tag="at_ptr")
                p.op("vector",
                     lambda eng, kt=kt, ptb=ptb: eng.tensor_copy(
                         PbT[:, kt * 128:(kt + 1) * 128],
                         ptb[:, 0:128]),
                     reads=[bkey(ptb)], writes=[bkey(PbT, kt)],
                     tag="at_ptc")
            for kt in range(nk):
                p.op("tensor",
                     lambda eng, kt=kt, po=po, nk=nk: eng.matmul(
                         po[:, 0:128], vh[:, kt * 128:(kt + 1) * 128],
                         PbT[:, kt * 128:(kt + 1) * 128],
                         start=(kt == 0), stop=(kt == nk - 1)),
                     reads=[bkey(vh), bkey(PbT, kt)], writes=[bkey(po)],
                     tag="at_av")
            off = o_off + h * Tq + qt * 128
            p.op("vector",
                 lambda eng, off=off, po=po: eng.tensor_copy(
                     o_fb[:, off:off + 128], po[:, 0:128]),
                 reads=[bkey(po)], writes=[bkey(o_fb, off // 512)],
                 tag="at_oc")


def emit_out_proj(p, nc, cfg, bufs, woT_ap, o_fb, o_off, xin_ap, xout_ap,
                  has_bias=None, res_key="OUTX", st_sz=256):
    T_q = cfg["q1"] - cfg["q0"]
    PS = bufs["PS"]; w_sb = bufs["w_sb"]; xtmp = bufs["xtmp"]
    wv = woT_ap.rearrange("q (c f) -> q c f", c=NCHUNK)
    for t0, tl in split_chunks(T_q, st_sz):
        dma_load_c(p, "sync", xtmp[:, :, 0:tl], xin_ap[:, :, t0:t0 + tl],
                   NCHUNK,
                   dst_keys=[bkey(xtmp, ("x", c)) for c in range(NCHUNK)],
                   tag="wo_xld")
        for wh in range(2):
            dma_load_c(p, "gpsimd", w_sb[:, :, 0:1024],
                       wv[:, :, wh * 1024:(wh + 1) * 1024], NCHUNK,
                       dst_keys=[bkey(w_sb)], tag="wo_ld")
            for j in range(8):
                oc = wh * 8 + j
                pg = PS[4 + j % 2]
                for c in range(NCHUNK):
                    off = o_off + c * T_q + t0
                    p.op("tensor",
                         lambda eng, c=c, j=j, tl=tl, pg=pg, off=off:
                         eng.matmul(
                             pg[:, 0:tl], w_sb[:, c, j * 128:(j + 1) * 128],
                             o_fb[:, off:off + tl],
                             start=(c == 0), stop=(c == NCHUNK - 1)),
                         reads=[bkey(w_sb), bkey(o_fb, off // 512),
                                bkey(o_fb, (off + tl - 1) // 512)],
                         writes=[bkey(pg)], tag="wo_mm")
                p.op("vector",
                     lambda eng, oc=oc, tl=tl, pg=pg: eng.scalar_tensor_tensor(
                         xtmp[:, oc, 0:tl], pg[:, 0:tl], 1.0,
                         xtmp[:, oc, 0:tl], ALU.mult, ALU.add),
                     reads=[bkey(pg), bkey(xtmp, ("x", oc))],
                     writes=[bkey(xtmp, ("x", oc))], tag="wo_acc")
        dma_load_c(p, "sync", xout_ap[:, :, t0:t0 + tl], xtmp[:, :, 0:tl],
                   NCHUNK, dst_keys=[(res_key, t0)],
                   src_keys=[bkey(xtmp, ("x", c)) for c in range(NCHUNK)],
                   tag="wo_st")


def alloc_attn_bufs(nc, ctx, T, Tq, rope=False):
    sb = lambda name, shape, dt=F32: ctx.enter_context(
        nc.sbuf_tensor("sa_" + name, shape, dt))
    bufs = {}
    bufs["lnb"] = sb("lnb", [128, NCHUNK, T], BF16)
    bufs["scr_bf"] = sb("scr_bf", [128, NCHUNK * 512], BF16)
    bufs["o_fb"] = sb("o_fb", [128, 16 * Tq], BF16)
    bufs["w_sb"] = sb("w_sb", [128, NCHUNK, 1024], BF16)
    bufs["xtmp"] = sb("xtmp", [128, NCHUNK, 128 if rope else 256])
    bufs["kh"] = sb("kh", [128, T], BF16)
    bufs["vh"] = sb("vh", [128, T], BF16)
    bufs["qh"] = sb("qh", [128, 128], BF16)
    bufs["Srow"] = sb("Srow", [128, T])
    bufs["Pb"] = sb("Pb", [128, T], BF16)
    bufs["PbT"] = sb("PbT", [128, 2048], BF16)
    bufs["am1"] = sb("am1", [128, 1])
    bufs["ars"] = sb("ars", [128, 1])
    bufs["anegm"] = sb("anegm", [128, 1])
    bufs["ltri"] = sb("ltri", [128, 128])
    bufs["iden_bf"] = sb("iden_bf", [128, 128], BF16)
    bufs["ones_bf"] = sb("ones_bf", [128, 128], BF16)
    bufs["stage"] = sb("stage", [128, 2560], BF16)
    bufs["stagef"] = sb("stagef", [128, 512])
    bufs["var"] = sb("var", [128, 512])
    bufs["rstd"] = sb("rstd", [128, 512])
    if rope:
        bufs["CS"] = sb("CS", [128, T], BF16)
        bufs["SN2"] = sb("SN2", [128, T], BF16)
        bufs["swp"] = sb("swp", [128, 128], BF16)
    else:
        bufs["mea"] = sb("mea", [128, 512])
        bufs["tmpf"] = sb("tmpf", [128, 512])
    bufs["PS"] = [ctx.enter_context(nc.psum_tensor(f"ps{i}", [128, 512], F32))
                  for i in range(6)]
    bufs["PT2"] = [ctx.enter_context(nc.psum_tensor(f"pst{i}", [128, 512],
                                                    BF16)) for i in range(2)]
    return bufs


def build_attn_l1(T, q0, q1, upto="all"):
    nc = bass.Bass("TRN2", target_bir_lowering=False)
    Tq = q1 - q0
    aps = {}
    aps["x"] = nc.dram_tensor("x", [128, NCHUNK * T], F32,
                              kind="ExternalInput").ap()
    for wn in ("wqT", "wkT", "wvT", "woT"):
        aps[wn] = nc.dram_tensor(wn, [128, NCHUNK * 2048], F32,
                                 kind="ExternalInput").ap()
    aps["ones_bf"] = nc.dram_tensor("ones_bf", [128, 128], BF16,
                                    kind="ExternalInput").ap()
    aps["iden_bf"] = nc.dram_tensor("iden_bf", [128, 128], BF16,
                                    kind="ExternalInput").ap()
    aps["ltri"] = nc.dram_tensor("ltri", [128, 128], F32,
                                 kind="ExternalInput").ap()
    xo = nc.dram_tensor("xo", [128, NCHUNK * Tq], F32,
                        kind="ExternalOutput").ap()
    qd = nc.dram_tensor("qd", [16, 128, Tq], BF16, kind="Internal").ap()
    kd = nc.dram_tensor("kd", [16, 128, T], BF16, kind="Internal").ap()
    vTd = nc.dram_tensor("vTd", [16, T // 128, 128, 128], BF16,
                         kind="Internal").ap()
    cfg = {"nq": 16, "nkv": 16, "T": T, "q0": q0, "q1": q1}
    with ExitStack() as ctx:
        bufs = alloc_attn_bufs(nc, ctx, T, Tq)
        p = Prog(nc)
        cp = ConstPool(p, nc, ctx)
        bufs["cp"] = cp
        for n, b in (("ones_bf", "ones_bf"), ("iden_bf", "iden_bf"),
                     ("ltri", "ltri")):
            dma_load(p, "sync", bufs[b][:], aps[n], dst_keys=[bkey(bufs[b])],
                     tag="c_ld")
        skey = lambda off, ln: [bkey(bufs["scr_bf"], i) for i in
                                range(off // 512, (off + ln + 511) // 512)]
        xv = aps["x"].rearrange("q (c t) -> q c t", c=NCHUNK)
        for t0, tl in split_chunks(T, 256):
            dma_load_c(p, "sync", bufs["xtmp"][:, :, 0:tl],
                       xv[:, :, t0:t0 + tl], NCHUNK,
                       dst_keys=[bkey(bufs["xtmp"], ("x", c))
                                 for c in range(NCHUNK)],
                       tag="x_ld")
            emit_layernorm_flat(p, nc, tl, bufs["xtmp"],
                                LnbView(bufs["lnb"], t0), bufs["scr_bf"],
                                bufs["ones_bf"], bufs["PS"][0], bufs["PS"][1],
                                bufs["mea"], bufs["var"], bufs["rstd"],
                                bufs["tmpf"], skey, cp=cp)
        if upto in ("proj", "attn", "all", "q"):
            emit_proj_to_dram(p, nc, Tq, SliceLnb(bufs["lnb"], q0),
                              bufs["w_sb"], aps["wqT"], 16, qd, bufs["stage"],
                              bufs["stagef"], bufs["PS"], bufs,
                              scale=1.0 / np.sqrt(128.0), tag="q")
        if upto in ("proj", "attn", "all", "k"):
            emit_proj_to_dram(p, nc, T, bufs["lnb"], bufs["w_sb"], aps["wkT"],
                              16, kd, bufs["stage"], bufs["stagef"],
                              bufs["PS"], bufs, tag="k")
        if upto in ("proj", "attn", "all", "v"):
            emit_v_transposed_to_dram(p, nc, T, bufs["lnb"], bufs["w_sb"],
                                      aps["wvT"], 16, vTd, bufs["stage"],
                                      bufs["PS"], bufs, tag="v")
        if upto in ("attn", "all"):
            emit_attn_phase(p, nc, cfg, bufs, qd, kd, vTd, bufs["o_fb"], 0)
        if upto == "all":
            xin = aps["x"].rearrange("q (c t) -> q c t", c=NCHUNK)[:, :, q0:q1]
            xout = xo.rearrange("q (c t) -> q c t", c=NCHUNK)
            emit_out_proj(p, nc, cfg, bufs, aps["woT"], bufs["o_fb"], 0,
                          xin, xout)
        else:
            for c in range(NCHUNK):
                p.op("vector", lambda eng, c=c: eng.memset(
                    bufs["xtmp"][:, c, 0:256], 0.0),
                    writes=[bkey(bufs["xtmp"], ("x", c))], tag="z")
            for t0, tl in split_chunks(Tq, 256):
                dma_load_c(p, "sync",
                           xo.rearrange("q (c t) -> q c t", c=NCHUNK)[
                               :, :, t0:t0 + tl],
                           bufs["xtmp"][:, :, 0:tl], NCHUNK,
                           dst_keys=[("OUT", t0)],
                           src_keys=[bkey(bufs["xtmp"], ("x", c))
                                     for c in range(NCHUNK)], tag="z_st")
        p.emit_all(ctx)
    return nc


class LnbView:

    def __init__(self, h, t0):
        self.h = h
        self.t0 = t0
        self.name = h.name

    def __getitem__(self, idx):
        q, c, sl = idx
        return self.h[q, c, slice(sl.start + self.t0, sl.stop + self.t0)]


class SliceLnb:

    def __init__(self, h, q0):
        self.h = h
        self.q0 = q0
        self.name = h.name

    def __getitem__(self, idx):
        q, c, sl = idx
        return self.h[q, c, slice(sl.start + self.q0, sl.stop + self.q0)]


def prep_attn_l1_weights(inputs):
    out = {}
    for src, dst in (("L1_attn_b_Wq", "wqT"), ("L1_attn_b_Wk", "wkT"),
                     ("L1_attn_b_Wv", "wvT"), ("L1_attn_b_Wo", "woT")):
        w = np.asarray(inputs[src])
        wT = np.ascontiguousarray(w.T)
        out[dst] = wT.reshape(NCHUNK, 128, 2048).transpose(1, 0, 2).reshape(
            128, NCHUNK * 2048).copy()
    return out


def ltri_mask():
    m = np.zeros((128, 128), np.float32)
    m[np.triu_indices(128, 1)] = -1e30
    return m


def build_attn_l0(T, q0, q1, upto="all"):
    nc = bass.Bass("TRN2", target_bir_lowering=False)
    Tq = q1 - q0
    aps = {}
    aps["x"] = nc.dram_tensor("x", [128, NCHUNK * T], F32,
                              kind="ExternalInput").ap()
    aps["wqT"] = nc.dram_tensor("wqT", [128, NCHUNK * 2048], F32,
                                kind="ExternalInput").ap()
    aps["wkT"] = nc.dram_tensor("wkT", [128, NCHUNK * 512], F32,
                                kind="ExternalInput").ap()
    aps["wvT"] = nc.dram_tensor("wvT", [128, NCHUNK * 512], F32,
                                kind="ExternalInput").ap()
    aps["woT"] = nc.dram_tensor("woT", [128, NCHUNK * 2048], F32,
                                kind="ExternalInput").ap()
    aps["anw"] = nc.dram_tensor("anw", [128, NCHUNK], F32,
                                kind="ExternalInput").ap()
    aps["CS"] = nc.dram_tensor("CS", [128, T], BF16,
                               kind="ExternalInput").ap()
    aps["SN2"] = nc.dram_tensor("SN2", [128, T], BF16,
                                kind="ExternalInput").ap()
    aps["swp"] = nc.dram_tensor("swp", [128, 128], BF16,
                                kind="ExternalInput").ap()
    aps["ones_bf"] = nc.dram_tensor("ones_bf", [128, 128], BF16,
                                    kind="ExternalInput").ap()
    aps["iden_bf"] = nc.dram_tensor("iden_bf", [128, 128], BF16,
                                    kind="ExternalInput").ap()
    aps["ltri"] = nc.dram_tensor("ltri", [128, 128], F32,
                                 kind="ExternalInput").ap()
    xo = nc.dram_tensor("xo", [128, NCHUNK * Tq], F32,
                        kind="ExternalOutput").ap()
    qd = nc.dram_tensor("qd", [16, 128, Tq], BF16, kind="Internal").ap()
    kd = nc.dram_tensor("kd", [4, 128, T], BF16, kind="Internal").ap()
    vTd = nc.dram_tensor("vTd", [4, T // 128, 128, 128], BF16,
                         kind="Internal").ap()
    lnfd = nc.dram_tensor("lnfd", [128, NCHUNK * T], F32,
                          kind="Internal").ap()
    cfg = {"nq": 16, "nkv": 4, "T": T, "q0": q0, "q1": q1}
    with ExitStack() as ctx:
        bufs = alloc_attn_bufs(nc, ctx, T, Tq, rope=True)
        sb = lambda name, shape, dt=F32: ctx.enter_context(
            nc.sbuf_tensor("sl0_" + name, shape, dt))
        bufs["anw"] = sb("anw", [128, NCHUNK])
        bufs["lnf"] = sb("lnf", [128, NCHUNK, 128])
        bufs["mea"] = sb("mea", [128, 256])
        bufs["tmpf"] = sb("tmpf", [128, 256])
        p = Prog(nc)
        cp = ConstPool(p, nc, ctx)
        bufs["cp"] = cp
        for n in ("ones_bf", "iden_bf", "ltri", "CS", "SN2", "swp", "anw"):
            dma_load(p, "sync", bufs[n][:], aps[n], dst_keys=[bkey(bufs[n])],
                     tag="c_ld")
        skey = lambda off, ln: [bkey(bufs["scr_bf"], i) for i in
                                range(off // 512, (off + ln + 511) // 512)]
        xv = aps["x"].rearrange("q (c t) -> q c t", c=NCHUNK)
        lnfv = lnfd.rearrange("q (c t) -> q c t", c=NCHUNK)
        for t0, tl in split_chunks(T, 128):
            dma_load_c(p, "sync", bufs["xtmp"][:, :, 0:tl],
                       xv[:, :, t0:t0 + tl], NCHUNK,
                       dst_keys=[bkey(bufs["xtmp"], ("x", c))
                                 for c in range(NCHUNK)],
                       tag="x_ld")
            emit_layernorm_flat(p, nc, tl, bufs["xtmp"],
                                LnbView(bufs["lnb"], t0), bufs["scr_bf"],
                                bufs["ones_bf"], bufs["PS"][0], bufs["PS"][1],
                                bufs["mea"], bufs["var"], bufs["rstd"],
                                bufs["tmpf"], skey, cp=cp,
                                lnf=bufs["lnf"])
            dma_load_c(p, "sync", lnfv[:, :, t0:t0 + tl],
                       bufs["lnf"][:, :, 0:tl], NCHUNK,
                       dst_keys=[("dram", "lnfd", t0)],
                       src_keys=[bkey(bufs["lnf"], ("x", c))
                                 for c in range(NCHUNK)], tag="lnf_st")
            emit_rmsnorm_flat(p, nc, tl, bufs["lnf"],
                              LnbView(bufs["lnb"], t0), bufs["scr_bf"],
                              bufs["ones_bf"], bufs["PS"][1], bufs["var"],
                              bufs["rstd"], skey, cp, bufs["anw"])
        if upto in ("proj", "attn", "all"):
            emit_proj_to_dram(p, nc, Tq, SliceLnb(bufs["lnb"], q0),
                              bufs["w_sb"], aps["wqT"], 16, qd, bufs["stage"],
                              bufs["stagef"], bufs["PS"], bufs,
                              scale=1.0 / np.sqrt(128.0), rope=True,
                              rope_t0=q0, tag="q")
            emit_proj_to_dram(p, nc, T, bufs["lnb"], bufs["w_sb"], aps["wkT"],
                              4, kd, bufs["stage"], bufs["stagef"],
                              bufs["PS"], bufs, rope=True, rope_t0=0, tag="k")
            emit_v_transposed_to_dram(p, nc, T, bufs["lnb"], bufs["w_sb"],
                                      aps["wvT"], 4, vTd, bufs["stage"],
                                      bufs["PS"], bufs, tag="v")
        if upto in ("attn", "all"):
            emit_attn_phase(p, nc, cfg, bufs, qd, kd, vTd, bufs["o_fb"], 0)
        if upto == "all":
            xin = lnfv[:, :, q0:q1]
            xout = xo.rearrange("q (c t) -> q c t", c=NCHUNK)
            emit_out_proj(p, nc, cfg, bufs, aps["woT"], bufs["o_fb"], 0,
                          xin, xout, st_sz=128)
        p.emit_all(ctx)
    return nc


F_AB = 1024
NF_AB = 8


def build_moe_attnb(T):
    nc = bass.Bass("TRN2", target_bir_lowering=False)
    aps = {}
    aps["x"] = nc.dram_tensor("x", [128, NCHUNK * T], F32,
                              kind="ExternalInput").ap()
    aps["gpT_h"] = nc.dram_tensor("gpT_h", [8, 128, NCHUNK * F_AB], F32,
                                  kind="ExternalInput").ap()
    aps["upT_h"] = nc.dram_tensor("upT_h", [8, 128, NCHUNK * F_AB], F32,
                                  kind="ExternalInput").ap()
    aps["dnT_h"] = nc.dram_tensor("dnT_h", [8, 2, 128, NF_AB * 1024], F32,
                                  kind="ExternalInput").ap()
    aps["gwT_h"] = nc.dram_tensor("gwT_h", [128, NCHUNK * 8], F32,
                                  kind="ExternalInput").ap()
    aps["mnw"] = nc.dram_tensor("mnw", [128, NCHUNK], F32,
                                kind="ExternalInput").ap()
    aps["xres"] = nc.dram_tensor("xres", [128, NCHUNK * T], F32,
                                 kind="ExternalInput").ap()
    aps["ones_bf"] = nc.dram_tensor("ones_bf", [128, 128], BF16,
                                    kind="ExternalInput").ap()
    aps["iden_f32"] = nc.dram_tensor("iden_f32", [128, 128], F32,
                                     kind="ExternalInput").ap()
    aps["ind8"] = nc.dram_tensor("ind8", [8, 8 * 128], F32,
                                 kind="ExternalInput").ap()
    xo = nc.dram_tensor("xo", [128, NCHUNK * T], F32,
                        kind="ExternalOutput").ap()
    with ExitStack() as ctx:
        bufs = alloc_common(nc, ctx, T, want=("router",))
        sb = lambda name, shape, dt=F32: ctx.enter_context(
            nc.sbuf_tensor("sm_" + name, shape, dt))
        bufs["scr_bf"] = sb("scr_bf", [128, max(NF_AB * T, NCHUNK * 512)],
                            BF16)
        bufs["w_gu"] = sb("w_gu", [128, NCHUNK, F_AB], BF16)
        bufs["w_dn"] = sb("w_dn", [128, NF_AB, 1024], BF16)
        bufs["mnw"] = sb("mnw", [128, NCHUNK])
        p = Prog(nc)
        bufs["cp"] = ConstPool(p, nc, ctx)
        load_common(p, nc, T, bufs, aps, want=("router",))
        dma_load(p, "sync", bufs["mnw"][:], aps["mnw"],
                 dst_keys=[bkey(bufs["mnw"])], tag="c_ld")
        w = {"gpT": [aps["gpT_h"][e] for e in range(8)],
             "upT": [aps["upT_h"][e] for e in range(8)],
             "dnT_h": aps["dnT_h"]}
        bufs["xres_t"] = sb("xres_t", [128, NCHUNK, 256])
        emit_moe(p, nc, T, bufs, w, NF=NF_AB, norm="rms", nw_sb=bufs["mnw"],
                 xres_ap=aps["xres"].rearrange("q (c t) -> q c t", c=NCHUNK),
                 xres_t=bufs["xres_t"])
        store_x(p, nc, T, bufs, xo, 0, T)
        p.emit_all(ctx)
    return nc


def deinterleave_cols(w2d, n_heads, dh=128):
    D_in, out = w2d.shape
    w = w2d.reshape(D_in, n_heads, dh)
    perm = np.concatenate([np.arange(0, dh, 2), np.arange(1, dh, 2)])
    return np.ascontiguousarray(w[:, :, perm].reshape(D_in, out))


def chunk128(wT):
    D_in, out = wT.shape
    return np.ascontiguousarray(
        wT.reshape(NCHUNK, 128, out).transpose(1, 0, 2).reshape(
            128, NCHUNK * out))


def rope_tables(T):
    half = 64
    inv = 1.0 / (10000.0 ** (np.arange(half, dtype=np.float64) / half))
    ang = np.arange(T, dtype=np.float64)[:, None] * inv[None, :]
    cos = np.cos(ang).T
    sin = np.sin(ang).T
    CS = np.concatenate([cos, cos], axis=0).astype(bf)
    SN2 = np.concatenate([-sin, sin], axis=0).astype(bf)
    swp = np.zeros((128, 128), np.float32)
    for pcol in range(128):
        swp[(pcol + 64) % 128, pcol] = 1.0
    return CS, SN2, swp.astype(bf)


def prep_attn_l0_weights(inputs):
    out = {}
    wq = np.asarray(inputs["L0_attn_b_w_q"])
    wk = np.asarray(inputs["L0_attn_b_w_k"])
    wv = np.asarray(inputs["L0_attn_b_w_v"])
    wo = np.asarray(inputs["L0_attn_b_w_o"])
    out["wqT"] = chunk128(deinterleave_cols(wq, 16))
    out["wkT"] = chunk128(deinterleave_cols(wk, 4))
    out["wvT"] = chunk128(wv)
    out["woT"] = chunk128(wo)
    anw = np.asarray(inputs["L0_attn_b_attn_norm_w"]).reshape(NCHUNK, 128).T
    out["anw"] = np.ascontiguousarray(anw)
    return out


def prep_moe_attnb_weights(inputs):
    w1 = np.asarray(inputs["L0_attn_b_w1"])
    w3 = np.asarray(inputs["L0_attn_b_w3"])
    w2 = np.asarray(inputs["L0_attn_b_w2"])
    gw = np.asarray(inputs["L0_attn_b_gate_w"])
    gpT_h = w1.reshape(8, NCHUNK, 128, F_AB).transpose(0, 2, 1, 3).reshape(
        8, 128, NCHUNK * F_AB).copy()
    upT_h = w3.reshape(8, NCHUNK, 128, F_AB).transpose(0, 2, 1, 3).reshape(
        8, 128, NCHUNK * F_AB).copy()
    w2_4d = w2.reshape(8, NF_AB, 128, 2048)
    dnT_h = np.stack(
        [w2_4d[:, :, :, 0:1024], w2_4d[:, :, :, 1024:2048]], axis=1
    ).transpose(0, 1, 3, 2, 4).reshape(8, 2, 128, NF_AB * 1024).copy()
    gwT_h = gw.reshape(NCHUNK, 128, 8).transpose(1, 0, 2).reshape(
        128, NCHUNK * 8).copy()
    mnw = np.asarray(inputs["L0_attn_b_moe_norm_w"]).reshape(NCHUNK, 128).T
    return {"gpT_h": gpT_h, "upT_h": upT_h, "dnT_h": dnT_h, "gwT_h": gwT_h,
            "mnw": np.ascontiguousarray(mnw)}


def build_rglru(T, q0, q1):
    nc = bass.Bass("TRN2", target_bir_lowering=False)
    Tq = q1 - q0
    aps = {}
    aps["x"] = nc.dram_tensor("x", [128, NCHUNK * T], F32,
                              kind="ExternalInput").ap()
    aps["winT"] = nc.dram_tensor("winT", [128, NCHUNK * 2048], F32,
                                 kind="ExternalInput").ap()
    aps["woutT"] = nc.dram_tensor("woutT", [128, NCHUNK * 2048], F32,
                                  kind="ExternalInput").ap()
    aps["wg"] = nc.dram_tensor("wg", [128, 2 * 2048], F32,
                               kind="ExternalInput").ap()
    aps["pv"] = nc.dram_tensor("pv", [128, 64], F32,
                               kind="ExternalInput").ap()
    aps["ones_bf"] = nc.dram_tensor("ones_bf", [128, 128], BF16,
                                    kind="ExternalInput").ap()
    xo = nc.dram_tensor("xo", [128, NCHUNK * Tq], F32,
                        kind="ExternalOutput").ap()
    S = 512
    nstrip = T // S
    with ExitStack() as ctx:
        sb = lambda name, shape, dt=F32: ctx.enter_context(
            nc.sbuf_tensor("sg_" + name, shape, dt))
        xtmp = sb("xtmp", [128, NCHUNK, S])
        lnbS = sb("lnbS", [128, NCHUNK, S], BF16)
        scrq = sb("scrq", [128, NCHUNK * 512], BF16)
        xh = sb("xh", [128, NCHUNK, S], BF16)
        hS = sb("hS", [128, NCHUNK, S], BF16)
        w_sb = sb("w_sb", [128, NCHUNK, 1024], BF16)
        wg_sb = sb("wg_sb", [128, 2 * 2048], BF16)
        pv_sb = sb("pv", [128, 64])
        m8sp = sb("m8sp", [128, 8])
        carry = sb("carry", [128, 24])
        zer = sb("zer", [128, S])
        SC = sb("SC", [128, 12, S])
        ki = sb("ki", [128, S], I32)
        ones_bf = sb("ones_bf", [128, 128], BF16)
        mea = sb("mea", [128, 256]); var = sb("var", [128, 256])
        rstd = sb("rstd", [128, 256]); tmpf = sb("tmpf", [128, 256])
        PS = [ctx.enter_context(nc.psum_tensor(f"ps{i}", [128, 512], F32))
              for i in range(8)]
        p = Prog(nc)
        cp = ConstPool(p, nc, ctx)
        dma_load(p, "sync", ones_bf[:], aps["ones_bf"],
                 dst_keys=[bkey(ones_bf)], tag="c_ld")
        dma_load(p, "sync", pv_sb[:], aps["pv"], dst_keys=[bkey(pv_sb)],
                 tag="c_ld")
        dma_load(p, "gpsimd", wg_sb[:], aps["wg"], dst_keys=[bkey(wg_sb)],
                 tag="c_ld")
        p.op("vector", lambda eng: eng.memset(zer[:], 0.0),
             writes=[bkey(zer)])
        p.op("vector", lambda eng: eng.memset(carry[:], 0.0),
             writes=[bkey(carry, "ph"), bkey(carry, "gr"),
                     bkey(carry, "gi")])
        arp = pv_sb[:, 48:56]
        one_ap, one_k = cp.get(1.0)
        p.op("scalar", lambda eng: eng.activation(SC[:, 0, 0:8], arp,
                                                  AF.Abs),
             reads=[bkey(pv_sb)], writes=[bkey(SC, 0)], tag="sp")
        p.op("scalar", lambda eng: eng.activation(SC[:, 0, 8:16],
                                                  SC[:, 0, 0:8], AF.Exp,
                                                  scale=-1.0),
             reads=[bkey(SC, 0)], writes=[bkey(SC, 0)], tag="sp")
        p.op("scalar", lambda eng, one_ap=one_ap: eng.activation(
            SC[:, 0, 8:16], SC[:, 0, 8:16], AF.Ln, bias=one_ap),
            reads=[bkey(SC, 0), one_k], writes=[bkey(SC, 0)], tag="sp")
        p.op("vector", lambda eng: eng.tensor_relu(SC[:, 0, 16:24], arp),
             reads=[bkey(pv_sb)], writes=[bkey(SC, 0)], tag="sp")
        p.op("vector", lambda eng: eng.tensor_add(m8sp[:, 0:8],
                                                  SC[:, 0, 8:16],
                                                  SC[:, 0, 16:24]),
             reads=[bkey(SC, 0)], writes=[bkey(m8sp)], tag="sp")
        p.op("vector", lambda eng: eng.tensor_scalar(m8sp[:, 0:8],
                                                     m8sp[:, 0:8], -8.0,
                                                     None, ALU.mult),
             reads=[bkey(m8sp)], writes=[bkey(m8sp)], tag="sp")

        xv = aps["x"].rearrange("q (c t) -> q c t", c=NCHUNK)
        xov = xo.rearrange("q (c t) -> q c t", c=NCHUNK)
        wiv = aps["winT"].rearrange("q (c f) -> q c f", c=NCHUNK)
        wov = aps["woutT"].rearrange("q (c f) -> q c f", c=NCHUNK)
        skey = lambda off, ln: [bkey(scrq, i) for i in
                                range(off // 512, (off + ln + 511) // 512)]
        inv2pi = 1.0 / (2 * np.pi)
        for st in range(nstrip):
            t0 = st * S
            own = (t0 >= q0) and (t0 < q1)
            for u0, ul in split_chunks(S, 256):
                dma_load_c(p, "sync", xtmp[:, :, u0:u0 + ul],
                           xv[:, :, t0 + u0:t0 + u0 + ul], NCHUNK,
                           dst_keys=[bkey(xtmp, ("x", c))
                                     for c in range(NCHUNK)], tag="x_ld")
            for u0, ul in split_chunks(S, 256):
                emit_layernorm_flat(p, nc, ul, XOffView(xtmp, u0),
                                    LnbView(lnbS, u0), scrq, ones_bf,
                                    PS[0], PS[1], mea, var, rstd, tmpf,
                                    skey, cp=cp)
            for wh in range(2):
                dma_load_c(p, "gpsimd", w_sb[:, :, 0:1024],
                           wiv[:, :, wh * 1024:(wh + 1) * 1024], NCHUNK,
                           dst_keys=[bkey(w_sb)], tag="wi_ld")
                for j in range(8):
                    oc = wh * 8 + j
                    pg = PS[4 + j % 2]
                    for c in range(NCHUNK):
                        p.op("tensor",
                             lambda eng, c=c, j=j, pg=pg: eng.matmul(
                                 pg[:, 0:S], w_sb[:, c, j * 128:(j + 1) * 128],
                                 lnbS[:, c, 0:S],
                                 start=(c == 0), stop=(c == NCHUNK - 1)),
                             reads=[bkey(w_sb), bkey(lnbS, c)],
                             writes=[bkey(pg)], tag="wi_mm")
                    p.op("vector",
                         lambda eng, oc=oc, pg=pg: eng.tensor_scalar(
                             xh[:, oc, 0:S], pg[:, 0:S],
                             pv_sb[:, oc:oc + 1], None, ALU.add),
                         reads=[bkey(pg), bkey(pv_sb)],
                         writes=[bkey(xh, oc)], tag="wi_ev")
            for j in range(8):
                GA, GX, PH, MG, CC, SS2, T1, T2, UR, UI, GR, GI = range(12)
                sc_ = lambda i: SC[:, i, 0:S]
                kS = lambda i: bkey(SC, i)
                for gi_, (base, buf) in enumerate(
                        (((0, GA)), ((1, GX)))):
                    pg = PS[4 + gi_]
                    for kc in range(2):
                        p.op("tensor",
                             lambda eng, j=j, kc=kc, base=base, pg=pg:
                             eng.matmul(
                                 pg[:, 0:S],
                                 wg_sb[:, (base * 8 + j) * 256 + kc * 128:
                                       (base * 8 + j) * 256 + (kc + 1) * 128],
                                 xh[:, 2 * j + kc, 0:S],
                                 start=(kc == 0), stop=(kc == 1)),
                             reads=[bkey(wg_sb), bkey(xh, 2 * j + kc)],
                             writes=[bkey(pg)], tag="g_mm")
                    bcol = 40 + j if base == 0 else 32 + j
                    p.op("scalar",
                         lambda eng, pg=pg, buf=buf, bcol=bcol:
                         eng.activation(SC[:, buf, 0:S], pg[:, 0:S],
                                        AF.Sigmoid,
                                        bias=pv_sb[:, bcol:bcol + 1]),
                         reads=[bkey(pg), bkey(pv_sb)], writes=[kS(buf)],
                         tag="g_sig")
                p.op("vector", lambda eng, j=j: eng.tensor_scalar(
                    sc_(PH), sc_(GA), pv_sb[:, 56 + j:57 + j], None,
                    ALU.mult),
                    reads=[kS(GA), bkey(pv_sb)], writes=[kS(PH)], tag="th")
                p.op("vector", lambda eng, j=j: eng.tensor_scalar(
                    sc_(GA), sc_(GA), m8sp[:, j:j + 1], None, ALU.mult),
                    reads=[kS(GA), bkey(m8sp)], writes=[kS(GA)], tag="lm")
                p.op("scalar", lambda eng: eng.activation(sc_(MG), sc_(GA),
                                                          AF.Exp),
                     reads=[kS(GA)], writes=[kS(MG)], tag="mag")
                p.op("scalar", lambda eng: eng.activation(sc_(T1), sc_(GA),
                                                          AF.Exp, scale=2.0),
                     reads=[kS(GA)], writes=[kS(T1)], tag="asq")
                p.op("vector", lambda eng: eng.tensor_scalar(
                    sc_(T1), sc_(T1), -1.0, 1.0, ALU.mult, ALU.add),
                    reads=[kS(T1)], writes=[kS(T1)], tag="nr1")
                p.op("scalar", lambda eng: eng.activation(sc_(T1), sc_(T1),
                                                          AF.Sqrt),
                     reads=[kS(T1)], writes=[kS(T1)], tag="nr2")
                p.op("vector", lambda eng: eng.tensor_mul(sc_(GX), sc_(GX),
                                                          sc_(T1)),
                     reads=[kS(GX), kS(T1)], writes=[kS(GX)], tag="sc")
                p.op("vector", lambda eng, j=j: eng.tensor_tensor_scan(
                    sc_(PH), sc_(PH), zer[:, 0:S], carry[:, j:j + 1],
                    ALU.add, ALU.add),
                    reads=[kS(PH), bkey(zer), bkey(carry, "ph")],
                    writes=[kS(PH)], tag="phs")
                p.op("vector", lambda eng, j=j: eng.tensor_copy(
                    carry[:, j:j + 1], SC[:, PH, S - 1:S]),
                    reads=[kS(PH)], writes=[bkey(carry, "ph")], tag="phc")
                p.op("vector", lambda eng: eng.tensor_scalar(
                    sc_(T1), sc_(PH), inv2pi, None, ALU.mult),
                    reads=[kS(PH)], writes=[kS(T1)], tag="rr1")
                p.op("vector", lambda eng: eng.tensor_copy(ki[:, 0:S],
                                                           sc_(T1)),
                     reads=[kS(T1)], writes=[bkey(ki)], tag="rr2")
                p.op("vector", lambda eng: eng.tensor_copy(sc_(T1),
                                                           ki[:, 0:S]),
                     reads=[bkey(ki)], writes=[kS(T1)], tag="rr3")
                p.op("vector", lambda eng: eng.scalar_tensor_tensor(
                    sc_(T1), sc_(T1), -2 * np.pi, sc_(PH), ALU.mult,
                    ALU.add),
                    reads=[kS(T1), kS(PH)], writes=[kS(T1)], tag="rr4")
                p.op("scalar", lambda eng: eng.activation(sc_(SS2), sc_(T1),
                                                          AF.Sin),
                     reads=[kS(T1)], writes=[kS(SS2)], tag="sin")
                p.op("vector", lambda eng: eng.tensor_scalar(
                    sc_(T1), sc_(PH), inv2pi, 0.25, ALU.mult, ALU.add),
                    reads=[kS(PH)], writes=[kS(T1)], tag="rc1")
                p.op("vector", lambda eng: eng.tensor_copy(ki[:, 0:S],
                                                           sc_(T1)),
                     reads=[kS(T1)], writes=[bkey(ki)], tag="rc2")
                p.op("vector", lambda eng: eng.tensor_copy(sc_(T1),
                                                           ki[:, 0:S]),
                     reads=[bkey(ki)], writes=[kS(T1)], tag="rc3")
                p.op("vector", lambda eng: eng.tensor_scalar(
                    sc_(T2), sc_(PH), np.pi / 2, None, ALU.add),
                    reads=[kS(PH)], writes=[kS(T2)], tag="rc4")
                p.op("vector", lambda eng: eng.scalar_tensor_tensor(
                    sc_(T1), sc_(T1), -2 * np.pi, sc_(T2), ALU.mult,
                    ALU.add),
                    reads=[kS(T1), kS(T2)], writes=[kS(T1)], tag="rc5")
                p.op("scalar", lambda eng: eng.activation(sc_(CC), sc_(T1),
                                                          AF.Sin),
                     reads=[kS(T1)], writes=[kS(CC)], tag="cos")
                p.op("vector", lambda eng, j=j: eng.tensor_mul(
                    sc_(T1), sc_(GX), xh[:, j, 0:S]),
                    reads=[kS(GX), bkey(xh, j)], writes=[kS(T1)], tag="xr")
                p.op("vector", lambda eng, j=j: eng.tensor_mul(
                    sc_(T2), sc_(GX), xh[:, j + 8, 0:S]),
                    reads=[kS(GX), bkey(xh, j + 8)], writes=[kS(T2)],
                    tag="xi")
                p.op("vector", lambda eng: eng.tensor_mul(sc_(UR), sc_(CC),
                                                          sc_(T1)),
                     reads=[kS(CC), kS(T1)], writes=[kS(UR)], tag="u1")
                p.op("vector", lambda eng: eng.tensor_mul(sc_(UI), sc_(SS2),
                                                          sc_(T2)),
                     reads=[kS(SS2), kS(T2)], writes=[kS(UI)], tag="u2")
                p.op("vector", lambda eng: eng.tensor_add(sc_(UR), sc_(UR),
                                                          sc_(UI)),
                     reads=[kS(UR), kS(UI)], writes=[kS(UR)], tag="u3")
                p.op("vector", lambda eng: eng.tensor_mul(sc_(UI), sc_(CC),
                                                          sc_(T2)),
                     reads=[kS(CC), kS(T2)], writes=[kS(UI)], tag="u4")
                p.op("vector", lambda eng: eng.tensor_mul(sc_(T2), sc_(SS2),
                                                          sc_(T1)),
                     reads=[kS(SS2), kS(T1)], writes=[kS(T2)], tag="u5")
                p.op("vector", lambda eng: eng.tensor_sub(sc_(UI), sc_(UI),
                                                          sc_(T2)),
                     reads=[kS(UI), kS(T2)], writes=[kS(UI)], tag="u6")
                p.op("vector", lambda eng, j=j: eng.tensor_tensor_scan(
                    sc_(GR), sc_(MG), sc_(UR), carry[:, 8 + j:9 + j],
                    ALU.mult, ALU.add),
                    reads=[kS(MG), kS(UR), bkey(carry, "gr")],
                    writes=[kS(GR)], tag="grs")
                p.op("vector", lambda eng, j=j: eng.tensor_tensor_scan(
                    sc_(GI), sc_(MG), sc_(UI), carry[:, 16 + j:17 + j],
                    ALU.mult, ALU.add),
                    reads=[kS(MG), kS(UI), bkey(carry, "gi")],
                    writes=[kS(GI)], tag="gis")
                p.op("vector", lambda eng, j=j: eng.tensor_copy(
                    carry[:, 8 + j:9 + j], SC[:, GR, S - 1:S]),
                    reads=[kS(GR)], writes=[bkey(carry, "gr")], tag="grc")
                p.op("vector", lambda eng, j=j: eng.tensor_copy(
                    carry[:, 16 + j:17 + j], SC[:, GI, S - 1:S]),
                    reads=[kS(GI)], writes=[bkey(carry, "gi")], tag="gic")
                p.op("vector", lambda eng: eng.tensor_mul(sc_(T1), sc_(CC),
                                                          sc_(GR)),
                     reads=[kS(CC), kS(GR)], writes=[kS(T1)], tag="h1")
                p.op("vector", lambda eng: eng.tensor_mul(sc_(T2), sc_(SS2),
                                                          sc_(GI)),
                     reads=[kS(SS2), kS(GI)], writes=[kS(T2)], tag="h2")
                p.op("vector", lambda eng, j=j: eng.tensor_sub(
                    hS[:, j, 0:S], sc_(T1), sc_(T2)),
                    reads=[kS(T1), kS(T2)], writes=[bkey(hS, j)], tag="h3")
                p.op("vector", lambda eng: eng.tensor_mul(sc_(T1), sc_(SS2),
                                                          sc_(GR)),
                     reads=[kS(SS2), kS(GR)], writes=[kS(T1)], tag="h4")
                p.op("vector", lambda eng: eng.tensor_mul(sc_(T2), sc_(CC),
                                                          sc_(GI)),
                     reads=[kS(CC), kS(GI)], writes=[kS(T2)], tag="h5")
                p.op("vector", lambda eng, j=j: eng.tensor_add(
                    hS[:, j + 8, 0:S], sc_(T1), sc_(T2)),
                    reads=[kS(T1), kS(T2)], writes=[bkey(hS, j + 8)],
                    tag="h6")
            if own:
                for oc in range(NCHUNK):
                    p.op("vector",
                         lambda eng, oc=oc: eng.tensor_scalar(
                             xtmp[:, oc, 0:S], xtmp[:, oc, 0:S],
                             pv_sb[:, 16 + oc:17 + oc], None, ALU.add),
                         reads=[bkey(xtmp, ("x", oc)), bkey(pv_sb)],
                         writes=[bkey(xtmp, ("x", oc))], tag="bo")
                for wh in range(2):
                    dma_load_c(p, "gpsimd", w_sb[:, :, 0:1024],
                               wov[:, :, wh * 1024:(wh + 1) * 1024], NCHUNK,
                               dst_keys=[bkey(w_sb)], tag="wo_ld")
                    for jj in range(8):
                        oc = wh * 8 + jj
                        pg = PS[4 + jj % 2]
                        for c in range(NCHUNK):
                            p.op("tensor",
                                 lambda eng, c=c, jj=jj, pg=pg: eng.matmul(
                                     pg[:, 0:S],
                                     w_sb[:, c, jj * 128:(jj + 1) * 128],
                                     hS[:, c, 0:S],
                                     start=(c == 0), stop=(c == NCHUNK - 1)),
                                 reads=[bkey(w_sb), bkey(hS, c)],
                                 writes=[bkey(pg)], tag="wo_mm")
                        p.op("vector",
                             lambda eng, oc=oc, pg=pg:
                             eng.scalar_tensor_tensor(
                                 xtmp[:, oc, 0:S], pg[:, 0:S], 1.0,
                                 xtmp[:, oc, 0:S], ALU.mult, ALU.add),
                             reads=[bkey(pg), bkey(xtmp, ("x", oc))],
                             writes=[bkey(xtmp, ("x", oc))], tag="wo_acc")
                dma_load_c(p, "sync", xov[:, :, t0 - q0:t0 - q0 + S],
                           xtmp[:, :, 0:S], NCHUNK,
                           dst_keys=[("OUT", t0)],
                           src_keys=[bkey(xtmp, ("x", c))
                                     for c in range(NCHUNK)], tag="x_st")
        p.emit_all(ctx)
    return nc


class XOffView:

    def __init__(self, h, u0):
        self.h = h
        self.u0 = u0
        self.name = h.name

    def __getitem__(self, idx):
        q, c, sl = idx
        return self.h[q, c, slice(sl.start + self.u0, sl.stop + self.u0)]


def prep_rglru_weights(inputs):
    W_in = np.asarray(inputs["L0_attn_a_W_in"])
    W_out = np.asarray(inputs["L0_attn_a_W_out"])
    Wg_x = np.asarray(inputs["L0_attn_a_Wg_x"])
    Wg_a = np.asarray(inputs["L0_attn_a_Wg_a"])
    out = {"winT": chunk128(W_in), "woutT": chunk128(W_out)}
    wgx = Wg_x.reshape(8, 2, 128, 128).transpose(2, 0, 1, 3).reshape(128, 2048)
    wga = Wg_a.reshape(8, 2, 128, 128).transpose(2, 0, 1, 3).reshape(128, 2048)
    out["wg"] = np.ascontiguousarray(np.concatenate([wga, wgx], axis=1))
    pv = np.zeros((128, 64), np.float32)
    pv[:, 0:16] = np.asarray(inputs["L0_attn_a_b_in"]).reshape(16, 128).T
    pv[:, 16:32] = np.asarray(inputs["L0_attn_a_b_out"]).reshape(16, 128).T
    pv[:, 32:40] = np.asarray(inputs["L0_attn_a_bg_x"]).reshape(8, 128).T
    pv[:, 40:48] = np.asarray(inputs["L0_attn_a_bg_a"]).reshape(8, 128).T
    pv[:, 48:56] = np.asarray(
        inputs["L0_attn_a_a_real_param"]).reshape(8, 128).T
    pv[:, 56:64] = np.asarray(
        inputs["L0_attn_a_a_imag_param"]).reshape(8, 128).T
    out["pv"] = pv
    return out


def build_mamba(T, q0, q1):
    nc = bass.Bass("TRN2", target_bir_lowering=False)
    Tq = q1 - q0
    CH = 256
    ncc = T // CH
    aps = {}
    aps["x"] = nc.dram_tensor("x", [128, NCHUNK * T], F32,
                              kind="ExternalInput").ap()
    aps["inT"] = nc.dram_tensor("inT", [128, NCHUNK * 8352], F32,
                                kind="ExternalInput").ap()
    aps["oT"] = nc.dram_tensor("oT", [128, 32 * 2048], F32,
                               kind="ExternalInput").ap()
    aps["cw"] = nc.dram_tensor("cw", [128, 33 * 4], F32,
                               kind="ExternalInput").ap()
    aps["cb"] = nc.dram_tensor("cb", [128, 33], F32,
                               kind="ExternalInput").ap()
    aps["hp"] = nc.dram_tensor("hp", [32, 4], F32,
                               kind="ExternalInput").ap()
    aps["nw"] = nc.dram_tensor("nw", [128, 32], F32,
                               kind="ExternalInput").ap()
    aps["ds_row"] = nc.dram_tensor("ds_row", [1, 32], F32,
                                   kind="ExternalInput").ap()
    aps["ones_bf"] = nc.dram_tensor("ones_bf", [128, 128], BF16,
                                    kind="ExternalInput").ap()
    aps["iden_bf"] = nc.dram_tensor("iden_bf", [128, 128], BF16,
                                    kind="ExternalInput").ap()
    aps["iden_f32"] = nc.dram_tensor("iden_f32", [128, 128], F32,
                                     kind="ExternalInput").ap()
    aps["mks"] = nc.dram_tensor("mks", [128, 2 * CH], F32,
                                kind="ExternalInput").ap()
    xo = nc.dram_tensor("xo", [128, NCHUNK * Tq], F32,
                        kind="ExternalOutput").ap()
    zxd = nc.dram_tensor("zxd", [66, 128, T], BF16, kind="Internal").ap()
    convd = nc.dram_tensor("convd", [33, 128, T], BF16, kind="Internal").ap()
    aD = nc.dram_tensor("aD", [32, T], F32, kind="Internal").ap()
    S = 512
    with ExitStack() as ctx:
        sb = lambda name, shape, dt=F32: ctx.enter_context(
            nc.sbuf_tensor("sm2_" + name, shape, dt))
        xtmp = sb("xtmp", [128, NCHUNK, CH])
        lnbS = sb("lnbS", [128, NCHUNK, S], BF16)
        scrq = sb("scrq", [128, NCHUNK * 512], BF16)
        w_sb = sb("w_sb", [128, NCHUNK, 1024], BF16)
        stage = sb("stage", [128, S], BF16)
        mea = sb("mea", [128, 256]); var = sb("var", [128, 256])
        rstd = sb("rstd", [128, 256]); tmpf = sb("tmpf", [128, 256])
        ones_bf = sb("ones_bf", [128, 128], BF16)
        iden_bf = sb("iden_bf", [128, 128], BF16)
        iden_f32 = sb("iden_f32", [128, 128])
        cw_sb = sb("cw", [128, 33 * 4]); cb_sb = sb("cb", [128, 33])
        cbuf = sb("cbuf", [128, T + 4], BF16)
        cacc = sb("cacc", [128, T], BF16)
        hp_sb = sb("hp", [32, 8])
        nw_sb = sb("nw", [128, 32])
        dsr = sb("dsr", [1, 32])
        dbc = sb("dbc", [128, 32])
        dt_sb = sb("dt", [32, T])
        a_sb = sb("a", [32, T])
        t32 = sb("t32", [32, T])
        arow = sb("arow", [1, CH])
        onesf = sb("onesf", [1, 128])
        B_sb = sb("B", [64, T], BF16)
        C_sb = sb("C", [64, T], BF16)
        xT = sb("xT", [128, 32, 2, 128], BF16)
        BT = sb("BT", [128, 2, 64], BF16)
        Msb = sb("M", [128, 2, CH], BF16)
        Ldif = sb("Ldif", [128, CH])
        Lh = sb("Lh", [128, CH])
        Mh = sb("Mh", [128, 2, CH], BF16)
        aT = sb("aT", [128, 2, 32]); naT = sb("naT", [128, 2, 32])
        eaT = sb("eaT", [128, 2, 32]); dtT = sb("dtT", [128, 2, 32])
        ddT = sb("ddT", [128, 2, 32])
        prev = sb("prev", [64, 32, 128])
        Btl = sb("Btl", [128, 64], BF16)
        Epair = sb("E", [64, 1])
        yfm = sb("yfm", [128, 32, CH], BF16)
        zc = sb("zc", [128, 8, CH], BF16)
        prevb = sb("prevb", [64, 128], BF16)
        mks = sb("mks", [128, 2, CH])
        PS = [ctx.enter_context(nc.psum_tensor(f"ps{i}", [128, 512], F32))
              for i in range(6)]
        PT2 = [ctx.enter_context(nc.psum_tensor(f"pst{i}", [128, 512], BF16))
               for i in range(2)]
        p = Prog(nc)
        cp = ConstPool(p, nc, ctx)
        for n, b in (("ones_bf", ones_bf), ("iden_bf", iden_bf),
                     ("iden_f32", iden_f32), ("cw", cw_sb), ("cb", cb_sb),
                     ("nw", nw_sb), ("ds_row", dsr), ("mks", mks)):
            dma_load(p, "sync", b[:], aps[n], dst_keys=[bkey(b)], tag="c_ld")
        dma_load(p, "sync", hp_sb[:, 0:4], aps["hp"], dst_keys=[bkey(hp_sb)],
                 tag="c_ld")
        p.op("vector", lambda eng: eng.memset(onesf[:], 1.0),
             writes=[bkey(onesf)], tag="c1")
        xv = aps["x"].rearrange("q (c t) -> q c t", c=NCHUNK)
        xov = xo.rearrange("q (c t) -> q c t", c=NCHUNK)
        wiv = aps["inT"].rearrange("q (c f) -> q c f", c=NCHUNK)
        wov = aps["oT"].rearrange("q (c f) -> q c f", c=32)
        skey = lambda off, ln: [bkey(scrq, i) for i in
                                range(off // 512, (off + ln + 511) // 512)]
        for st in range(T // S):
            t0 = st * S
            for u0, ul in split_chunks(S, 256):
                dma_load_c(p, "sync", xtmp[:, :, 0:ul],
                           xv[:, :, t0 + u0:t0 + u0 + ul], NCHUNK,
                           dst_keys=[bkey(xtmp, ("x", c))
                                     for c in range(NCHUNK)], tag="x_ld")
                emit_layernorm_flat(p, nc, ul, xtmp,
                                    LnbView(lnbS, u0), scrq, ones_bf,
                                    PS[0], PS[1], mea, var, rstd, tmpf,
                                    skey, cp=cp)
            for wh in range(9):
                oc0 = wh * 8
                ocn = min(8, 66 - oc0)
                wend = min((oc0 + ocn) * 128, 8352)
                dma_load_c(p, "gpsimd", w_sb[:, :, 0:wend - oc0 * 128],
                           wiv[:, :, oc0 * 128:wend], NCHUNK,
                           dst_keys=[bkey(w_sb)], tag="wi_ld")
                for j in range(ocn):
                    oc = oc0 + j
                    ow = 128 if oc < 65 else 32
                    pg = PS[4 + j % 2]
                    for c in range(NCHUNK):
                        p.op("tensor",
                             lambda eng, c=c, j=j, pg=pg, ow=ow: eng.matmul(
                                 pg[0:ow, 0:S],
                                 w_sb[:, c, j * 128:j * 128 + ow],
                                 lnbS[:, c, 0:S],
                                 start=(c == 0), stop=(c == NCHUNK - 1)),
                             reads=[bkey(w_sb), bkey(lnbS, c)],
                             writes=[bkey(pg)], tag="wi_mm")
                    p.op("vector",
                         lambda eng, pg=pg, ow=ow: eng.tensor_copy(
                             stage[0:ow, 0:S], pg[0:ow, 0:S]),
                         reads=[bkey(pg)], writes=[bkey(stage)], tag="wi_ev")
                    dma_load(p, "sync", zxd[oc, 0:ow, t0:t0 + S],
                             stage[0:ow, 0:S],
                             dst_keys=[("dram", "zxd", oc)],
                             src_keys=[bkey(stage)], tag="wi_st")
        for ch in range(33):
            p.op("vector", lambda eng: eng.memset(cbuf[:, 0:4], 0.0),
                 writes=[bkey(cbuf)], tag="cv0")
            dma_load(p, "sync", cbuf[:, 4:4 + T], zxd[32 + ch, :, 0:T],
                     dst_keys=[bkey(cbuf)], src_keys=[("dram", "zxd", 32 + ch)],
                     tag="cv_ld")
            p.op("vector",
                 lambda eng, ch=ch: eng.tensor_scalar(
                     cacc[:, 0:T], cbuf[:, 1:1 + T],
                     cw_sb[:, ch * 4:ch * 4 + 1], None, ALU.mult),
                 reads=[bkey(cbuf), bkey(cw_sb)], writes=[bkey(cacc)],
                 tag="cv1")
            for k in range(1, 4):
                p.op("vector",
                     lambda eng, ch=ch, k=k: eng.scalar_tensor_tensor(
                         cacc[:, 0:T], cbuf[:, 1 + k:1 + k + T],
                         cw_sb[:, ch * 4 + k:ch * 4 + k + 1], cacc[:, 0:T],
                         ALU.mult, ALU.add),
                     reads=[bkey(cbuf), bkey(cw_sb), bkey(cacc)],
                     writes=[bkey(cacc)], tag="cv2")
            p.op("scalar",
                 lambda eng, ch=ch: eng.activation(
                     cbuf[:, 4:4 + T], cacc[:, 0:T], AF.Silu,
                     bias=cb_sb[:, ch:ch + 1]),
                 reads=[bkey(cacc), bkey(cb_sb)], writes=[bkey(cbuf)],
                 tag="cv3")
            dma_load(p, "sync", convd[ch, :, 0:T], cbuf[:, 4:4 + T],
                     dst_keys=[("dram", "convd", ch)], src_keys=[bkey(cbuf)],
                     tag="cv_st")
        dma_load(p, "gpsimd", dt_sb[:, 0:T], zxd[65, 0:32, 0:T],
                 dst_keys=[bkey(dt_sb)], src_keys=[("dram", "zxd", 65)],
                 tag="dt_ld")
        p.op("scalar", lambda eng: eng.activation(hp_sb[:, 4:5],
                                                  hp_sb[:, 0:1], AF.Exp),
             reads=[bkey(hp_sb)], writes=[bkey(hp_sb, "A")], tag="hp")
        p.op("vector", lambda eng: eng.tensor_scalar(
            hp_sb[:, 4:5], hp_sb[:, 4:5], -1.0, None, ALU.mult),
            reads=[bkey(hp_sb, "A")], writes=[bkey(hp_sb, "A")], tag="hp")
        p.op("vector", lambda eng: eng.tensor_scalar(
            dt_sb[:, 0:T], dt_sb[:, 0:T], hp_sb[:, 1:2], None, ALU.add),
            reads=[bkey(dt_sb), bkey(hp_sb)], writes=[bkey(dt_sb)], tag="dt")
        p.op("scalar", lambda eng: eng.activation(t32[:, 0:T], dt_sb[:, 0:T],
                                                  AF.Abs),
             reads=[bkey(dt_sb)], writes=[bkey(t32)], tag="dt")
        p.op("scalar", lambda eng: eng.activation(t32[:, 0:T], t32[:, 0:T],
                                                  AF.Exp, scale=-1.0),
             reads=[bkey(t32)], writes=[bkey(t32)], tag="dt")
        one_ap, one_k = cp.get_p(1.0, 32)
        p.op("scalar", lambda eng, one_ap=one_ap: eng.activation(
            t32[:, 0:T], t32[:, 0:T], AF.Ln, bias=one_ap),
            reads=[bkey(t32), one_k], writes=[bkey(t32)], tag="dt")
        p.op("vector", lambda eng: eng.tensor_relu(a_sb[:, 0:T],
                                                   dt_sb[:, 0:T]),
             reads=[bkey(dt_sb)], writes=[bkey(a_sb)], tag="dt")
        p.op("vector", lambda eng: eng.tensor_add(dt_sb[:, 0:T], a_sb[:, 0:T],
                                                  t32[:, 0:T]),
             reads=[bkey(a_sb), bkey(t32)], writes=[bkey(dt_sb)], tag="dt")
        p.op("vector", lambda eng: eng.tensor_scalar(
            t32[:, 0:T], dt_sb[:, 0:T], hp_sb[:, 4:5], None, ALU.mult),
            reads=[bkey(dt_sb), bkey(hp_sb, "A")], writes=[bkey(t32)],
            tag="da")
        p.op("vector", lambda eng: eng.memset(a_sb[:, 0:T], 1.0),
             writes=[bkey(a_sb)], tag="msk")
        for cc in range(ncc):
            p.op("vector", lambda eng, cc=cc: eng.memset(
                a_sb[:, cc * CH:cc * CH + 1], 0.0),
                writes=[bkey(a_sb)], tag="msk")
        p.op("vector", lambda eng: eng.tensor_tensor_scan(
            a_sb[:, 0:T], a_sb[:, 0:T], t32[:, 0:T], 0.0, ALU.mult, ALU.add),
            reads=[bkey(a_sb), bkey(t32)], writes=[bkey(a_sb)], tag="asc")
        dma_load(p, "sync", aD[:, 0:T], a_sb[:, 0:T],
                 dst_keys=[("dram", "aD")], src_keys=[bkey(a_sb)], tag="a_st")
        for cc in range(ncc):
            p.op("vector", lambda eng, cc=cc: eng.tensor_scalar(
                t32[:, cc * CH:(cc + 1) * CH], a_sb[:, cc * CH:(cc + 1) * CH],
                -1.0, None, ALU.mult),
                reads=[bkey(a_sb)], writes=[bkey(t32)], tag="dc1")
            p.op("vector", lambda eng, cc=cc: eng.tensor_scalar(
                t32[:, cc * CH:(cc + 1) * CH], t32[:, cc * CH:(cc + 1) * CH],
                a_sb[:, (cc + 1) * CH - 1:(cc + 1) * CH], None, ALU.add),
                reads=[bkey(t32), bkey(a_sb)], writes=[bkey(t32)], tag="dc2")
        p.op("scalar", lambda eng: eng.activation(t32[:, 0:T], t32[:, 0:T],
                                                  AF.Exp),
             reads=[bkey(t32)], writes=[bkey(t32)], tag="dc3")
        p.op("vector", lambda eng: eng.tensor_mul(t32[:, 0:T], t32[:, 0:T],
                                                  dt_sb[:, 0:T]),
             reads=[bkey(t32), bkey(dt_sb)], writes=[bkey(t32)], tag="dc4")
        p.op("tensor", lambda eng: eng.matmul(
            PS[0][:, 0:32], onesf[0:1, :], dsr[0:1, 0:32],
            start=True, stop=True),
            reads=[bkey(onesf), bkey(dsr)], writes=[bkey(PS[0])], tag="db")
        p.op("vector", lambda eng: eng.tensor_copy(dbc[:, 0:32],
                                                   PS[0][:, 0:32]),
             reads=[bkey(PS[0])], writes=[bkey(dbc)], tag="db2")
        dma_load(p, "sync", B_sb[:, 0:T], convd[32, 0:64, 0:T],
                 dst_keys=[bkey(B_sb)], src_keys=[("dram", "convd", 32)],
                 tag="B_ld")
        dma_load(p, "sync", C_sb[:, 0:T], convd[32, 64:128, 0:T],
                 dst_keys=[bkey(C_sb)], src_keys=[("dram", "convd", 32)],
                 tag="C_ld")
        p.op("vector", lambda eng: eng.memset(prev[:, :, :], 0.0),
             writes=[bkey(prev, h) for h in range(32)], tag="pv0")
        xc_sb = ctx.enter_context(
            nc.sbuf_tensor("sm2_xc", [128, 32, CH], BF16))
        for cc in range(ncc):
            c0 = cc * CH
            own = (c0 >= q0) and (c0 < q1)
            for (src, dst) in ((a_sb, aT), (dt_sb, dtT), (t32, ddT)):
                for qt in range(2):
                    p.op("tensor",
                         lambda eng, src=src, qt=qt, c0=c0: eng.transpose(
                             PS[1][0:128, 0:32],
                             src[0:32, c0 + qt * 128:c0 + (qt + 1) * 128],
                             iden_f32[0:32, 0:32]),
                         reads=[bkey(src), bkey(iden_f32)],
                         writes=[bkey(PS[1])], tag="tr_a")
                    p.op("vector",
                         lambda eng, dst=dst, qt=qt: eng.tensor_copy(
                             dst[:, qt, 0:32], PS[1][0:128, 0:32]),
                         reads=[bkey(PS[1])], writes=[bkey(dst)], tag="trac")
            p.op("vector", lambda eng: eng.tensor_scalar(
                naT[:, :, 0:32], aT[:, :, 0:32], -1.0, None, ALU.mult),
                reads=[bkey(aT)], writes=[bkey(naT)], tag="naT")
            p.op("scalar", lambda eng: eng.activation(
                eaT[:, :, 0:32], aT[:, :, 0:32], AF.Exp),
                reads=[bkey(aT)], writes=[bkey(eaT)], tag="eaT")
            for h in range(32):
                dma_load(p, "sync", xc_sb[:, h, 0:CH],
                         convd[h, :, c0:c0 + CH],
                         dst_keys=[bkey(xc_sb, h)],
                         src_keys=[("dram", "convd", h)], tag="xc_ld")
            for h in range(32):
                for qt in range(2):
                    ptb = PT2[qt % 2]
                    p.op("tensor",
                         lambda eng, h=h, qt=qt, ptb=ptb: eng.transpose(
                             ptb[:, 0:128],
                             xc_sb[:, h, qt * 128:(qt + 1) * 128],
                             iden_bf[:]),
                         reads=[bkey(xc_sb, h), bkey(iden_bf)],
                         writes=[bkey(ptb)], tag="xT_tr")
                    p.op("vector",
                         lambda eng, h=h, qt=qt, ptb=ptb: eng.tensor_copy(
                             xT[:, h, qt, 0:128], ptb[:, 0:128]),
                         reads=[bkey(ptb)], writes=[bkey(xT, h)], tag="xT_c")
            for qt in range(2):
                ptb = PT2[qt % 2]
                p.op("tensor",
                     lambda eng, qt=qt, c0=c0, ptb=ptb: eng.transpose(
                         ptb[:, 0:64],
                         B_sb[0:64, c0 + qt * 128:c0 + (qt + 1) * 128],
                         iden_bf[0:64, 0:64]),
                     reads=[bkey(B_sb), bkey(iden_bf)], writes=[bkey(ptb)],
                     tag="BT_tr")
                p.op("vector",
                     lambda eng, qt=qt, ptb=ptb: eng.tensor_copy(
                         BT[:, qt, 0:64], ptb[:, 0:64]),
                     reads=[bkey(ptb)], writes=[bkey(BT)], tag="BT_c")
            for jt in range(2):
                pm = PS[2 + jt]
                p.op("tensor",
                     lambda eng, jt=jt, c0=c0, pm=pm: eng.matmul(
                         pm[:, 0:CH],
                         B_sb[0:64, c0 + jt * 128:c0 + (jt + 1) * 128],
                         C_sb[0:64, c0:c0 + CH],
                         start=True, stop=True),
                     reads=[bkey(B_sb), bkey(C_sb)], writes=[bkey(pm)],
                     tag="M_mm")
                p.op("vector",
                     lambda eng, jt=jt, pm=pm: eng.tensor_copy(
                         Msb[:, jt, 0:CH], pm[:, 0:CH]),
                     reads=[bkey(pm)], writes=[bkey(Msb)], tag="M_c")
            for h in range(32):
                dma_load(p, "sync", arow[0:1, 0:CH],
                         aD[h:h + 1, c0:c0 + CH],
                         dst_keys=[bkey(arow)], src_keys=[("dram", "aD")],
                         tag="ar_ld")
                p.op("vector",
                     lambda eng, h=h: eng.tensor_copy(
                         prevb[0:64, 0:128], prev[0:64, h, 0:128]),
                     reads=[bkey(prev, h)], writes=[bkey(prevb)],
                     tag="pvb")
                for jt in range(2):
                    p.op("tensor",
                         lambda eng: eng.matmul(
                             PS[1][:, 0:CH], onesf[0:1, :],
                             arow[0:1, 0:CH], start=True, stop=True),
                         reads=[bkey(onesf), bkey(arow)],
                         writes=[bkey(PS[1])], tag="Lb_mm")
                    p.op("vector",
                         lambda eng, jt=jt: eng.tensor_add(
                             Ldif[:, 0:CH], PS[1][:, 0:CH],
                             mks[:, jt, 0:CH]),
                         reads=[bkey(PS[1]), bkey(mks)], writes=[bkey(Ldif)],
                         tag="Lb_d")
                    p.op("scalar",
                         lambda eng, jt=jt, h=h: eng.activation(
                             Lh[:, 0:CH], Ldif[:, 0:CH], AF.Exp,
                             bias=naT[:, jt, h:h + 1]),
                         reads=[bkey(Ldif), bkey(naT)], writes=[bkey(Lh)],
                         tag="Lb_e")
                    p.op("vector",
                         lambda eng, jt=jt, h=h: eng.scalar_tensor_tensor(
                             Mh[:, jt, 0:CH], Lh[:, 0:CH],
                             dtT[:, jt, h:h + 1], Msb[:, jt, 0:CH],
                             ALU.mult, ALU.mult),
                         reads=[bkey(Lh), bkey(dtT), bkey(Msb)],
                         writes=[bkey(Mh)], tag="Mh")
                for it in range(2):
                    pz = PS[4 + it]
                    for jt in range(it + 1):
                        p.op("tensor",
                             lambda eng, jt=jt, it=it, h=h, pz=pz:
                             eng.matmul(
                                 pz[:, 0:128],
                                 Mh[:, jt, it * 128:(it + 1) * 128],
                                 xT[:, h, jt, 0:128],
                                 start=(jt == 0), stop=(jt == it)),
                             reads=[bkey(Mh), bkey(xT, h)],
                             writes=[bkey(pz)], tag="zi_mm")
                    p.op("tensor",
                         lambda eng, it=it, h=h, c0=c0: eng.matmul(
                             PS[2][:, 0:128],
                             C_sb[0:64, c0 + it * 128:c0 + (it + 1) * 128],
                             prevb[0:64, 0:128],
                             start=True, stop=True),
                         reads=[bkey(C_sb), bkey(prevb)],
                         writes=[bkey(PS[2])], tag="ze_mm")
                    p.op("vector",
                         lambda eng, it=it, h=h: eng.tensor_scalar(
                             Ldif[:, 0:128], PS[2][:, 0:128],
                             eaT[:, it, h:h + 1], None, ALU.mult),
                         reads=[bkey(PS[2]), bkey(eaT)], writes=[bkey(Ldif)],
                         tag="ye1")
                    p.op("vector",
                         lambda eng, pz=pz: eng.scalar_tensor_tensor(
                             Ldif[:, 0:128], pz[:, 0:128], 1.0,
                             Ldif[:, 0:128], ALU.mult, ALU.add),
                         reads=[bkey(pz), bkey(Ldif)], writes=[bkey(Ldif)],
                         tag="ye1b")
                    p.op("vector",
                         lambda eng, it=it, h=h: eng.scalar_tensor_tensor(
                             Lh[:, 0:128], xT[:, h, it, 0:128],
                             dbc[:, h:h + 1], Ldif[:, 0:128],
                             ALU.mult, ALU.add),
                         reads=[bkey(xT, h), bkey(dbc), bkey(Ldif)],
                         writes=[bkey(Lh)], tag="ye2")
                    p.op("vector",
                         lambda eng: eng.tensor_copy(stage[:, 0:128],
                                                     Lh[:, 0:128]),
                         reads=[bkey(Lh)], writes=[bkey(stage)], tag="ye3")
                    ptb = PT2[it % 2]
                    p.op("tensor",
                         lambda eng, ptb=ptb: eng.transpose(
                             ptb[:, 0:128], stage[:, 0:128], iden_bf[:]),
                         reads=[bkey(stage), bkey(iden_bf)],
                         writes=[bkey(ptb)], tag="ye4")
                    p.op("vector",
                         lambda eng, h=h, it=it, ptb=ptb: eng.tensor_copy(
                             yfm[:, h, it * 128:(it + 1) * 128],
                             ptb[:, 0:128]),
                         reads=[bkey(ptb)], writes=[bkey(yfm, h)], tag="ye5")
                for qt in range(2):
                    p.op("vector",
                         lambda eng, qt=qt, h=h: eng.tensor_scalar(
                             Btl[:, 0:64], BT[:, qt, 0:64],
                             ddT[:, qt, h:h + 1], None, ALU.mult),
                         reads=[bkey(BT), bkey(ddT)], writes=[bkey(Btl)],
                         tag="Bt1")
                    p.op("tensor",
                         lambda eng, qt=qt, h=h: eng.matmul(
                             PS[1][0:64, 0:128], Btl[:, 0:64],
                             xT[:, h, qt, 0:128],
                             start=(qt == 0), stop=(qt == 1)),
                         reads=[bkey(Btl), bkey(xT, h)],
                         writes=[bkey(PS[1])], tag="st_mm")
                p.op("tensor",
                     lambda eng: eng.matmul(
                         PS[0][0:64, 0:1], onesf[0:1, 0:64],
                         arow[0:1, CH - 1:CH],
                         start=True, stop=True),
                     reads=[bkey(onesf), bkey(arow)], writes=[bkey(PS[0])],
                     tag="E_mm")
                p.op("scalar", lambda eng: eng.activation(
                    Epair[0:64, 0:1], PS[0][0:64, 0:1], AF.Exp),
                    reads=[bkey(PS[0])], writes=[bkey(Epair)], tag="E_e")
                p.op("vector",
                     lambda eng, h=h: eng.scalar_tensor_tensor(
                         prev[0:64, h, 0:128], prev[0:64, h, 0:128],
                         Epair[0:64, 0:1], PS[1][0:64, 0:128],
                         ALU.mult, ALU.add),
                     reads=[bkey(prev, h), bkey(Epair), bkey(PS[1])],
                     writes=[bkey(prev, h)], tag="st_up")
            if own:
                for h in range(32):
                    if h % 8 == 0:
                        for k in range(8):
                            dma_load(p, "sync", zc[:, k, 0:CH],
                                     zxd[h + k, :, c0:c0 + CH],
                                     dst_keys=[bkey(zc, k)],
                                     src_keys=[("dram", "zxd", h + k)],
                                     tag="z_ld")
                    p.op("scalar",
                         lambda eng, h=h: eng.activation(
                             stage[:, 0:CH], zc[:, h % 8, 0:CH], AF.Sigmoid),
                         reads=[bkey(zc, h % 8)], writes=[bkey(stage)],
                         tag="gz1")
                    p.op("vector",
                         lambda eng, h=h: eng.tensor_mul(
                             yfm[:, h, 0:CH], yfm[:, h, 0:CH],
                             stage[:, 0:CH]),
                         reads=[bkey(yfm, h), bkey(stage)],
                         writes=[bkey(yfm, h)], tag="gz2")
                    p.op("scalar",
                         lambda eng, h=h: eng.activation(
                             stage[:, 0:CH], yfm[:, h, 0:CH], AF.Square),
                         reads=[bkey(yfm, h)], writes=[bkey(stage)],
                         tag="gn1")
                    p.op("tensor",
                         lambda eng: eng.matmul(
                             PS[1][:, 0:CH], ones_bf[:], stage[:, 0:CH],
                             start=True, stop=True),
                         reads=[bkey(stage), bkey(ones_bf)],
                         writes=[bkey(PS[1])], tag="gn2")
                    eps_ap, eps_k = cp.get(1e-6)
                    p.op("vector",
                         lambda eng: eng.tensor_scalar(
                             Ldif[:, 0:CH], PS[1][:, 0:CH], 1.0 / 128,
                             None, ALU.mult),
                         reads=[bkey(PS[1])], writes=[bkey(Ldif)], tag="gn3")
                    p.op("scalar",
                         lambda eng, eps_ap=eps_ap: eng.activation(
                             Ldif[:, 0:CH], Ldif[:, 0:CH], AF.Ln,
                             bias=eps_ap),
                         reads=[bkey(Ldif), eps_k], writes=[bkey(Ldif)],
                         tag="gn4")
                    p.op("scalar",
                         lambda eng: eng.activation(
                             Ldif[:, 0:CH], Ldif[:, 0:CH], AF.Exp,
                             scale=-0.5),
                         reads=[bkey(Ldif)], writes=[bkey(Ldif)], tag="gn5")
                    p.op("vector",
                         lambda eng, h=h: eng.tensor_mul(
                             yfm[:, h, 0:CH], yfm[:, h, 0:CH], Ldif[:, 0:CH]),
                         reads=[bkey(yfm, h), bkey(Ldif)],
                         writes=[bkey(yfm, h)], tag="gn6")
                    p.op("vector",
                         lambda eng, h=h: eng.tensor_scalar(
                             yfm[:, h, 0:CH], yfm[:, h, 0:CH],
                             nw_sb[:, h:h + 1], None, ALU.mult),
                         reads=[bkey(yfm, h), bkey(nw_sb)],
                         writes=[bkey(yfm, h)], tag="gn7")
                dma_load_c(p, "sync", xtmp[:, :, 0:CH],
                           xv[:, :, c0:c0 + CH], NCHUNK,
                           dst_keys=[bkey(xtmp, ("x", c))
                                     for c in range(NCHUNK)], tag="xr_ld")
                for wq in range(4):
                    dma_load_c(p, "gpsimd",
                               w_sb[:, :, :].rearrange(
                                   "q a b -> q (a b)").rearrange(
                                   "q (c f) -> q c f", c=32),
                               wov[:, :, wq * 512:(wq + 1) * 512], 32,
                               dst_keys=[bkey(w_sb)], tag="wo_ld", group=8)
                    for j in range(4):
                        oc = wq * 4 + j
                        pg = PS[4 + j % 2]
                        for c32 in range(32):
                            p.op("tensor",
                                 lambda eng, c32=c32, j=j, pg=pg: eng.matmul(
                                     pg[:, 0:CH],
                                     w_sb[:, c32 // 2,
                                          (c32 % 2) * 512 + j * 128:
                                          (c32 % 2) * 512 + (j + 1) * 128],
                                     yfm[:, c32, 0:CH],
                                     start=(c32 == 0), stop=(c32 == 31)),
                                 reads=[bkey(w_sb), bkey(yfm, c32)],
                                 writes=[bkey(pg)], tag="op_mm")
                        p.op("vector",
                             lambda eng, oc=oc, pg=pg:
                             eng.scalar_tensor_tensor(
                                 xtmp[:, oc, 0:CH], pg[:, 0:CH], 1.0,
                                 xtmp[:, oc, 0:CH], ALU.mult, ALU.add),
                             reads=[bkey(pg), bkey(xtmp, ("x", oc))],
                             writes=[bkey(xtmp, ("x", oc))], tag="op_acc")
                dma_load_c(p, "sync", xov[:, :, c0 - q0:c0 - q0 + CH],
                           xtmp[:, :, 0:CH], NCHUNK,
                           dst_keys=[("OUT", c0)],
                           src_keys=[bkey(xtmp, ("x", c))
                                     for c in range(NCHUNK)], tag="x_st")
        p.emit_all(ctx)
    return nc


def prep_mamba_weights(inputs):
    inw = np.asarray(inputs["L1_attn_a_in_proj_w"])
    ow = np.asarray(inputs["L1_attn_a_o_proj_w"])
    cw = np.asarray(inputs["L1_attn_a_conv_w"])
    cb = np.asarray(inputs["L1_attn_a_conv_b"])
    out = {}
    inT = np.ascontiguousarray(inw.T)
    out["inT"] = inT.reshape(NCHUNK, 128, 8352).transpose(1, 0, 2).reshape(
        128, NCHUNK * 8352).copy()
    oT = np.ascontiguousarray(ow.T)
    out["oT"] = oT.reshape(32, 128, 2048).transpose(1, 0, 2).reshape(
        128, 32 * 2048).copy()
    out["cw"] = cw.reshape(33, 128, 4).transpose(1, 0, 2).reshape(128, 132).copy()
    out["cb"] = cb.reshape(33, 128).T.copy()
    hp = np.zeros((32, 4), np.float32)
    hp[:, 0] = np.asarray(inputs["L1_attn_a_A_log"])
    hp[:, 1] = np.asarray(inputs["L1_attn_a_dt_bias"])
    hp[:, 2] = np.asarray(inputs["L1_attn_a_D_skip"])
    out["hp"] = hp
    out["nw"] = np.asarray(inputs["L1_attn_a_norm_w"]).reshape(32, 128).T.copy()
    out["ds_row"] = np.asarray(inputs["L1_attn_a_D_skip"]).reshape(1, 32).copy()
    CH = 256
    mks = np.zeros((128, 2, CH), np.float32)
    for jt in range(2):
        for j in range(128):
            jg = jt * 128 + j
            mks[j, jt, :jg] = -1e30
    out["mks"] = mks.reshape(128, 2 * CH).copy()
    return out


def _half_cols(xfm, h):
    v = xfm.reshape(128, NCHUNK, SEQ)[:, :, h * 1024:(h + 1) * 1024]
    return np.ascontiguousarray(v).reshape(128, NCHUNK * 1024)


def _merge_halves(h0, h1):
    out = np.empty((128, NCHUNK, SEQ), np.float32)
    out[:, :, 0:1024] = h0.reshape(128, NCHUNK, 1024)
    out[:, :, 1024:2048] = h1.reshape(128, NCHUNK, 1024)
    return out.reshape(128, NCHUNK * SEQ)


def kernel(**inputs):
    return kernel_mega(**inputs)


def kernel_unfused(**inputs):
    cst = consts_common()
    iden_bf = np.eye(128, dtype=bf)
    ltri = ltri_mask()
    CS, SN2, swp = rope_tables(SEQ)

    w_rg = prep_rglru_weights(inputs)
    w_f0 = prep_ffn0_weights(inputs)
    w_a0 = prep_attn_l0_weights(inputs)
    w_m0 = prep_moe_attnb_weights(inputs)
    w_mb = prep_mamba_weights(inputs)
    w_f1 = prep_l1_ffn_weights(inputs)
    w_a1 = prep_attn_l1_weights(inputs)

    x_in = np.asarray(inputs["L0_attn_a_x"])
    xfm = [to_fm(x_in[b]) for b in range(BATCH)]

    def run_split(mk, extra):
        halves = []
        for h in range(2):
            nc = mk(SEQ, h * 1024, (h + 1) * 1024)
            im = [{"x": xfm[b], **extra} for b in range(BATCH)]
            res = run_bass_kernel_spmd(nc, im, core_ids=list(range(BATCH)))
            halves.append([res.results[b]["xo"] for b in range(BATCH)])
        for b in range(BATCH):
            xfm[b] = _merge_halves(halves[0][b], halves[1][b])

    def run_half(nc, extra):
        im = []
        for c in range(NCORES):
            b, h = c // 2, c % 2
            d = {"x": _half_cols(xfm[b], h)}
            d.update(extra)
            im.append(d)
        res = run_bass_kernel_spmd(nc, im, core_ids=list(range(NCORES)))
        for b in range(BATCH):
            xfm[b] = _merge_halves(res.results[2 * b]["xo"],
                                   res.results[2 * b + 1]["xo"])

    def run_half_xres(nc, extra, xres):
        im = []
        for c in range(NCORES):
            b, h = c // 2, c % 2
            d = {"x": _half_cols(xfm[b], h), "xres": _half_cols(xres[b], h)}
            d.update(extra)
            im.append(d)
        res = run_bass_kernel_spmd(nc, im, core_ids=list(range(NCORES)))
        for b in range(BATCH):
            xfm[b] = _merge_halves(res.results[2 * b]["xo"],
                                   res.results[2 * b + 1]["xo"])

    run_split(build_rglru, {**w_rg, "ones_bf": cst["ones_bf"]})

    nc_f0 = build_ffn0(1024)
    run_half(nc_f0, {**w_f0, **cst})

    xfm_prev = [x.copy() for x in xfm]
    run_split(build_attn_l0, {**w_a0, "CS": CS, "SN2": SN2, "swp": swp,
                              "ones_bf": cst["ones_bf"], "iden_bf": iden_bf,
                              "ltri": ltri})

    xres_snapshot = [x.copy() for x in xfm_prev]
    nc = build_moe_attnb(1024)
    run_half_xres(nc, {**w_m0, "ones_bf": cst["ones_bf"],
                       "iden_f32": cst["iden_f32"], "ind8": cst["ind8"]},
                  xres_snapshot)

    run_half(nc_f0, {**w_f0, **cst})

    run_split(build_mamba, {**w_mb, "ones_bf": cst["ones_bf"],
                            "iden_bf": iden_bf, "iden_f32": cst["iden_f32"]})

    nc_f1 = build_l1_ffn(1024)
    run_half(nc_f1, {**w_f1, "ones_bf": cst["ones_bf"],
                     "iden_f32": cst["iden_f32"]})

    run_split(build_attn_l1, {**w_a1, "ones_bf": cst["ones_bf"],
                              "iden_bf": iden_bf, "ltri": ltri})

    run_half(nc_f1, {**w_f1, "ones_bf": cst["ones_bf"],
                     "iden_f32": cst["iden_f32"]})

    out = np.stack([from_fm(xfm[b], SEQ) for b in range(BATCH)], axis=0)
    return out.astype(np.float32)


from concourse.expressions import RuntimeValue


def prog_barrier(p):
    waits = []
    for e in p.ENGINES:
        if p.cnt[e] > 0:
            waits.append((("eng", e), p.cnt[e]))
    for q in ("sync", "gpsimd"):
        n = p.dma_cnt[q]
        for s in range(p.NDMA):
            uses = (n - s + p.NDMA - 1) // p.NDMA
            if uses > 0:
                waits.append((("dma", q, s), 16 * uses))
    for e in p.ENGINES:
        def em(eng):
            return eng.engine_nop() if hasattr(eng, "engine_nop") else eng.nop()
        o = p.op(e, lambda eng: eng.nop(), tag="barrier")
        for w in waits:
            if p.wm[e].get(w[0], 0) < w[1]:
                o.waits.append(w)
                p.wm[e][w[0]] = w[1]
    return


class Exchanger:

    def __init__(self, p, nc, ctx, n_ch):
        self.p, self.nc = p, nc
        self.n_ch = n_ch
        self.shf = nc.dram_tensor("xshf", [n_ch, 2, 1, 8], I32, kind="Internal",
                                  addr_space="Shared").ap()
        self.sls = nc.dram_tensor("xsls", [1, 2 + n_ch + 8], I32,
                                  kind="ExternalInput").ap()
        self.sls_sb = ctx.enter_context(
            nc.sbuf_tensor("sb_xsls", [1, 2 + n_ch + 8], I32))
        self.pf = ctx.enter_context(nc.sbuf_tensor("sb_xpf", [1, 8], I32))
        self.posem = ctx.enter_context(nc.semaphore("xposem"))
        self.myr = nc.sync.alloc_register("xmyr")
        self.otr = nc.sync.alloc_register("xotr")
        self.mgr = nc.sync.alloc_register("xmgr")
        self.dfr = nc.sync.alloc_register("xdfr")
        dma_load(p, "sync", self.sls_sb[:], self.sls,
                 dst_keys=[bkey(self.sls_sb)], tag="xc_ld")

        self.blkr = [nc.sync.alloc_register(f"xblk{i}") for i in range(8)]

        def em(eng, n_ch=n_ch):
            eng.reg_load(self.myr, self.sls_sb[0:1, 0:1])
            eng.reg_load(self.otr, self.sls_sb[0:1, 1:2])
            for i in range(8):
                eng.reg_load(self.blkr[i],
                             self.sls_sb[0:1, 2 + n_ch + i:3 + n_ch + i])
            return eng.nop()
        self.p.op("sync", em, reads=[bkey(self.sls_sb)], tag="xc_regs")
        self.my = RuntimeValue(val=self.myr, min_val=0, max_val=1)
        self.ot = RuntimeValue(val=self.otr, min_val=0, max_val=1)

    def blk(self, t0):
        assert t0 % 128 == 0 and t0 < 1024
        i = t0 // 128
        return RuntimeValue(val=self.blkr[i], min_val=i, max_val=8 + i)

    def send_flag(self, ch, after_keys):
        def em(eng, ch=ch):
            return eng.dma_start(out=self.shf[ch, bass.ds(self.my, 1), 0, 0:1],
                                 in_=self.sls_sb[0:1, 2 + ch:3 + ch])
        self.p.op("sync", em, reads=list(after_keys) + [bkey(self.sls_sb)],
                  writes=[("xchf", ch)], dma=True, tag="xc_flag")

    def poll(self, ch):
        def em(eng, ch=ch):
            eng.reg_load(self.mgr, self.sls_sb[0:1, 2 + ch:3 + ch])
            eng.reg_mov(self.dfr, 1)
            with eng.While(self.dfr):
                eng.sem_clear(self.posem)
                eng.dma_start(out=self.pf[0:1, 0:1],
                              in_=self.shf[ch, bass.ds(self.ot, 1), 0, 0:1]
                              ).then_inc(self.posem, 16)
                eng.wait_ge(self.posem, 16)
                eng.reg_load(self.dfr, self.pf[0:1, 0:1])
                eng.reg_alu(self.dfr, self.dfr, self.mgr, ALU.subtract)
            return eng.nop()
        return self.p.op("sync", em, reads=[("xchf", ch), bkey(self.sls_sb)],
                         writes=[("xrecv", ch)], tag="xc_poll")


def emit_xfull_exchange(p, nc, ex, ch, x_sb, xfull_ap, my1024):
    for c0, cl in split_chunks(NCHUNK, 4):
        def em(eng, c0=c0, cl=cl, ch=ch):
            return eng.dma_start(
                out=xfull_ap[:, c0:c0 + cl, bass.ds(my1024, 1024)],
                in_=x_sb[:, c0:c0 + cl, 0:1024])
        p.op("sync", em,
             reads=[bkey(x_sb, ("x", c)) for c in range(c0, c0 + cl)],
             writes=[("xch", ch)], dma=True, tag="xf_st")
    ex.send_flag(ch, [("xch", ch)])
    ex.poll(ch)


def emit_own_add(p, nc, T2, x_sb, yv, my1024, xtmp, tag="oad"):
    for t0, tl in split_chunks(1024, 256):
        for c0, cl in split_chunks(NCHUNK, 4):
            def em(eng, c0=c0, cl=cl, t0=t0, tl=tl):
                return eng.dma_start(
                    out=xtmp[:, c0:c0 + cl, 0:tl],
                    in_=yv[:, c0:c0 + cl, bass.ds(my1024 + t0, tl)])
            p.op("sync", em, reads=[(tag + "_y", 0)],
                 writes=[bkey(xtmp, ("x", c)) for c in range(c0, c0 + cl)],
                 dma=True, tag=tag)
        for c in range(NCHUNK):
            p.op("vector",
                 lambda eng, c=c, t0=t0, tl=tl: eng.tensor_add(
                     x_sb[:, c, t0:t0 + tl], x_sb[:, c, t0:t0 + tl],
                     xtmp[:, c, 0:tl]),
                 reads=[bkey(x_sb, ("x", c)), bkey(xtmp, ("x", c))],
                 writes=[bkey(x_sb, ("x", c))], tag=tag)


class HalfView:

    def __init__(self, ap4):
        self.ap4 = ap4

    def __getitem__(self, idx):
        q, csl, tsl = idx
        H = tsl.start // 1024
        assert (tsl.stop - 1) // 1024 == H, "slice crosses half boundary"
        return self.ap4[q, H, csl, tsl.start - H * 1024:tsl.stop - H * 1024]


def _xload(p, xtmp, xv, t0, tl, xkeys, tag="x_ld"):
    for c0, cl in split_chunks(NCHUNK, 4):
        dma_load(p, "sync", xtmp[:, c0:c0 + cl, 0:tl],
                 xv[:, c0:c0 + cl, t0:t0 + tl],
                 dst_keys=[bkey(xtmp, ("x", c)) for c in range(c0, c0 + cl)],
                 src_keys=xkeys, tag=tag)


def emit_st_rglru(p, nc, sctx, pref, aps, xv, yv, xkeys, ykey):
    T = SEQ
    S = 512
    nstrip = T // S
    sb = lambda name, shape, dt=F32: sctx.enter_context(
        nc.sbuf_tensor(pref + name, shape, dt))
    xtmp = sb("xtmp", [128, NCHUNK, S])
    lnbS = sb("lnbS", [128, NCHUNK, S], BF16)
    scrq = sb("scrq", [128, NCHUNK * 512], BF16)
    xh = sb("xh", [128, NCHUNK, S], BF16)
    hS = sb("hS", [128, NCHUNK, S], BF16)
    w_sb = sb("w_sb", [128, NCHUNK, 512], BF16)
    wg_sb = sb("wg_sb", [128, 2 * 2048], BF16)
    pv_sb = sb("pv", [128, 64])
    m8sp = sb("m8sp", [128, 8])
    carry = sb("carry", [128, 24])
    zer = sb("zer", [128, S])
    SC = sb("SC", [128, 12, S])
    ki = sb("ki", [128, S], I32)
    ones_bf = sb("ones_bf", [128, 128], BF16)
    mea = sb("mea", [128, 256]); var = sb("var", [128, 256])
    rstd = sb("rstd", [128, 256]); tmpf = sb("tmpf", [128, 256])
    PS = [sctx.enter_context(nc.psum_tensor(pref + f"ps{i}", [128, 512], F32))
          for i in range(8)]
    cp = ConstPool(p, nc, sctx, prefix=pref)
    dma_load(p, "sync", ones_bf[:], aps["ones_bf"],
             dst_keys=[bkey(ones_bf)], tag="c_ld")
    dma_load(p, "sync", pv_sb[:], aps["g_pv"], dst_keys=[bkey(pv_sb)],
             tag="c_ld")
    dma_load(p, "gpsimd", wg_sb[:], aps["g_wg"], dst_keys=[bkey(wg_sb)],
             tag="c_ld")
    p.op("vector", lambda eng: eng.memset(zer[:], 0.0), writes=[bkey(zer)])
    p.op("vector", lambda eng: eng.memset(carry[:], 0.0),
         writes=[bkey(carry, "ph"), bkey(carry, "gr"), bkey(carry, "gi")])
    arp = pv_sb[:, 48:56]
    one_ap, one_k = cp.get(1.0)
    p.op("scalar", lambda eng: eng.activation(SC[:, 0, 0:8], arp, AF.Abs),
         reads=[bkey(pv_sb)], writes=[bkey(SC, 0)], tag="sp")
    p.op("scalar", lambda eng: eng.activation(SC[:, 0, 8:16], SC[:, 0, 0:8],
                                              AF.Exp, scale=-1.0),
         reads=[bkey(SC, 0)], writes=[bkey(SC, 0)], tag="sp")
    p.op("scalar", lambda eng, one_ap=one_ap: eng.activation(
        SC[:, 0, 8:16], SC[:, 0, 8:16], AF.Ln, bias=one_ap),
        reads=[bkey(SC, 0), one_k], writes=[bkey(SC, 0)], tag="sp")
    p.op("vector", lambda eng: eng.tensor_relu(SC[:, 0, 16:24], arp),
         reads=[bkey(pv_sb)], writes=[bkey(SC, 0)], tag="sp")
    p.op("vector", lambda eng: eng.tensor_add(m8sp[:, 0:8], SC[:, 0, 8:16],
                                              SC[:, 0, 16:24]),
         reads=[bkey(SC, 0)], writes=[bkey(m8sp)], tag="sp")
    p.op("vector", lambda eng: eng.tensor_scalar(m8sp[:, 0:8], m8sp[:, 0:8],
                                                 -8.0, None, ALU.mult),
         reads=[bkey(m8sp)], writes=[bkey(m8sp)], tag="sp")
    wiv = aps["g_winT"].rearrange("q (c f) -> q c f", c=NCHUNK)
    wov = aps["g_woutT"].rearrange("q (c f) -> q c f", c=NCHUNK)
    skey = lambda off, ln: [bkey(scrq, i) for i in
                            range(off // 512, (off + ln + 511) // 512)]
    inv2pi = 1.0 / (2 * np.pi)
    for st in range(nstrip):
        t0 = st * S
        for u0, ul in split_chunks(S, 256):
            _xload(p, XOffView(xtmp, u0), xv, t0 + u0, ul, xkeys)
            emit_layernorm_flat(p, nc, ul, XOffView(xtmp, u0),
                                LnbView(lnbS, u0), scrq, ones_bf,
                                PS[0], PS[1], mea, var, rstd, tmpf,
                                skey, cp=cp)
        for wh in range(4):
            dma_load_c(p, "gpsimd", w_sb[:, :, 0:512],
                       wiv[:, :, wh * 512:(wh + 1) * 512], NCHUNK,
                       dst_keys=[bkey(w_sb)], tag="wi_ld")
            for j in range(4):
                oc = wh * 4 + j
                pg = PS[4 + j % 2]
                for c in range(NCHUNK):
                    p.op("tensor",
                         lambda eng, c=c, j=j, pg=pg: eng.matmul(
                             pg[:, 0:S], w_sb[:, c, j * 128:(j + 1) * 128],
                             lnbS[:, c, 0:S],
                             start=(c == 0), stop=(c == NCHUNK - 1)),
                         reads=[bkey(w_sb), bkey(lnbS, c)],
                         writes=[bkey(pg)], tag="wi_mm")
                p.op("vector",
                     lambda eng, oc=oc, pg=pg: eng.tensor_scalar(
                         xh[:, oc, 0:S], pg[:, 0:S], pv_sb[:, oc:oc + 1],
                         None, ALU.add),
                     reads=[bkey(pg), bkey(pv_sb)], writes=[bkey(xh, oc)],
                     tag="wi_ev")
        for j in range(8):
            GA, GX, PH, MG, CC, SS2, T1, T2, UR, UI, GR, GI = range(12)
            sc_ = lambda i: SC[:, i, 0:S]
            kS = lambda i: bkey(SC, i)
            for gi_, (base, buf) in enumerate(((0, GA), (1, GX))):
                pg = PS[4 + gi_]
                for kc in range(2):
                    p.op("tensor",
                         lambda eng, j=j, kc=kc, base=base, pg=pg: eng.matmul(
                             pg[:, 0:S],
                             wg_sb[:, (base * 8 + j) * 256 + kc * 128:
                                   (base * 8 + j) * 256 + (kc + 1) * 128],
                             xh[:, 2 * j + kc, 0:S],
                             start=(kc == 0), stop=(kc == 1)),
                         reads=[bkey(wg_sb), bkey(xh, 2 * j + kc)],
                         writes=[bkey(pg)], tag="g_mm")
                bcol = 40 + j if base == 0 else 32 + j
                p.op("scalar",
                     lambda eng, pg=pg, buf=buf, bcol=bcol: eng.activation(
                         SC[:, buf, 0:S], pg[:, 0:S], AF.Sigmoid,
                         bias=pv_sb[:, bcol:bcol + 1]),
                     reads=[bkey(pg), bkey(pv_sb)], writes=[kS(buf)],
                     tag="g_sig")
            p.op("vector", lambda eng, j=j: eng.tensor_scalar(
                sc_(PH), sc_(GA), pv_sb[:, 56 + j:57 + j], None, ALU.mult),
                reads=[kS(GA), bkey(pv_sb)], writes=[kS(PH)], tag="th")
            p.op("vector", lambda eng, j=j: eng.tensor_scalar(
                sc_(GA), sc_(GA), m8sp[:, j:j + 1], None, ALU.mult),
                reads=[kS(GA), bkey(m8sp)], writes=[kS(GA)], tag="lm")
            p.op("scalar", lambda eng: eng.activation(sc_(MG), sc_(GA),
                                                      AF.Exp),
                 reads=[kS(GA)], writes=[kS(MG)], tag="mag")
            p.op("scalar", lambda eng: eng.activation(sc_(T1), sc_(GA),
                                                      AF.Exp, scale=2.0),
                 reads=[kS(GA)], writes=[kS(T1)], tag="asq")
            p.op("vector", lambda eng: eng.tensor_scalar(
                sc_(T1), sc_(T1), -1.0, 1.0, ALU.mult, ALU.add),
                reads=[kS(T1)], writes=[kS(T1)], tag="nr1")
            p.op("scalar", lambda eng: eng.activation(sc_(T1), sc_(T1),
                                                      AF.Sqrt),
                 reads=[kS(T1)], writes=[kS(T1)], tag="nr2")
            p.op("vector", lambda eng: eng.tensor_mul(sc_(GX), sc_(GX),
                                                      sc_(T1)),
                 reads=[kS(GX), kS(T1)], writes=[kS(GX)], tag="sc")
            p.op("vector", lambda eng, j=j: eng.tensor_tensor_scan(
                sc_(PH), sc_(PH), zer[:, 0:S], carry[:, j:j + 1],
                ALU.add, ALU.add),
                reads=[kS(PH), bkey(zer), bkey(carry, "ph")],
                writes=[kS(PH)], tag="phs")
            p.op("vector", lambda eng, j=j: eng.tensor_copy(
                carry[:, j:j + 1], SC[:, PH, S - 1:S]),
                reads=[kS(PH)], writes=[bkey(carry, "ph")], tag="phc")
            p.op("vector", lambda eng: eng.tensor_scalar(
                sc_(T1), sc_(PH), inv2pi, None, ALU.mult),
                reads=[kS(PH)], writes=[kS(T1)], tag="rr1")
            p.op("vector", lambda eng: eng.tensor_copy(ki[:, 0:S], sc_(T1)),
                 reads=[kS(T1)], writes=[bkey(ki)], tag="rr2")
            p.op("vector", lambda eng: eng.tensor_copy(sc_(T1), ki[:, 0:S]),
                 reads=[bkey(ki)], writes=[kS(T1)], tag="rr3")
            p.op("vector", lambda eng: eng.scalar_tensor_tensor(
                sc_(T1), sc_(T1), -2 * np.pi, sc_(PH), ALU.mult, ALU.add),
                reads=[kS(T1), kS(PH)], writes=[kS(T1)], tag="rr4")
            p.op("scalar", lambda eng: eng.activation(sc_(SS2), sc_(T1),
                                                      AF.Sin),
                 reads=[kS(T1)], writes=[kS(SS2)], tag="sin")
            p.op("vector", lambda eng: eng.tensor_scalar(
                sc_(T1), sc_(PH), inv2pi, 0.25, ALU.mult, ALU.add),
                reads=[kS(PH)], writes=[kS(T1)], tag="rc1")
            p.op("vector", lambda eng: eng.tensor_copy(ki[:, 0:S], sc_(T1)),
                 reads=[kS(T1)], writes=[bkey(ki)], tag="rc2")
            p.op("vector", lambda eng: eng.tensor_copy(sc_(T1), ki[:, 0:S]),
                 reads=[bkey(ki)], writes=[kS(T1)], tag="rc3")
            p.op("vector", lambda eng: eng.tensor_scalar(
                sc_(T2), sc_(PH), np.pi / 2, None, ALU.add),
                reads=[kS(PH)], writes=[kS(T2)], tag="rc4")
            p.op("vector", lambda eng: eng.scalar_tensor_tensor(
                sc_(T1), sc_(T1), -2 * np.pi, sc_(T2), ALU.mult, ALU.add),
                reads=[kS(T1), kS(T2)], writes=[kS(T1)], tag="rc5")
            p.op("scalar", lambda eng: eng.activation(sc_(CC), sc_(T1),
                                                      AF.Sin),
                 reads=[kS(T1)], writes=[kS(CC)], tag="cos")
            p.op("vector", lambda eng, j=j: eng.tensor_mul(
                sc_(T1), sc_(GX), xh[:, j, 0:S]),
                reads=[kS(GX), bkey(xh, j)], writes=[kS(T1)], tag="xr")
            p.op("vector", lambda eng, j=j: eng.tensor_mul(
                sc_(T2), sc_(GX), xh[:, j + 8, 0:S]),
                reads=[kS(GX), bkey(xh, j + 8)], writes=[kS(T2)], tag="xi")
            p.op("vector", lambda eng: eng.tensor_mul(sc_(UR), sc_(CC),
                                                      sc_(T1)),
                 reads=[kS(CC), kS(T1)], writes=[kS(UR)], tag="u1")
            p.op("vector", lambda eng: eng.tensor_mul(sc_(UI), sc_(SS2),
                                                      sc_(T2)),
                 reads=[kS(SS2), kS(T2)], writes=[kS(UI)], tag="u2")
            p.op("vector", lambda eng: eng.tensor_add(sc_(UR), sc_(UR),
                                                      sc_(UI)),
                 reads=[kS(UR), kS(UI)], writes=[kS(UR)], tag="u3")
            p.op("vector", lambda eng: eng.tensor_mul(sc_(UI), sc_(CC),
                                                      sc_(T2)),
                 reads=[kS(CC), kS(T2)], writes=[kS(UI)], tag="u4")
            p.op("vector", lambda eng: eng.tensor_mul(sc_(T2), sc_(SS2),
                                                      sc_(T1)),
                 reads=[kS(SS2), kS(T1)], writes=[kS(T2)], tag="u5")
            p.op("vector", lambda eng: eng.tensor_sub(sc_(UI), sc_(UI),
                                                      sc_(T2)),
                 reads=[kS(UI), kS(T2)], writes=[kS(UI)], tag="u6")
            p.op("vector", lambda eng, j=j: eng.tensor_tensor_scan(
                sc_(GR), sc_(MG), sc_(UR), carry[:, 8 + j:9 + j],
                ALU.mult, ALU.add),
                reads=[kS(MG), kS(UR), bkey(carry, "gr")],
                writes=[kS(GR)], tag="grs")
            p.op("vector", lambda eng, j=j: eng.tensor_tensor_scan(
                sc_(GI), sc_(MG), sc_(UI), carry[:, 16 + j:17 + j],
                ALU.mult, ALU.add),
                reads=[kS(MG), kS(UI), bkey(carry, "gi")],
                writes=[kS(GI)], tag="gis")
            p.op("vector", lambda eng, j=j: eng.tensor_copy(
                carry[:, 8 + j:9 + j], SC[:, GR, S - 1:S]),
                reads=[kS(GR)], writes=[bkey(carry, "gr")], tag="grc")
            p.op("vector", lambda eng, j=j: eng.tensor_copy(
                carry[:, 16 + j:17 + j], SC[:, GI, S - 1:S]),
                reads=[kS(GI)], writes=[bkey(carry, "gi")], tag="gic")
            p.op("vector", lambda eng: eng.tensor_mul(sc_(T1), sc_(CC),
                                                      sc_(GR)),
                 reads=[kS(CC), kS(GR)], writes=[kS(T1)], tag="h1")
            p.op("vector", lambda eng: eng.tensor_mul(sc_(T2), sc_(SS2),
                                                      sc_(GI)),
                 reads=[kS(SS2), kS(GI)], writes=[kS(T2)], tag="h2")
            p.op("vector", lambda eng, j=j: eng.tensor_sub(
                hS[:, j, 0:S], sc_(T1), sc_(T2)),
                reads=[kS(T1), kS(T2)], writes=[bkey(hS, j)], tag="h3")
            p.op("vector", lambda eng: eng.tensor_mul(sc_(T1), sc_(SS2),
                                                      sc_(GR)),
                 reads=[kS(SS2), kS(GR)], writes=[kS(T1)], tag="h4")
            p.op("vector", lambda eng: eng.tensor_mul(sc_(T2), sc_(CC),
                                                      sc_(GI)),
                 reads=[kS(CC), kS(GI)], writes=[kS(T2)], tag="h5")
            p.op("vector", lambda eng, j=j: eng.tensor_add(
                hS[:, j + 8, 0:S], sc_(T1), sc_(T2)),
                reads=[kS(T1), kS(T2)], writes=[bkey(hS, j + 8)], tag="h6")
        for wh in range(4):
            dma_load_c(p, "gpsimd", w_sb[:, :, 0:512],
                       wov[:, :, wh * 512:(wh + 1) * 512], NCHUNK,
                       dst_keys=[bkey(w_sb)], tag="wo_ld")
            for jj in range(4):
                oc = wh * 4 + jj
                pg = PS[4 + jj % 2]
                for c in range(NCHUNK):
                    p.op("tensor",
                         lambda eng, c=c, jj=jj, pg=pg: eng.matmul(
                             pg[:, 0:S], w_sb[:, c, jj * 128:(jj + 1) * 128],
                             hS[:, c, 0:S],
                             start=(c == 0), stop=(c == NCHUNK - 1)),
                         reads=[bkey(w_sb), bkey(hS, c)],
                         writes=[bkey(pg)], tag="wo_mm")
                p.op("vector",
                     lambda eng, oc=oc, pg=pg: eng.tensor_scalar(
                         xtmp[:, oc, 0:S], pg[:, 0:S],
                         pv_sb[:, 16 + oc:17 + oc], None, ALU.add),
                     reads=[bkey(pg), bkey(pv_sb)],
                     writes=[bkey(xtmp, ("x", oc))], tag="wo_ev")
        for c0, cl in split_chunks(NCHUNK, 4):
            dma_load(p, "sync", yv[:, c0:c0 + cl, t0:t0 + S],
                     xtmp[:, c0:c0 + cl, 0:S],
                     dst_keys=[ykey],
                     src_keys=[bkey(xtmp, ("x", c))
                               for c in range(c0, c0 + cl)], tag="y_st")


def emit_st_attn(p, nc, sctx, pref, aps, xv, yv, xkeys, ykey, kind):
    T = SEQ
    Tq = SEQ
    sb = lambda name, shape, dt=F32: sctx.enter_context(
        nc.sbuf_tensor(pref + name, shape, dt))
    rope = kind == "l0"
    nq, nkv = (16, 4) if kind == "l0" else (16, 16)
    bufs = {}
    bufs["lnb"] = sb("lnb", [128, NCHUNK, T], BF16)
    bufs["scr_bf"] = sb("scr_bf", [128, NCHUNK * 512], BF16)
    bufs["w_sb"] = sb("w_sb", [128, NCHUNK, 512], BF16)
    bufs["xtmp"] = sb("xtmp", [128, NCHUNK, 128])
    bufs["kh"] = sb("kh", [128, T], BF16)
    bufs["vh"] = sb("vh", [128, T], BF16)
    bufs["qh"] = sb("qh", [128, 128], BF16)
    bufs["Srow"] = sb("Srow", [128, T])
    bufs["Pb"] = sb("Pb", [128, T], BF16)
    bufs["PbT"] = sb("PbT", [128, 2048], BF16)
    bufs["am1"] = sb("am1", [128, 1])
    bufs["ars"] = sb("ars", [128, 1])
    bufs["anegm"] = sb("anegm", [128, 1])
    bufs["ltri"] = sb("ltri", [128, 128])
    bufs["iden_bf"] = sb("iden_bf", [128, 128], BF16)
    bufs["ones_bf"] = sb("ones_bf", [128, 128], BF16)
    bufs["stage"] = sb("stage", [128, 2560], BF16)
    bufs["stagef"] = sb("stagef", [128, 512])
    bufs["var"] = sb("var", [128, 512])
    bufs["rstd"] = sb("rstd", [128, 512])
    bufs["mea"] = sb("mea", [128, 256])
    bufs["tmpf"] = sb("tmpf", [128, 256])
    bufs["ostrip"] = sb("ostrip", [128, NCHUNK, 512], BF16)
    if rope:
        bufs["CS"] = sb("CS", [128, T], BF16)
        bufs["SN2"] = sb("SN2", [128, T], BF16)
        bufs["swp"] = sb("swp", [128, 128], BF16)
        bufs["anw"] = sb("anw", [128, NCHUNK])
        bufs["lnf"] = sb("lnf", [128, NCHUNK, 128])
    bufs["PS"] = [sctx.enter_context(
        nc.psum_tensor(pref + f"ps{i}", [128, 512], F32)) for i in range(6)]
    bufs["PT2"] = [sctx.enter_context(
        nc.psum_tensor(pref + f"pst{i}", [128, 512], BF16)) for i in range(2)]
    cp = ConstPool(p, nc, sctx, prefix=pref)
    bufs["cp"] = cp
    pre = "a0_" if kind == "l0" else "a1_"
    names = ["ones_bf", "iden_bf", "ltri"] + (
        ["CS", "SN2", "swp", "anw"] if rope else [])
    srcmap = {"ones_bf": "ones_bf", "iden_bf": "iden_bf", "ltri": "ltri",
              "CS": pre + "CS", "SN2": pre + "SN2", "swp": pre + "swp",
              "anw": pre + "anw"}
    for n in names:
        dma_load(p, "sync", bufs[n][:], aps[srcmap[n]],
                 dst_keys=[bkey(bufs[n])], tag="c_ld")
    qd = nc.dram_tensor(pref + "qd", [nq, 128, Tq], BF16, kind="Internal").ap()
    kd = nc.dram_tensor(pref + "kd", [nkv, 128, T], BF16,
                        kind="Internal").ap()
    vTd = nc.dram_tensor(pref + "vTd", [nkv, T // 128, 128, 128], BF16,
                         kind="Internal").ap()
    od = nc.dram_tensor(pref + "od", [16, 128, Tq], BF16,
                        kind="Internal").ap()
    lnfd = (nc.dram_tensor(pref + "lnfd", [128, NCHUNK * T], F32,
                           kind="Internal").ap() if rope else None)
    cfg = {"nq": nq, "nkv": nkv, "T": T, "q0": 0, "q1": Tq}
    skey = lambda off, ln: [bkey(bufs["scr_bf"], i) for i in
                            range(off // 512, (off + ln + 511) // 512)]
    if rope:
        lnfv = lnfd.rearrange("q (c t) -> q c t", c=NCHUNK)
        for t0, tl in split_chunks(T, 128):
            _xload(p, bufs["xtmp"], xv, t0, tl, xkeys)
            emit_layernorm_flat(p, nc, tl, bufs["xtmp"],
                                LnbView(bufs["lnb"], t0), bufs["scr_bf"],
                                bufs["ones_bf"], bufs["PS"][0], bufs["PS"][1],
                                bufs["mea"], bufs["var"], bufs["rstd"],
                                bufs["tmpf"], skey, cp=cp, lnf=bufs["lnf"])
            dma_load_c(p, "sync", lnfv[:, :, t0:t0 + tl],
                       bufs["lnf"][:, :, 0:tl], NCHUNK,
                       dst_keys=[("lnfd", 0)],
                       src_keys=[bkey(bufs["lnf"], ("x", c))
                                 for c in range(NCHUNK)], tag="lnf_st")
            emit_rmsnorm_flat(p, nc, tl, bufs["lnf"],
                              LnbView(bufs["lnb"], t0), bufs["scr_bf"],
                              bufs["ones_bf"], bufs["PS"][1], bufs["var"],
                              bufs["rstd"], skey, cp, bufs["anw"])
    else:
        for t0, tl in split_chunks(T, 128):
            _xload(p, bufs["xtmp"], xv, t0, tl, xkeys)
            emit_layernorm_flat(p, nc, tl, bufs["xtmp"],
                                LnbView(bufs["lnb"], t0), bufs["scr_bf"],
                                bufs["ones_bf"], bufs["PS"][0], bufs["PS"][1],
                                bufs["mea"], bufs["var"], bufs["rstd"],
                                bufs["tmpf"], skey, cp=cp)
    emit_proj_to_dram(p, nc, Tq, bufs["lnb"], bufs["w_sb"], aps[pre + "wqT"],
                      16, qd, bufs["stage"], bufs["stagef"], bufs["PS"], bufs,
                      scale=1.0 / np.sqrt(128.0), rope=rope, rope_t0=0,
                      tag="q")
    emit_proj_to_dram(p, nc, T, bufs["lnb"], bufs["w_sb"], aps[pre + "wkT"],
                      nkv, kd, bufs["stage"], bufs["stagef"], bufs["PS"],
                      bufs, rope=rope, rope_t0=0, tag="k")
    emit_v_transposed_to_dram(p, nc, T, bufs["lnb"], bufs["w_sb"],
                              aps[pre + "wvT"], nkv, vTd, bufs["stage"],
                              bufs["PS"], bufs, tag="v")
    emit_attn_phase_dram(p, nc, cfg, bufs, qd, kd, vTd, od)
    emit_out_proj_dram(p, nc, cfg, bufs, aps[pre + "woT"], od,
                       lnfv if rope else None, yv, ykey)


def emit_attn_phase_dram(p, nc, cfg, bufs, qd, kd, vTd, od):
    nq, nkv, T, q0, q1 = (cfg["nq"], cfg["nkv"], cfg["T"], cfg["q0"],
                          cfg["q1"])
    Tq = q1 - q0
    PS = bufs["PS"]
    kh = bufs["kh"]; vh = bufs["vh"]; qh = bufs["qh"]
    Srow = bufs["Srow"]; Pb = bufs["Pb"]; PbT = bufs["PbT"]
    m1 = bufs["am1"]; rs = bufs["ars"]; negm = bufs["anegm"]
    ltri = bufs["ltri"]; iden_bf = bufs["iden_bf"]
    grp = nq // nkv
    for h in range(nq):
        kv = h // grp
        if h % grp == 0:
            dma_load(p, "sync", kh[:, 0:T], kd[kv, :, 0:T],
                     dst_keys=[bkey(kh)],
                     src_keys=[("dram", qd.tensor.name, "k", kv)
                               if False else ("dram", kd.tensor.name, kv)],
                     tag="at_kld")
            for kt in range(T // 128):
                dma_load(p, "sync", vh[:, kt * 128:(kt + 1) * 128],
                         vTd[kv, kt, :, :],
                         dst_keys=[bkey(vh)],
                         src_keys=[("dram", vTd.tensor.name, kv)],
                         tag="at_vld")
        for qt in range(Tq // 128):
            gq = q0 + qt * 128
            kend = gq + 128
            nk = kend // 128
            dma_load(p, "sync", qh[:, 0:128],
                     qd[h, :, qt * 128:(qt + 1) * 128],
                     dst_keys=[bkey(qh)],
                     src_keys=[("dram", qd.tensor.name, h)], tag="at_qld")
            for s0, sl2 in split_chunks(kend, 512):
                pg = PS[4 + (s0 // 512) % 2]
                p.op("tensor",
                     lambda eng, s0=s0, sl2=sl2, pg=pg: eng.matmul(
                         pg[:, 0:sl2], qh[:, 0:128], kh[:, s0:s0 + sl2],
                         start=True, stop=True),
                     reads=[bkey(qh), bkey(kh)], writes=[bkey(pg)],
                     tag="at_smm")
                p.op("vector",
                     lambda eng, s0=s0, sl2=sl2, pg=pg: eng.tensor_copy(
                         Srow[:, s0:s0 + sl2], pg[:, 0:sl2]),
                     reads=[bkey(pg)], writes=[bkey(Srow)], tag="at_sc")
            p.op("vector",
                 lambda eng, kend=kend: eng.tensor_add(
                     Srow[:, kend - 128:kend], Srow[:, kend - 128:kend],
                     ltri[:, 0:128]),
                 reads=[bkey(Srow), bkey(ltri)], writes=[bkey(Srow)],
                 tag="at_mask")
            p.op("vector",
                 lambda eng, kend=kend: eng.reduce_max(
                     m1[:, 0:1], Srow[:, 0:kend], axis=AX.X),
                 reads=[bkey(Srow)], writes=[bkey(m1)], tag="at_max")
            p.op("vector", lambda eng: eng.tensor_scalar(
                negm[:, 0:1], m1[:, 0:1], -1.0, None, ALU.mult),
                reads=[bkey(m1)], writes=[bkey(negm)], tag="at_neg")
            p.op("scalar",
                 lambda eng, kend=kend: eng.activation(
                     Pb[:, 0:kend], Srow[:, 0:kend], AF.Exp,
                     bias=negm[:, 0:1], accum_out=rs[:, 0:1]),
                 reads=[bkey(Srow), bkey(negm)],
                 writes=[bkey(Pb), bkey(rs)], tag="at_exp")
            p.op("vector", lambda eng: eng.reciprocal(rs[:, 0:1], rs[:, 0:1]),
                 reads=[bkey(rs)], writes=[bkey(rs)], tag="at_rr")
            p.op("vector",
                 lambda eng, kend=kend: eng.tensor_scalar(
                     Pb[:, 0:kend], Pb[:, 0:kend], rs[:, 0:1], None,
                     ALU.mult),
                 reads=[bkey(Pb), bkey(rs)], writes=[bkey(Pb)], tag="at_pn")
            po = PS[3]
            PT2 = bufs["PT2"]
            for kt in range(nk):
                ptb = PT2[kt % 2]
                p.op("tensor",
                     lambda eng, kt=kt, ptb=ptb: eng.transpose(
                         ptb[:, 0:128], Pb[:, kt * 128:(kt + 1) * 128],
                         iden_bf[:]),
                     reads=[bkey(Pb), bkey(iden_bf)], writes=[bkey(ptb)],
                     tag="at_ptr")
                p.op("vector",
                     lambda eng, kt=kt, ptb=ptb: eng.tensor_copy(
                         PbT[:, kt * 128:(kt + 1) * 128], ptb[:, 0:128]),
                     reads=[bkey(ptb)], writes=[bkey(PbT, kt)], tag="at_ptc")
            for kt in range(nk):
                p.op("tensor",
                     lambda eng, kt=kt, po=po, nk=nk: eng.matmul(
                         po[:, 0:128], vh[:, kt * 128:(kt + 1) * 128],
                         PbT[:, kt * 128:(kt + 1) * 128],
                         start=(kt == 0), stop=(kt == nk - 1)),
                     reads=[bkey(vh), bkey(PbT, kt)], writes=[bkey(po)],
                     tag="at_av")
            p.op("vector",
                 lambda eng, po=po: eng.tensor_copy(
                     bufs["stage"][:, 0:128], po[:, 0:128]),
                 reads=[bkey(po)], writes=[bkey(bufs["stage"], "o")],
                 tag="at_oc")
            dma_load(p, "sync", od[h, :, qt * 128:(qt + 1) * 128],
                     bufs["stage"][:, 0:128],
                     dst_keys=[("dram", od.tensor.name, h)],
                     src_keys=[bkey(bufs["stage"], "o")], tag="at_ost")


def emit_out_proj_dram(p, nc, cfg, bufs, woT_ap, od, base_lnfv, yv, ykey):
    T_q = cfg["q1"] - cfg["q0"]
    PS = bufs["PS"]; w_sb = bufs["w_sb"]; ostrip = bufs["ostrip"]
    wv = woT_ap.rearrange("q (c f) -> q c f", c=NCHUNK)
    for t0, tl in split_chunks(T_q, 512):
        for c in range(NCHUNK):
            dma_load(p, "sync", ostrip[:, c, 0:tl],
                     od[c, :, t0:t0 + tl],
                     dst_keys=[bkey(ostrip, c)],
                     src_keys=[("dram", od.tensor.name, c)], tag="wo_old")
        wgn = w_sb.shape[2] // 128
        for wh0, whn in split_chunks(NCHUNK, wgn):
            dma_load_c(p, "gpsimd", w_sb[:, :, 0:whn * 128],
                       wv[:, :, wh0 * 128:(wh0 + whn) * 128], NCHUNK,
                       dst_keys=[bkey(w_sb)], tag="wo_ld")
            for j in range(whn):
                oc = wh0 + j
                pg = PS[4 + j % 2]
                for c in range(NCHUNK):
                    p.op("tensor",
                         lambda eng, c=c, j=j, tl=tl, pg=pg: eng.matmul(
                             pg[:, 0:tl], w_sb[:, c, j * 128:(j + 1) * 128],
                             ostrip[:, c, 0:tl],
                             start=(c == 0), stop=(c == NCHUNK - 1)),
                         reads=[bkey(w_sb), bkey(ostrip, c)],
                         writes=[bkey(pg)], tag="wo_mm")
                p.op("vector",
                     lambda eng, oc=oc, tl=tl, pg=pg: eng.tensor_copy(
                         bufs["Srow"][:, 0:tl], pg[:, 0:tl]),
                     reads=[bkey(pg)], writes=[bkey(bufs["Srow"])],
                     tag="wo_ev")
                if base_lnfv is not None:
                    dma_load(p, "sync", bufs["stagef"][:, 0:tl],
                             base_lnfv[:, oc, t0:t0 + tl],
                             dst_keys=[bkey(bufs["stagef"])],
                             src_keys=[("lnfd", 0)], tag="wo_bld")
                    p.op("vector",
                         lambda eng, tl=tl: eng.tensor_add(
                             bufs["Srow"][:, 0:tl], bufs["Srow"][:, 0:tl],
                             bufs["stagef"][:, 0:tl]),
                         reads=[bkey(bufs["Srow"]), bkey(bufs["stagef"])],
                         writes=[bkey(bufs["Srow"])], tag="wo_badd")
                dma_load(p, "sync", yv[:, oc, t0:t0 + tl],
                         bufs["Srow"][:, 0:tl],
                         dst_keys=[ykey], src_keys=[bkey(bufs["Srow"])],
                         tag="wo_yst")


def emit_st_mamba(p, nc, sctx, pref, aps, xv, yv, xkeys, ykey):
    T = SEQ
    CH = 256
    ncc = T // CH
    S = 512
    sb = lambda name, shape, dt=F32: sctx.enter_context(
        nc.sbuf_tensor(pref + name, shape, dt))
    xtmp = sb("xtmp", [128, NCHUNK, CH])
    lnbS = sb("lnbS", [128, NCHUNK, S], BF16)
    scrq = sb("scrq", [128, NCHUNK * 512], BF16)
    w_sb = sb("w_sb", [128, NCHUNK, 512], BF16)
    stage = sb("stage", [128, S], BF16)
    mea = sb("mea", [128, 256]); var = sb("var", [128, 256])
    rstd = sb("rstd", [128, 256]); tmpf = sb("tmpf", [128, 256])
    ones_bf = sb("ones_bf", [128, 128], BF16)
    iden_bf = sb("iden_bf", [128, 128], BF16)
    iden_f32 = sb("iden_f32", [128, 128])
    cw_sb = sb("cw", [128, 33 * 4]); cb_sb = sb("cb", [128, 33])
    cbuf = sb("cbuf", [128, T + 4], BF16)
    cacc = sb("cacc", [128, T], BF16)
    hp_sb = sb("hp", [32, 8])
    nw_sb = sb("nw", [128, 32])
    dsr = sb("dsr", [1, 32])
    dbc = sb("dbc", [128, 32])
    dt_sb = sb("dt", [32, T])
    a_sb = sb("a", [32, T])
    t32 = sb("t32", [32, T])
    arow = sb("arow", [1, CH])
    onesf = sb("onesf", [1, 128])
    B_sb = sb("B", [64, T], BF16)
    C_sb = sb("C", [64, T], BF16)
    xT = sb("xT", [128, 32, 2, 128], BF16)
    BT = sb("BT", [128, 2, 64], BF16)
    Msb = sb("M", [128, 2, CH], BF16)
    Ldif = sb("Ldif", [128, CH])
    Lh = sb("Lh", [128, CH])
    Mh = sb("Mh", [128, 2, CH], BF16)
    aT = sb("aT", [128, 2, 32]); naT = sb("naT", [128, 2, 32])
    eaT = sb("eaT", [128, 2, 32]); dtT = sb("dtT", [128, 2, 32])
    ddT = sb("ddT", [128, 2, 32])
    prev = sb("prev", [64, 32, 128])
    prevb = sb("prevb", [64, 128], BF16)
    Btl = sb("Btl", [128, 64], BF16)
    Epair = sb("E", [64, 1])
    yfm = sb("yfm", [128, 32, CH], BF16)
    zc = sb("zc", [128, 8, CH], BF16)
    mks = sb("mks", [128, 2, CH])
    xch = sb("xch", [128, CH], BF16)
    PS = [sctx.enter_context(nc.psum_tensor(pref + f"ps{i}", [128, 512], F32))
          for i in range(6)]
    PT2 = [sctx.enter_context(nc.psum_tensor(pref + f"pst{i}", [128, 512],
                                             BF16)) for i in range(2)]
    cp = ConstPool(p, nc, sctx, prefix=pref)
    zxd = nc.dram_tensor(pref + "zxd", [66, 128, T], BF16,
                         kind="Internal").ap()
    convd = nc.dram_tensor(pref + "convd", [33, 128, T], BF16,
                           kind="Internal").ap()
    aD = nc.dram_tensor(pref + "aD", [32, T], F32, kind="Internal").ap()
    for n, b in (("ones_bf", ones_bf), ("iden_bf", iden_bf),
                 ("iden_f32", iden_f32), ("m_cw", cw_sb), ("m_cb", cb_sb),
                 ("m_nw", nw_sb), ("m_ds_row", dsr), ("m_mks", mks)):
        dma_load(p, "sync", b[:], aps[n], dst_keys=[bkey(b)], tag="c_ld")
    dma_load(p, "sync", hp_sb[:, 0:4], aps["m_hp"], dst_keys=[bkey(hp_sb)],
             tag="c_ld")
    p.op("vector", lambda eng: eng.memset(onesf[:], 1.0),
         writes=[bkey(onesf)], tag="c1")
    wiv = aps["m_inT"].rearrange("q (c f) -> q c f", c=NCHUNK)
    wov = aps["m_oT"].rearrange("q (c f) -> q c f", c=32)
    skey = lambda off, ln: [bkey(scrq, i) for i in
                            range(off // 512, (off + ln + 511) // 512)]
    for st in range(T // S):
        t0 = st * S
        for u0, ul in split_chunks(S, 256):
            _xload(p, xtmp, xv, t0 + u0, ul, xkeys)
            emit_layernorm_flat(p, nc, ul, xtmp, LnbView(lnbS, u0), scrq,
                                ones_bf, PS[0], PS[1], mea, var, rstd, tmpf,
                                skey, cp=cp)
        for wh in range(17):
            oc0 = wh * 4
            ocn = min(4, 66 - oc0)
            wend = min((oc0 + ocn) * 128, 8352)
            dma_load_c(p, "gpsimd", w_sb[:, :, 0:wend - oc0 * 128],
                       wiv[:, :, oc0 * 128:wend], NCHUNK,
                       dst_keys=[bkey(w_sb)], tag="wi_ld")
            for j in range(ocn):
                oc = oc0 + j
                ow = 128 if oc < 65 else 32
                pg = PS[4 + j % 2]
                for c in range(NCHUNK):
                    p.op("tensor",
                         lambda eng, c=c, j=j, pg=pg, ow=ow: eng.matmul(
                             pg[0:ow, 0:S],
                             w_sb[:, c, j * 128:j * 128 + ow],
                             lnbS[:, c, 0:S],
                             start=(c == 0), stop=(c == NCHUNK - 1)),
                         reads=[bkey(w_sb), bkey(lnbS, c)],
                         writes=[bkey(pg)], tag="wi_mm")
                p.op("vector",
                     lambda eng, pg=pg, ow=ow: eng.tensor_copy(
                         stage[0:ow, 0:S], pg[0:ow, 0:S]),
                     reads=[bkey(pg)], writes=[bkey(stage)], tag="wi_ev")
                dma_load(p, "sync", zxd[oc, 0:ow, t0:t0 + S],
                         stage[0:ow, 0:S],
                         dst_keys=[("dram", pref + "zxd", oc)],
                         src_keys=[bkey(stage)], tag="wi_st")
    for ch in range(33):
        p.op("vector", lambda eng: eng.memset(cbuf[:, 0:4], 0.0),
             writes=[bkey(cbuf)], tag="cv0")
        dma_load(p, "sync", cbuf[:, 4:4 + T], zxd[32 + ch, :, 0:T],
                 dst_keys=[bkey(cbuf)],
                 src_keys=[("dram", pref + "zxd", 32 + ch)], tag="cv_ld")
        p.op("vector",
             lambda eng, ch=ch: eng.tensor_scalar(
                 cacc[:, 0:T], cbuf[:, 1:1 + T],
                 cw_sb[:, ch * 4:ch * 4 + 1], None, ALU.mult),
             reads=[bkey(cbuf), bkey(cw_sb)], writes=[bkey(cacc)], tag="cv1")
        for k in range(1, 4):
            p.op("vector",
                 lambda eng, ch=ch, k=k: eng.scalar_tensor_tensor(
                     cacc[:, 0:T], cbuf[:, 1 + k:1 + k + T],
                     cw_sb[:, ch * 4 + k:ch * 4 + k + 1], cacc[:, 0:T],
                     ALU.mult, ALU.add),
                 reads=[bkey(cbuf), bkey(cw_sb), bkey(cacc)],
                 writes=[bkey(cacc)], tag="cv2")
        p.op("scalar",
             lambda eng, ch=ch: eng.activation(
                 cbuf[:, 4:4 + T], cacc[:, 0:T], AF.Silu,
                 bias=cb_sb[:, ch:ch + 1]),
             reads=[bkey(cacc), bkey(cb_sb)], writes=[bkey(cbuf)], tag="cv3")
        dma_load(p, "sync", convd[ch, :, 0:T], cbuf[:, 4:4 + T],
                 dst_keys=[("dram", pref + "convd", ch)],
                 src_keys=[bkey(cbuf)], tag="cv_st")
    dma_load(p, "gpsimd", dt_sb[:, 0:T], zxd[65, 0:32, 0:T],
             dst_keys=[bkey(dt_sb)], src_keys=[("dram", pref + "zxd", 65)],
             tag="dt_ld")
    p.op("scalar", lambda eng: eng.activation(hp_sb[:, 4:5], hp_sb[:, 0:1],
                                              AF.Exp),
         reads=[bkey(hp_sb)], writes=[bkey(hp_sb, "A")], tag="hp")
    p.op("vector", lambda eng: eng.tensor_scalar(
        hp_sb[:, 4:5], hp_sb[:, 4:5], -1.0, None, ALU.mult),
        reads=[bkey(hp_sb, "A")], writes=[bkey(hp_sb, "A")], tag="hp")
    p.op("vector", lambda eng: eng.tensor_scalar(
        dt_sb[:, 0:T], dt_sb[:, 0:T], hp_sb[:, 1:2], None, ALU.add),
        reads=[bkey(dt_sb), bkey(hp_sb)], writes=[bkey(dt_sb)], tag="dt")
    p.op("scalar", lambda eng: eng.activation(t32[:, 0:T], dt_sb[:, 0:T],
                                              AF.Abs),
         reads=[bkey(dt_sb)], writes=[bkey(t32)], tag="dt")
    p.op("scalar", lambda eng: eng.activation(t32[:, 0:T], t32[:, 0:T],
                                              AF.Exp, scale=-1.0),
         reads=[bkey(t32)], writes=[bkey(t32)], tag="dt")
    one_ap, one_k = cp.get_p(1.0, 32)
    p.op("scalar", lambda eng, one_ap=one_ap: eng.activation(
        t32[:, 0:T], t32[:, 0:T], AF.Ln, bias=one_ap),
        reads=[bkey(t32), one_k], writes=[bkey(t32)], tag="dt")
    p.op("vector", lambda eng: eng.tensor_relu(a_sb[:, 0:T], dt_sb[:, 0:T]),
         reads=[bkey(dt_sb)], writes=[bkey(a_sb)], tag="dt")
    p.op("vector", lambda eng: eng.tensor_add(dt_sb[:, 0:T], a_sb[:, 0:T],
                                              t32[:, 0:T]),
         reads=[bkey(a_sb), bkey(t32)], writes=[bkey(dt_sb)], tag="dt")
    p.op("vector", lambda eng: eng.tensor_scalar(
        t32[:, 0:T], dt_sb[:, 0:T], hp_sb[:, 4:5], None, ALU.mult),
        reads=[bkey(dt_sb), bkey(hp_sb, "A")], writes=[bkey(t32)], tag="da")
    p.op("vector", lambda eng: eng.memset(a_sb[:, 0:T], 1.0),
         writes=[bkey(a_sb)], tag="msk")
    for cc in range(ncc):
        p.op("vector", lambda eng, cc=cc: eng.memset(
            a_sb[:, cc * CH:cc * CH + 1], 0.0),
            writes=[bkey(a_sb)], tag="msk")
    p.op("vector", lambda eng: eng.tensor_tensor_scan(
        a_sb[:, 0:T], a_sb[:, 0:T], t32[:, 0:T], 0.0, ALU.mult, ALU.add),
        reads=[bkey(a_sb), bkey(t32)], writes=[bkey(a_sb)], tag="asc")
    dma_load(p, "sync", aD[:, 0:T], a_sb[:, 0:T],
             dst_keys=[("dram", pref + "aD")], src_keys=[bkey(a_sb)],
             tag="a_st")
    for cc in range(ncc):
        p.op("vector", lambda eng, cc=cc: eng.tensor_scalar(
            t32[:, cc * CH:(cc + 1) * CH], a_sb[:, cc * CH:(cc + 1) * CH],
            -1.0, None, ALU.mult),
            reads=[bkey(a_sb)], writes=[bkey(t32)], tag="dc1")
        p.op("vector", lambda eng, cc=cc: eng.tensor_scalar(
            t32[:, cc * CH:(cc + 1) * CH], t32[:, cc * CH:(cc + 1) * CH],
            a_sb[:, (cc + 1) * CH - 1:(cc + 1) * CH], None, ALU.add),
            reads=[bkey(t32), bkey(a_sb)], writes=[bkey(t32)], tag="dc2")
    p.op("scalar", lambda eng: eng.activation(t32[:, 0:T], t32[:, 0:T],
                                              AF.Exp),
         reads=[bkey(t32)], writes=[bkey(t32)], tag="dc3")
    p.op("vector", lambda eng: eng.tensor_mul(t32[:, 0:T], t32[:, 0:T],
                                              dt_sb[:, 0:T]),
         reads=[bkey(t32), bkey(dt_sb)], writes=[bkey(t32)], tag="dc4")
    p.op("tensor", lambda eng: eng.matmul(
        PS[0][:, 0:32], onesf[0:1, :], dsr[0:1, 0:32], start=True, stop=True),
        reads=[bkey(onesf), bkey(dsr)], writes=[bkey(PS[0])], tag="db")
    p.op("vector", lambda eng: eng.tensor_copy(dbc[:, 0:32], PS[0][:, 0:32]),
         reads=[bkey(PS[0])], writes=[bkey(dbc)], tag="db2")
    dma_load(p, "sync", B_sb[:, 0:T], convd[32, 0:64, 0:T],
             dst_keys=[bkey(B_sb)], src_keys=[("dram", pref + "convd", 32)],
             tag="B_ld")
    dma_load(p, "sync", C_sb[:, 0:T], convd[32, 64:128, 0:T],
             dst_keys=[bkey(C_sb)], src_keys=[("dram", pref + "convd", 32)],
             tag="C_ld")
    p.op("vector", lambda eng: eng.memset(prev[:, :, :], 0.0),
         writes=[bkey(prev, h) for h in range(32)], tag="pv0")
    for cc in range(ncc):
        c0 = cc * CH
        for (src, dst) in ((a_sb, aT), (dt_sb, dtT), (t32, ddT)):
            for qt in range(2):
                p.op("tensor",
                     lambda eng, src=src, qt=qt, c0=c0: eng.transpose(
                         PS[1][0:128, 0:32],
                         src[0:32, c0 + qt * 128:c0 + (qt + 1) * 128],
                         iden_f32[0:32, 0:32]),
                     reads=[bkey(src), bkey(iden_f32)], writes=[bkey(PS[1])],
                     tag="tr_a")
                p.op("vector",
                     lambda eng, dst=dst, qt=qt: eng.tensor_copy(
                         dst[:, qt, 0:32], PS[1][0:128, 0:32]),
                     reads=[bkey(PS[1])], writes=[bkey(dst)], tag="trac")
        p.op("vector", lambda eng: eng.tensor_scalar(
            naT[:, :, 0:32], aT[:, :, 0:32], -1.0, None, ALU.mult),
            reads=[bkey(aT)], writes=[bkey(naT)], tag="naT")
        p.op("scalar", lambda eng: eng.activation(
            eaT[:, :, 0:32], aT[:, :, 0:32], AF.Exp),
            reads=[bkey(aT)], writes=[bkey(eaT)], tag="eaT")
        for h in range(32):
            dma_load(p, "sync", xch[:, 0:CH], convd[h, :, c0:c0 + CH],
                     dst_keys=[bkey(xch)],
                     src_keys=[("dram", pref + "convd", h)], tag="xc_ld")
            for qt in range(2):
                ptb = PT2[qt % 2]
                p.op("tensor",
                     lambda eng, qt=qt, ptb=ptb: eng.transpose(
                         ptb[:, 0:128], xch[:, qt * 128:(qt + 1) * 128],
                         iden_bf[:]),
                     reads=[bkey(xch), bkey(iden_bf)], writes=[bkey(ptb)],
                     tag="xT_tr")
                p.op("vector",
                     lambda eng, h=h, qt=qt, ptb=ptb: eng.tensor_copy(
                         xT[:, h, qt, 0:128], ptb[:, 0:128]),
                     reads=[bkey(ptb)], writes=[bkey(xT, h)], tag="xT_c")
        for qt in range(2):
            ptb = PT2[qt % 2]
            p.op("tensor",
                 lambda eng, qt=qt, c0=c0, ptb=ptb: eng.transpose(
                     ptb[:, 0:64],
                     B_sb[0:64, c0 + qt * 128:c0 + (qt + 1) * 128],
                     iden_bf[0:64, 0:64]),
                 reads=[bkey(B_sb), bkey(iden_bf)], writes=[bkey(ptb)],
                 tag="BT_tr")
            p.op("vector",
                 lambda eng, qt=qt, ptb=ptb: eng.tensor_copy(
                     BT[:, qt, 0:64], ptb[:, 0:64]),
                 reads=[bkey(ptb)], writes=[bkey(BT)], tag="BT_c")
        for jt in range(2):
            pm = PS[2 + jt]
            p.op("tensor",
                 lambda eng, jt=jt, c0=c0, pm=pm: eng.matmul(
                     pm[:, 0:CH],
                     B_sb[0:64, c0 + jt * 128:c0 + (jt + 1) * 128],
                     C_sb[0:64, c0:c0 + CH], start=True, stop=True),
                 reads=[bkey(B_sb), bkey(C_sb)], writes=[bkey(pm)],
                 tag="M_mm")
            p.op("vector",
                 lambda eng, jt=jt, pm=pm: eng.tensor_copy(
                     Msb[:, jt, 0:CH], pm[:, 0:CH]),
                 reads=[bkey(pm)], writes=[bkey(Msb)], tag="M_c")
        for h in range(32):
            dma_load(p, "sync", arow[0:1, 0:CH], aD[h:h + 1, c0:c0 + CH],
                     dst_keys=[bkey(arow)], src_keys=[("dram", pref + "aD")],
                     tag="ar_ld")
            p.op("vector",
                 lambda eng, h=h: eng.tensor_copy(prevb[0:64, 0:128],
                                                  prev[0:64, h, 0:128]),
                 reads=[bkey(prev, h)], writes=[bkey(prevb)], tag="pvb")
            for jt in range(2):
                p.op("tensor",
                     lambda eng: eng.matmul(
                         PS[1][:, 0:CH], onesf[0:1, :], arow[0:1, 0:CH],
                         start=True, stop=True),
                     reads=[bkey(onesf), bkey(arow)], writes=[bkey(PS[1])],
                     tag="Lb_mm")
                p.op("vector",
                     lambda eng, jt=jt: eng.tensor_add(
                         Ldif[:, 0:CH], PS[1][:, 0:CH], mks[:, jt, 0:CH]),
                     reads=[bkey(PS[1]), bkey(mks)], writes=[bkey(Ldif)],
                     tag="Lb_d")
                p.op("scalar",
                     lambda eng, jt=jt, h=h: eng.activation(
                         Lh[:, 0:CH], Ldif[:, 0:CH], AF.Exp,
                         bias=naT[:, jt, h:h + 1]),
                     reads=[bkey(Ldif), bkey(naT)], writes=[bkey(Lh)],
                     tag="Lb_e")
                p.op("vector",
                     lambda eng, jt=jt, h=h: eng.scalar_tensor_tensor(
                         Mh[:, jt, 0:CH], Lh[:, 0:CH], dtT[:, jt, h:h + 1],
                         Msb[:, jt, 0:CH], ALU.mult, ALU.mult),
                     reads=[bkey(Lh), bkey(dtT), bkey(Msb)],
                     writes=[bkey(Mh)], tag="Mh")
            for it in range(2):
                pz = PS[4 + it]
                for jt in range(it + 1):
                    p.op("tensor",
                         lambda eng, jt=jt, it=it, h=h, pz=pz: eng.matmul(
                             pz[:, 0:128],
                             Mh[:, jt, it * 128:(it + 1) * 128],
                             xT[:, h, jt, 0:128],
                             start=(jt == 0), stop=(jt == it)),
                         reads=[bkey(Mh), bkey(xT, h)], writes=[bkey(pz)],
                         tag="zi_mm")
                p.op("tensor",
                     lambda eng, it=it, h=h, c0=c0: eng.matmul(
                         PS[2][:, 0:128],
                         C_sb[0:64, c0 + it * 128:c0 + (it + 1) * 128],
                         prevb[0:64, 0:128], start=True, stop=True),
                     reads=[bkey(C_sb), bkey(prevb)], writes=[bkey(PS[2])],
                     tag="ze_mm")
                p.op("vector",
                     lambda eng, it=it, h=h: eng.tensor_scalar(
                         Ldif[:, 0:128], PS[2][:, 0:128],
                         eaT[:, it, h:h + 1], None, ALU.mult),
                     reads=[bkey(PS[2]), bkey(eaT)], writes=[bkey(Ldif)],
                     tag="ye1")
                p.op("vector",
                     lambda eng, pz=pz: eng.scalar_tensor_tensor(
                         Ldif[:, 0:128], pz[:, 0:128], 1.0, Ldif[:, 0:128],
                         ALU.mult, ALU.add),
                     reads=[bkey(pz), bkey(Ldif)], writes=[bkey(Ldif)],
                     tag="ye1b")
                p.op("vector",
                     lambda eng, it=it, h=h: eng.scalar_tensor_tensor(
                         Lh[:, 0:128], xT[:, h, it, 0:128], dbc[:, h:h + 1],
                         Ldif[:, 0:128], ALU.mult, ALU.add),
                     reads=[bkey(xT, h), bkey(dbc), bkey(Ldif)],
                     writes=[bkey(Lh)], tag="ye2")
                p.op("vector",
                     lambda eng: eng.tensor_copy(stage[:, 0:128],
                                                 Lh[:, 0:128]),
                     reads=[bkey(Lh)], writes=[bkey(stage)], tag="ye3")
                ptb = PT2[it % 2]
                p.op("tensor",
                     lambda eng, ptb=ptb: eng.transpose(
                         ptb[:, 0:128], stage[:, 0:128], iden_bf[:]),
                     reads=[bkey(stage), bkey(iden_bf)], writes=[bkey(ptb)],
                     tag="ye4")
                p.op("vector",
                     lambda eng, h=h, it=it, ptb=ptb: eng.tensor_copy(
                         yfm[:, h, it * 128:(it + 1) * 128], ptb[:, 0:128]),
                     reads=[bkey(ptb)], writes=[bkey(yfm, h)], tag="ye5")
            for qt in range(2):
                p.op("vector",
                     lambda eng, qt=qt, h=h: eng.tensor_scalar(
                         Btl[:, 0:64], BT[:, qt, 0:64], ddT[:, qt, h:h + 1],
                         None, ALU.mult),
                     reads=[bkey(BT), bkey(ddT)], writes=[bkey(Btl)],
                     tag="Bt1")
                p.op("tensor",
                     lambda eng, qt=qt, h=h: eng.matmul(
                         PS[1][0:64, 0:128], Btl[:, 0:64],
                         xT[:, h, qt, 0:128],
                         start=(qt == 0), stop=(qt == 1)),
                     reads=[bkey(Btl), bkey(xT, h)], writes=[bkey(PS[1])],
                     tag="st_mm")
            p.op("tensor",
                 lambda eng: eng.matmul(
                     PS[0][0:64, 0:1], onesf[0:1, 0:64],
                     arow[0:1, CH - 1:CH], start=True, stop=True),
                 reads=[bkey(onesf), bkey(arow)], writes=[bkey(PS[0])],
                 tag="E_mm")
            p.op("scalar", lambda eng: eng.activation(
                Epair[0:64, 0:1], PS[0][0:64, 0:1], AF.Exp),
                reads=[bkey(PS[0])], writes=[bkey(Epair)], tag="E_e")
            p.op("vector",
                 lambda eng, h=h: eng.scalar_tensor_tensor(
                     prev[0:64, h, 0:128], prev[0:64, h, 0:128],
                     Epair[0:64, 0:1], PS[1][0:64, 0:128],
                     ALU.mult, ALU.add),
                 reads=[bkey(prev, h), bkey(Epair), bkey(PS[1])],
                 writes=[bkey(prev, h)], tag="st_up")
        for h in range(32):
            if h % 8 == 0:
                for k in range(8):
                    dma_load(p, "sync", zc[:, k, 0:CH],
                             zxd[h + k, :, c0:c0 + CH],
                             dst_keys=[bkey(zc, k)],
                             src_keys=[("dram", pref + "zxd", h + k)],
                             tag="z_ld")
            p.op("scalar",
                 lambda eng, h=h: eng.activation(
                     stage[:, 0:CH], zc[:, h % 8, 0:CH], AF.Sigmoid),
                 reads=[bkey(zc, h % 8)], writes=[bkey(stage)], tag="gz1")
            p.op("vector",
                 lambda eng, h=h: eng.tensor_mul(
                     yfm[:, h, 0:CH], yfm[:, h, 0:CH], stage[:, 0:CH]),
                 reads=[bkey(yfm, h), bkey(stage)], writes=[bkey(yfm, h)],
                 tag="gz2")
            p.op("scalar",
                 lambda eng, h=h: eng.activation(
                     stage[:, 0:CH], yfm[:, h, 0:CH], AF.Square),
                 reads=[bkey(yfm, h)], writes=[bkey(stage)], tag="gn1")
            p.op("tensor",
                 lambda eng: eng.matmul(
                     PS[1][:, 0:CH], ones_bf[:], stage[:, 0:CH],
                     start=True, stop=True),
                 reads=[bkey(stage), bkey(ones_bf)], writes=[bkey(PS[1])],
                 tag="gn2")
            eps_ap, eps_k = cp.get(1e-6)
            p.op("vector",
                 lambda eng: eng.tensor_scalar(
                     Ldif[:, 0:CH], PS[1][:, 0:CH], 1.0 / 128, None,
                     ALU.mult),
                 reads=[bkey(PS[1])], writes=[bkey(Ldif)], tag="gn3")
            p.op("scalar",
                 lambda eng, eps_ap=eps_ap: eng.activation(
                     Ldif[:, 0:CH], Ldif[:, 0:CH], AF.Ln, bias=eps_ap),
                 reads=[bkey(Ldif), eps_k], writes=[bkey(Ldif)], tag="gn4")
            p.op("scalar",
                 lambda eng: eng.activation(
                     Ldif[:, 0:CH], Ldif[:, 0:CH], AF.Exp, scale=-0.5),
                 reads=[bkey(Ldif)], writes=[bkey(Ldif)], tag="gn5")
            p.op("vector",
                 lambda eng, h=h: eng.tensor_mul(
                     yfm[:, h, 0:CH], yfm[:, h, 0:CH], Ldif[:, 0:CH]),
                 reads=[bkey(yfm, h), bkey(Ldif)], writes=[bkey(yfm, h)],
                 tag="gn6")
            p.op("vector",
                 lambda eng, h=h: eng.tensor_scalar(
                     yfm[:, h, 0:CH], yfm[:, h, 0:CH], nw_sb[:, h:h + 1],
                     None, ALU.mult),
                 reads=[bkey(yfm, h), bkey(nw_sb)], writes=[bkey(yfm, h)],
                 tag="gn7")
        for wq in range(8):
            dma_load_c(p, "gpsimd",
                       w_sb[:, :, :].rearrange("q a b -> q (a b)").rearrange(
                           "q (c f) -> q c f", c=32),
                       wov[:, :, wq * 256:(wq + 1) * 256], 32,
                       dst_keys=[bkey(w_sb)], tag="wo_ld", group=8)
            for j in range(2):
                oc = wq * 2 + j
                pg = PS[4 + j % 2]
                for c32 in range(32):
                    p.op("tensor",
                         lambda eng, c32=c32, j=j, pg=pg: eng.matmul(
                             pg[:, 0:CH],
                             w_sb[:, (c32 * 256 + j * 128) // 512,
                                  (c32 * 256 + j * 128) % 512:
                                  (c32 * 256 + j * 128) % 512 + 128],
                             yfm[:, c32, 0:CH],
                             start=(c32 == 0), stop=(c32 == 31)),
                         reads=[bkey(w_sb), bkey(yfm, c32)],
                         writes=[bkey(pg)], tag="op_mm")
                p.op("vector",
                     lambda eng, oc=oc, pg=pg: eng.tensor_copy(
                         xtmp[:, oc, 0:CH], pg[:, 0:CH]),
                     reads=[bkey(pg)],
                     writes=[bkey(xtmp, ("x", oc))], tag="op_ev")
        for c0b, clb in split_chunks(NCHUNK, 4):
            dma_load(p, "sync", yv[:, c0b:c0b + clb, c0:c0 + CH],
                     xtmp[:, c0b:c0b + clb, 0:CH],
                     dst_keys=[ykey],
                     src_keys=[bkey(xtmp, ("x", c))
                               for c in range(c0b, c0b + clb)], tag="y_st")


def _xstate_load(p, x_sb, xsv, tag="xs_ld"):
    for c0, cl in split_chunks(NCHUNK, 4):
        dma_load(p, "sync", x_sb[:, c0:c0 + cl, :], xsv[:, c0:c0 + cl, :],
                 dst_keys=[bkey(x_sb, ("x", c)) for c in range(c0, c0 + cl)],
                 src_keys=[("xs", c) for c in range(c0, c0 + cl)], tag=tag)


def _xstate_store(p, x_sb, xsv, tag="xs_st"):
    for c0, cl in split_chunks(NCHUNK, 4):
        dma_load(p, "sync", xsv[:, c0:c0 + cl, :], x_sb[:, c0:c0 + cl, :],
                 dst_keys=[("xs", c) for c in range(c0, c0 + cl)],
                 src_keys=[bkey(x_sb, ("x", c)) for c in range(c0, c0 + cl)],
                 tag=tag)


def emit_own_add_dram(p, nc, xsv, y4, yhalf, yhv, ex, xtmp, ytmp, ytag):
    def em(eng):
        return eng.dma_start(out=yhalf, in_=y4[:, bass.ds(ex.my, 1), :, :])
    p.op("sync", em, reads=[(ytag + "_y", 0)], writes=[("yhalf", 0)],
         dma=True, tag=ytag + "_cp")
    for t0, tl in split_chunks(1024, 128):
        for c0, cl in split_chunks(NCHUNK, 4):
            dma_load(p, "sync", xtmp[:, c0:c0 + cl, 0:tl],
                     xsv[:, c0:c0 + cl, t0:t0 + tl],
                     dst_keys=[bkey(xtmp, c) for c in range(c0, c0 + cl)],
                     src_keys=[("xs", c) for c in range(c0, c0 + cl)],
                     tag=ytag + "_xl")
            dma_load(p, "sync", ytmp[:, c0:c0 + cl, 0:tl],
                     yhv[:, c0:c0 + cl, t0:t0 + tl],
                     dst_keys=[bkey(ytmp, c) for c in range(c0, c0 + cl)],
                     src_keys=[("yhalf", 0)], tag=ytag + "_yl")
        for c in range(NCHUNK):
            p.op("vector",
                 lambda eng, c=c, tl=tl: eng.tensor_add(
                     xtmp[:, c, 0:tl], xtmp[:, c, 0:tl], ytmp[:, c, 0:tl]),
                 reads=[bkey(xtmp, c), bkey(ytmp, c)],
                 writes=[bkey(xtmp, c)], tag=ytag + "_ad")
        for c0, cl in split_chunks(NCHUNK, 4):
            dma_load(p, "sync", xsv[:, c0:c0 + cl, t0:t0 + tl],
                     xtmp[:, c0:c0 + cl, 0:tl],
                     dst_keys=[("xs", c) for c in range(c0, c0 + cl)],
                     src_keys=[bkey(xtmp, c) for c in range(c0, c0 + cl)],
                     tag=ytag + "_xs")


def build_mega():
    nc = bass.Bass("TRN2", target_bir_lowering=False)
    T1 = 1024
    aps = {}
    aps["xin"] = nc.dram_tensor("xin", [128, NCHUNK * T1], F32,
                                kind="ExternalInput").ap()
    for n, sh, dt in [
        ("ones_bf", [128, 128], BF16), ("iden_bf", [128, 128], BF16),
        ("iden_f32", [128, 128], F32), ("ltri", [128, 128], F32),
        ("ind8", [8, 8 * 128], F32),
        ("g_winT", [128, NCHUNK * 2048], F32),
        ("g_woutT", [128, NCHUNK * 2048], F32),
        ("g_wg", [128, 2 * 2048], F32), ("g_pv", [128, 64], F32),
        ("f0_gpT_h", [8, 128, NCHUNK * F_L0], F32),
        ("f0_upT_h", [8, 128, NCHUNK * F_L0], F32),
        ("f0_dnT_h", [8, 2, 128, NF_L0 * 1024], F32),
        ("f0_gwT_h", [128, NCHUNK * 8], F32),
        ("a0_wqT", [128, NCHUNK * 2048], F32),
        ("a0_wkT", [128, NCHUNK * 512], F32),
        ("a0_wvT", [128, NCHUNK * 512], F32),
        ("a0_woT", [128, NCHUNK * 2048], F32),
        ("a0_CS", [128, SEQ], BF16), ("a0_SN2", [128, SEQ], BF16),
        ("a0_swp", [128, 128], BF16), ("a0_anw", [128, NCHUNK], F32),
        ("m0_gpT_h", [8, 128, NCHUNK * F_AB], F32),
        ("m0_upT_h", [8, 128, NCHUNK * F_AB], F32),
        ("m0_dnT_h", [8, 2, 128, NF_AB * 1024], F32),
        ("m0_gwT_h", [128, NCHUNK * 8], F32),
        ("m0_mnw", [128, NCHUNK], F32),
        ("m_inT", [128, NCHUNK * 8352], F32),
        ("m_oT", [128, 32 * 2048], F32),
        ("m_cw", [128, 33 * 4], F32), ("m_cb", [128, 33], F32),
        ("m_hp", [32, 4], F32), ("m_nw", [128, 32], F32),
        ("m_ds_row", [1, 32], F32), ("m_mks", [128, 2 * 256], F32),
        ("f1_guT_q", [4, 4, 128, NCHUNK * 1024], F32),
        ("f1_dnT_hh", [4, 2, 128, NCHUNK * 1024], F32),
        ("a1_wqT", [128, NCHUNK * 2048], F32),
        ("a1_wkT", [128, NCHUNK * 2048], F32),
        ("a1_wvT", [128, NCHUNK * 2048], F32),
        ("a1_woT", [128, NCHUNK * 2048], F32),
    ]:
        aps[n] = nc.dram_tensor(n, sh, dt, kind="ExternalInput").ap()
    xo = nc.dram_tensor("xo", [128, NCHUNK * T1], F32,
                        kind="ExternalOutput").ap()
    xstate = nc.dram_tensor("xstate", [128, NCHUNK * T1], F32,
                            kind="Internal").ap()
    xfull = [nc.dram_tensor(f"xfull{i}", [128, 2 * NCHUNK * 1024], F32,
                            kind="Internal", addr_space="Shared").ap()
             for i in range(4)]
    ydram = nc.dram_tensor("ydram", [128, 2 * NCHUNK * 1024], F32,
                           kind="Internal").ap()
    yhalf = nc.dram_tensor("yhalf", [128, NCHUNK * 1024], F32,
                           kind="Internal").ap()
    xsv = xstate.rearrange("q (c t) -> q c t", c=NCHUNK)
    xf4 = [x.rearrange("q (H c t) -> q H c t", H=2, c=NCHUNK)
           for x in xfull]
    xfv = [HalfView(a) for a in xf4]
    y4 = ydram.rearrange("q (H c t) -> q H c t", H=2, c=NCHUNK)
    yv = HalfView(y4)
    yhv = yhalf.rearrange("q (c t) -> q c t", c=NCHUNK)

    with ExitStack() as ctx:
        p = Prog(nc)
        ex = Exchanger(p, nc, ctx, n_ch=4)
        xtmpP = ctx.enter_context(
            nc.sbuf_tensor("sb_mg_xt", [128, NCHUNK, 128], F32))
        ytmpP = ctx.enter_context(
            nc.sbuf_tensor("sb_mg_yt", [128, NCHUNK, 128], F32))
        for c0, cl in split_chunks(NCHUNK, 4):
            dma_load(p, "sync",
                     xsv[:, c0:c0 + cl, :],
                     aps["xin"].rearrange("q (c t) -> q c t",
                                          c=NCHUNK)[:, c0:c0 + cl, :],
                     dst_keys=[("xs", c) for c in range(c0, c0 + cl)],
                     tag="x_init")


        def exchange(ch):
            def em(eng, ch=ch):
                return eng.dma_start(
                    out=xf4[ch][:, bass.ds(ex.my, 1), :, :],
                    in_=xstate)
            p.op("sync", em, reads=[("xs", c) for c in range(NCHUNK)],
                 writes=[("xch", ch)], dma=True, tag="xf_st")
            ex.send_flag(ch, [("xch", ch)])
            ex.poll(ch)
            return [("xch", ch), ("xrecv", ch)]

        k0 = exchange(0)
        with ExitStack() as sctx:
            emit_st_rglru(p, nc, sctx, "sg_", aps, xfv[0], yv, k0,
                          ("g_y", 0))
            emit_own_add_dram(p, nc, xsv, y4, yhalf, yhv, ex, xtmpP, ytmpP, "g")
            prog_barrier(p)
        with ExitStack() as sctx:
            bufs = alloc_common(nc, sctx, T1, want=("router",), prefix="sf0_")
            sb2 = lambda n2, sh2, dt2=F32: sctx.enter_context(
                nc.sbuf_tensor("sf0b_" + n2, sh2, dt2))
            bufs["scr_bf"] = sb2("scr", [128, NF_L0 * T1], BF16)
            bufs["w_gu"] = sb2("wgu", [128, NCHUNK, 768], BF16)
            bufs["w_dn"] = sb2("wdn", [128, NF_L0, 1024], BF16)
            p2 = p
            bufs["cp"] = ConstPool(p2, nc, sctx, prefix="sf0_")
            load_common(p2, nc, T1, bufs, {"x": None, "ones_bf": aps["ones_bf"],
                                           "iden_f32": aps["iden_f32"],
                                           "ind8": aps["ind8"],
                                           "gwT_h": aps["f0_gwT_h"]},
                        want=("router",), skip_x=True)
            _xstate_load(p2, bufs["x_sb"], xsv)
            w = {"gpT": [aps["f0_gpT_h"][e] for e in range(8)],
                 "upT": [aps["f0_upT_h"][e] for e in range(8)],
                 "dnT_h": aps["f0_dnT_h"]}
            emit_moe(p2, nc, T1, bufs, w, NF=NF_L0)
            _xstate_store(p2, bufs["x_sb"], xsv)
            prog_barrier(p)
        k1 = exchange(1)
        with ExitStack() as sctx:
            emit_st_attn(p, nc, sctx, "sa0_", aps, xfv[1], yv, k1,
                         ("xx_y", 0), "l0")
            prog_barrier(p)
        with ExitStack() as sctx:
            bufs = alloc_common(nc, sctx, T1, want=("router",), prefix="sm0_")
            sb2 = lambda n2, sh2, dt2=F32: sctx.enter_context(
                nc.sbuf_tensor("sm0b_" + n2, sh2, dt2))
            bufs["scr_bf"] = sb2("scr", [128, max(NF_AB * T1, NCHUNK * 512)],
                                 BF16)
            bufs["w_gu"] = sb2("wgu", [128, NCHUNK, 768], BF16)
            bufs["w_dn"] = sb2("wdn", [128, NF_AB, 1024], BF16)
            bufs["mnw"] = sb2("mnw", [128, NCHUNK])
            bufs["xxs"] = sb2("xxs", [128, NCHUNK, 256])
            bufs["cp"] = ConstPool(p, nc, sctx, prefix="sm0_")
            load_common(p, nc, T1, bufs, {"ones_bf": aps["ones_bf"],
                                          "iden_f32": aps["iden_f32"],
                                          "ind8": aps["ind8"],
                                          "gwT_h": aps["m0_gwT_h"]},
                        want=("router",), skip_x=True)
            dma_load(p, "sync", bufs["mnw"][:], aps["m0_mnw"],
                     dst_keys=[bkey(bufs["mnw"])], tag="c_ld")
            _xstate_load(p, bufs["x_sb"], xsv)
            skey = lambda off, ln: [bkey(bufs["scr_bf"], i) for i in
                                    range(off // 512, (off + ln + 511) // 512)]
            def emcp(eng):
                return eng.dma_start(out=yhalf,
                                     in_=y4[:, bass.ds(ex.my, 1), :, :])
            p.op("sync", emcp, reads=[("xx_y", 0)], writes=[("yhalf", 0)],
                 dma=True, tag="xx_cp")
            for t0, tl in split_chunks(T1, 128):
                for c0, cl in split_chunks(NCHUNK, 4):
                    dma_load(p, "sync", bufs["xxs"][:, c0:c0 + cl, 0:128],
                             yhv[:, c0:c0 + cl, t0:t0 + tl],
                             dst_keys=[bkey(bufs["xxs"], ("x", c))
                                       for c in range(c0, c0 + cl)],
                             src_keys=[("yhalf", 0)], tag="xx_ld")
                emit_rmsnorm_flat(p, nc, tl, bufs["xxs"],
                                  LnbView(bufs["lnb"], t0), bufs["scr_bf"],
                                  bufs["ones_bf"], bufs["PS"][1],
                                  bufs["var"], bufs["rstd"], skey,
                                  bufs["cp"], bufs["mnw"])
                for c in range(NCHUNK):
                    p.op("vector",
                         lambda eng, c=c, t0=t0, tl=tl, bufs=bufs:
                         eng.tensor_add(
                             bufs["x_sb"][:, c, t0:t0 + tl],
                             bufs["x_sb"][:, c, t0:t0 + tl],
                             bufs["xxs"][:, c, 0:tl]),
                         reads=[bkey(bufs["x_sb"], ("x", c)),
                                bkey(bufs["xxs"], ("x", c))],
                         writes=[bkey(bufs["x_sb"], ("x", c))], tag="xx_add")
            w = {"gpT": [aps["m0_gpT_h"][e] for e in range(8)],
                 "upT": [aps["m0_upT_h"][e] for e in range(8)],
                 "dnT_h": aps["m0_dnT_h"]}
            emit_moe(p, nc, T1, bufs, w, NF=NF_AB, norm=None)
            _xstate_store(p, bufs["x_sb"], xsv)
            prog_barrier(p)
        with ExitStack() as sctx:
            bufs = alloc_common(nc, sctx, T1, want=("router",), prefix="sf2_")
            sb2 = lambda n2, sh2, dt2=F32: sctx.enter_context(
                nc.sbuf_tensor("sf2b_" + n2, sh2, dt2))
            bufs["scr_bf"] = sb2("scr", [128, NF_L0 * T1], BF16)
            bufs["w_gu"] = sb2("wgu", [128, NCHUNK, 768], BF16)
            bufs["w_dn"] = sb2("wdn", [128, NF_L0, 1024], BF16)
            bufs["cp"] = ConstPool(p, nc, sctx, prefix="sf2_")
            load_common(p, nc, T1, bufs, {"ones_bf": aps["ones_bf"],
                                          "iden_f32": aps["iden_f32"],
                                          "ind8": aps["ind8"],
                                          "gwT_h": aps["f0_gwT_h"]},
                        want=("router",), skip_x=True)
            _xstate_load(p, bufs["x_sb"], xsv)
            w = {"gpT": [aps["f0_gpT_h"][e] for e in range(8)],
                 "upT": [aps["f0_upT_h"][e] for e in range(8)],
                 "dnT_h": aps["f0_dnT_h"]}
            emit_moe(p, nc, T1, bufs, w, NF=NF_L0)
            _xstate_store(p, bufs["x_sb"], xsv)
            prog_barrier(p)
        k2 = exchange(2)
        with ExitStack() as sctx:
            emit_st_mamba(p, nc, sctx, "smb_", aps, xfv[2], yv, k2,
                          ("mb_y", 0))
            emit_own_add_dram(p, nc, xsv, y4, yhalf, yhv, ex, xtmpP, ytmpP, "mb")
            prog_barrier(p)
        for rep, pf2 in (("f1a", "sfa_"), ):
            with ExitStack() as sctx:
                bufs = alloc_common(nc, sctx, T1, prefix=pf2)
                sb2 = lambda n2, sh2, dt2=F32: sctx.enter_context(
                    nc.sbuf_tensor(pf2 + "b_" + n2, sh2, dt2))
                bufs["scr_bf"] = sb2("scr", [128, max(NCHUNK * 512,
                                                      NCHUNK * (T1 // 4))],
                                     BF16)
                bufs["w_gu16"] = sb2("wgu16", [128, NCHUNK, 1024], BF16)
                bufs["cp"] = ConstPool(p, nc, sctx, prefix=pf2)
                load_common(p, nc, T1, bufs, {"ones_bf": aps["ones_bf"],
                                              "iden_f32": aps["iden_f32"]},
                            skip_x=True)
                _xstate_load(p, bufs["x_sb"], xsv)
                emit_l1_ffn(p, nc, T1, bufs,
                            {"guT_q": aps["f1_guT_q"],
                             "dnT_hh": aps["f1_dnT_hh"]}, 0)
                _xstate_store(p, bufs["x_sb"], xsv)
                prog_barrier(p)
        k3 = exchange(3)
        with ExitStack() as sctx:
            emit_st_attn(p, nc, sctx, "sa1_", aps, xfv[3], yv, k3,
                         ("a1_y", 0), "l1")
            emit_own_add_dram(p, nc, xsv, y4, yhalf, yhv, ex, xtmpP, ytmpP, "a1")
            prog_barrier(p)
        with ExitStack() as sctx:
            pf2 = "sfb_"
            bufs = alloc_common(nc, sctx, T1, prefix=pf2)
            sb2 = lambda n2, sh2, dt2=F32: sctx.enter_context(
                nc.sbuf_tensor(pf2 + "b_" + n2, sh2, dt2))
            bufs["scr_bf"] = sb2("scr", [128, max(NCHUNK * 512,
                                                  NCHUNK * (T1 // 4))], BF16)
            bufs["w_gu16"] = sb2("wgu16", [128, NCHUNK, 1024], BF16)
            bufs["cp"] = ConstPool(p, nc, sctx, prefix=pf2)
            load_common(p, nc, T1, bufs, {"ones_bf": aps["ones_bf"],
                                          "iden_f32": aps["iden_f32"]},
                        skip_x=True)
            _xstate_load(p, bufs["x_sb"], xsv)
            emit_l1_ffn(p, nc, T1, bufs,
                        {"guT_q": aps["f1_guT_q"],
                         "dnT_hh": aps["f1_dnT_hh"]}, 0)
            for c0, cl in split_chunks(NCHUNK, 4):
                dma_load(p, "sync",
                         xo.rearrange("q (c t) -> q c t",
                                      c=NCHUNK)[:, c0:c0 + cl, :],
                         bufs["x_sb"][:, c0:c0 + cl, :],
                         dst_keys=[("OUT", c) for c in range(c0, c0 + cl)],
                         src_keys=[bkey(bufs["x_sb"], ("x", c))
                                   for c in range(c0, c0 + cl)], tag="xo_st")
        p.emit_all(ctx)
    return nc


def kernel_mega(**inputs):
    cst = consts_common()
    iden_bf = np.eye(128, dtype=bf)
    ltri = ltri_mask()
    CS, SN2, swp = rope_tables(SEQ)
    w_rg = prep_rglru_weights(inputs)
    w_f0 = prep_ffn0_weights(inputs)
    w_a0 = prep_attn_l0_weights(inputs)
    w_m0 = prep_moe_attnb_weights(inputs)
    w_mb = prep_mamba_weights(inputs)
    w_f1 = prep_l1_ffn_weights(inputs)
    w_a1 = prep_attn_l1_weights(inputs)
    x_in = np.asarray(inputs["L0_attn_a_x"])
    xfm = [to_fm(x_in[b]) for b in range(BATCH)]
    base = {
        "ones_bf": cst["ones_bf"], "iden_bf": iden_bf,
        "iden_f32": cst["iden_f32"], "ltri": ltri, "ind8": cst["ind8"],
        "g_winT": w_rg["winT"], "g_woutT": w_rg["woutT"],
        "g_wg": w_rg["wg"], "g_pv": w_rg["pv"],
        "f0_gpT_h": w_f0["gpT_h"], "f0_upT_h": w_f0["upT_h"],
        "f0_dnT_h": w_f0["dnT_h"], "f0_gwT_h": w_f0["gwT_h"],
        "a0_wqT": w_a0["wqT"], "a0_wkT": w_a0["wkT"],
        "a0_wvT": w_a0["wvT"], "a0_woT": w_a0["woT"],
        "a0_CS": CS, "a0_SN2": SN2, "a0_swp": swp, "a0_anw": w_a0["anw"],
        "m0_gpT_h": w_m0["gpT_h"], "m0_upT_h": w_m0["upT_h"],
        "m0_dnT_h": w_m0["dnT_h"], "m0_gwT_h": w_m0["gwT_h"],
        "m0_mnw": w_m0["mnw"],
        "m_inT": w_mb["inT"], "m_oT": w_mb["oT"], "m_cw": w_mb["cw"],
        "m_cb": w_mb["cb"], "m_hp": w_mb["hp"], "m_nw": w_mb["nw"],
        "m_ds_row": w_mb["ds_row"], "m_mks": w_mb["mks"],
        "f1_guT_q": w_f1["guT_q"], "f1_dnT_hh": w_f1["dnT_hh"],
        "a1_wqT": w_a1["wqT"], "a1_wkT": w_a1["wkT"],
        "a1_wvT": w_a1["wvT"], "a1_woT": w_a1["woT"],
    }
    magics = np.random.randint(1, 2 ** 30, size=4)
    nc = build_mega()
    im = []
    for c in range(NCORES):
        b, h = c // 2, c % 2
        d = dict(base)
        d["xin"] = _half_cols(xfm[b], h)
        d["xsls"] = np.array([[h, h ^ 1] + list(magics) +
                              [h * 8 + i for i in range(8)]], np.int32)
        im.append(d)
    res = run_bass_kernel_spmd(nc, im, core_ids=list(range(NCORES)))
    outs = []
    for b in range(BATCH):
        fm = _merge_halves(res.results[2 * b]["xo"],
                           res.results[2 * b + 1]["xo"])
        outs.append(from_fm(fm, SEQ))
    return np.stack(outs, axis=0).astype(np.float32)
